# version 05
# baseline (speedup 1.0000x reference)
import os
import numpy as np
import contextlib
import concourse.bass as bass
import concourse.mybir as mybir
from concourse.bass_utils import run_bass_kernel_spmd

F32 = mybir.dt.float32
BF16 = mybir.dt.bfloat16
I32 = mybir.dt.int32
AF = mybir.ActivationFunctionType
ALU = mybir.AluOpType
AX = mybir.AxisListType.X

L, DM, DI, DTR, DS, DC, INTER, NLUT = 8192, 2048, 4096, 128, 16, 4, 5504, 16
EPS = 1e-5
TC = 1024
NKC = L // TC
NDT = DI // 128
NFT = DM // 128
NIT = INTER // 128
NTT = TC // 128

ENGS = ("sync", "tensor", "vector", "scalar", "gpsimd")


def _impl(ctx, sc):
    if sc is None:
        return True
    if ctx is None:
        return False
    t1, k1 = ctx
    t2, k2 = sc
    if k2 > k1:
        return False
    if t2 == "a":
        return True
    if t2 == "f":
        return t1 in ("f", "o", "p2") and k1 // 4 == k2 // 4
    if t2 == "l":
        return t1 == "l"
    if t2 == "o":
        return t1 == "o" and k1 == k2
    return False


class B:

    SAFE = True

    def __init__(self):
        self.q = {g: [] for g in ENGS}
        self.cnt = {}
        self.owner = {}
        self.sems = {}
        self.kreg = {}
        self.ctx = None
        self.scls = {}

    def _c(self, name):
        return self.cnt.get(name, 0)

    def op(self, g, fn, inc=1, selfwait=True):
        sems = self.sems
        if self.SAFE and selfwait and self._c(g) > 0:
            pv = self._c(g)

            def wrun(eng, pv=pv, g=g):
                eng.wait_ge(sems[g], pv)
            self.q[g].append(wrun)
        self.cnt[g] = self._c(g) + inc
        self.owner[g] = g

        def run(eng, fn=fn, g=g, inc=inc):
            fn(eng).then_inc(sems[g], inc)
        self.q[g].append(run)
        return self.cnt[g]

    def silent(self, g, fn):
        self.q[g].append(fn)

    def wait(self, g, src, val=None):
        if src in self.scls:
            assert _impl(self.ctx, self.scls[src]), (src, self.ctx, self.scls[src])
        if val is None:
            val = self._c(src)
        sems = self.sems

        def run(eng, src=src, val=val):
            if val > 0:
                eng.wait_ge(sems[src], val)
        self.q[g].append(run)

    def dma(self, g, sem, out, in_, **kw):
        self.cnt[sem] = self._c(sem) + 16
        self.owner[sem] = g
        sems = self.sems

        def run(eng, sem=sem):
            eng.dma_start(out=out, in_=in_, **kw).then_inc(sems[sem], 16)
        self.q[g].append(run)
        return self.cnt[sem]

    def dma_tr(self, g, sem, out, in_):
        self.cnt[sem] = self._c(sem) + 16
        self.owner[sem] = g
        sems = self.sems

        def run(eng, sem=sem):
            eng.dma_start_transpose(out, in_).then_inc(sems[sem], 16)
        self.q[g].append(run)
        return self.cnt[sem]

    def guarded(self, k, builder):
        gb = B.__new__(B)
        gb.q = {g: [] for g in ENGS}
        gb.cnt = dict(self.cnt)
        gb.owner = self.owner
        gb.sems = self.sems
        gb.kreg = self.kreg
        builder(gb)
        for g in ENGS:
            span = gb.q[g]
            deltas = [(name, gb._c(name) - self._c(name))
                      for name in gb.cnt
                      if gb.owner.get(name) == g and gb._c(name) > self._c(name)]
            if not span and not deltas:
                continue
            sems = self.sems
            kreg = self.kreg

            def run(eng, g=g, span=span, deltas=deltas, k=k):
                with eng.If_eq(kreg[g], k):
                    for fn in span:
                        fn(eng)
                if deltas:
                    with eng.Else():
                        for name, d in deltas:
                            eng.sem_inc(sems[name], d)
            self.q[g].append(run)
        self.cnt = gb.cnt


def build_program_v0(nkc=NKC):
    nc = bass.Bass("TRN2", target_bir_lowering=False)

    def din(name, shape, dt=F32):
        return nc.dram_tensor(name, shape, dt, kind="ExternalInput").ap()

    xin = din("xin", [L, DM])
    xmy = din("xmy", [TC, DM])
    wx_blk = din("wx_blk", [128, NDT * 2048])
    wz_blk = din("wz_blk", [128, NDT * 2048])
    wot_blk = din("wot_blk", [128, NDT * 2048])
    wxp_dt = din("wxp_dt", [128, NDT * DTR])
    wxp_b = din("wxp_b", [128, NDT * DS])
    wxp_c = din("wxp_c", [128, NDT * DS])
    wdtt = din("wdtt", [DTR, DI])
    p_convw_in = din("p_convw_in", [128, NDT * DC])
    p_convb_in = din("p_convb_in", [128, NDT])
    p_dtb_in = din("p_dtb_in", [128, NDT])
    p_alog_in = din("p_alog_in", [128, NDT * DS])
    p_dvec_in = din("p_dvec_in", [128, NDT])
    lutv = din("lutv", [128, NLUT])
    g_blk = din("g_blk", [128, NIT * NFT * 128], I32)
    u_blk = din("u_blk", [128, NIT * NFT * 128], I32)
    d_blk = din("d_blk", [128, NFT * NIT * 128], I32)
    selmat = din("selmat", [DS, DS * 128], BF16)
    kown_in = din("kown_in", [1, 1], I32)

    yout = nc.dram_tensor("yout", [TC, DM], F32, kind="ExternalOutput").ap()

    def dscr(name, shape, dt):
        return nc.dram_tensor(name, shape, dt, kind="Internal")

    wxbf = dscr("wxbf", [128, NDT * 2048], BF16)
    xld_sc = dscr("xld_sc", [DI, TC], BF16)
    ln_tm = dscr("ln_tm", [TC, DM], BF16)
    ygate_sc = dscr("ygate_sc", [DI, TC], BF16)
    sz_sc = dscr("sz_sc", [DI, TC], BF16)
    wgbf = dscr("wgbf", [128, NIT * NFT * 128], BF16)
    wubf = dscr("wubf", [128, NIT * NFT * 128], BF16)
    wdbf = dscr("wdbf", [128, NFT * NIT * 128], BF16)
    h_sc = dscr("h_sc", [INTER, TC], BF16)
    fm_sc = dscr("fm_sc", [DM, TC], BF16)
    x2_sc = dscr("x2_sc", [TC, DM], F32)

    with contextlib.ExitStack() as stx:
        e = stx.enter_context

        def sb(name, p, f, dt):
            return e(nc.sbuf_tensor(name, [p, f], dt))

        ARENA_F = 20480
        arena = sb("arena", 128, ARENA_F, F32)
        a_bf = arena.bitcast(BF16)
        a_i32 = arena.bitcast(I32)

        def vbf(off_f32, free):
            return a_bf[0:128, off_f32 * 2:off_f32 * 2 + free]

        def vi32(off_f32, free):
            return a_i32[0:128, off_f32:off_f32 + free]

        ln1t = vbf(0, NFT * TC)
        b_bc = vbf(4096, DS * TC)
        c_bc = vbf(12288, DS * TC)
        cv_idx = [vi32(0, INTER), vi32(5504, INTER)]
        cv_ixb = vbf(11008, INTER)
        cv_acc = vbf(13760, INTER)
        cv_t = vbf(16512, INTER)
        ln2t = vbf(0, NFT * TC)
        dn_l = vbf(4096, NIT * 128)

        p_negA = sb("p_negA", 128, NDT * DS, F32)
        p_convw = sb("p_convw", 128, NDT * DC, F32)
        p_convb = sb("p_convb", 128, NDT, F32)
        p_dtb = sb("p_dtb", 128, NDT, F32)
        p_dvec = sb("p_dvec", 128, NDT, F32)
        p_sel = sb("p_sel", DS, DS * 128, BF16)
        p_lut = sb("p_lut", 128, NLUT, F32)
        p_kown = sb("p_kown", 1, 1, I32)
        carry = sb("carry", 128, NDT * DS, F32)
        dt_lhs = sb("dt_lhs", DTR, DI, BF16)
        xp_dt_l = sb("xp_dt_l", 128, NDT * DTR, BF16)
        xp_b_l = sb("xp_b_l", 128, NDT * DS, BF16)
        xp_c_l = sb("xp_c_l", 128, NDT * DS, BF16)
        dtlo_bf = sb("dtlo_bf", DTR, TC, BF16)
        bmat = sb("bmat", DS, TC, BF16)
        cmat = sb("cmat", DS, TC, BF16)
        xpre = sb("xpre", 128, NDT * (DC - 1), F32)

        t_ln = sb("t_ln", 128, DM, F32)
        t_lnsq = sb("t_lnsq", 128, DM, F32)
        t_st = sb("t_st", 128, 8, F32)
        t_lno = sb("t_lno", 128, DM, BF16)
        w_f32 = sb("w_f32", 128, DM, F32)
        w_bf = [sb("w_bf_%d" % i, 128, DM, BF16) for i in range(2)]
        t_xr = sb("t_xr", 128, TC + DC - 1, F32)
        t_cf = sb("t_cf", 128, TC, F32)
        t_sp = sb("t_sp", 128, TC, F32)
        t_dl = sb("t_dl", 128, TC, BF16)
        t_u = sb("t_u", 128, TC, BF16)
        t_xld = sb("t_xld", 128, TC, BF16)
        t_a = sb("t_a", 128, TC, F32)
        t_b = sb("t_b", 128, TC, BF16)
        t_h = sb("t_h", 128, TC, F32)
        t_ch = sb("t_ch", 128, TC, BF16)
        t_y = sb("t_y", 128, TC, BF16)
        t_yg = sb("t_yg", 128, TC, BF16)
        t_sz = sb("t_sz", 128, TC, BF16)
        t_ev = sb("t_ev", 128, TC, BF16)
        t_yt = [sb("t_yt_%d" % i, 128, TC, BF16) for i in range(2)]

        ps_mm = e(nc.psum_tensor("ps_mm", [128, TC], F32))
        ps_d = e(nc.psum_tensor("ps_d", [128, TC], F32))
        ps_bc = e(nc.psum_tensor("ps_bc", [128, TC], F32))
        ps_c = e(nc.psum_tensor("ps_c", [DS, TC], F32))

        P = B()

        def mm_pair(t, psum, lhsT, rhs, start, stop, pl=128, pbase=0):
            t.matmul(psum[pbase:pbase + pl, 0:512], lhsT, rhs[:, 0:512],
                     start=start, stop=stop)
            return t.matmul(psum[pbase:pbase + pl, 512:1024], lhsT,
                            rhs[:, 512:1024], start=start, stop=stop)

        P.dma("sync", "pld", p_sel[:, :], selmat)
        P.dma("sync", "pld", p_lut[:, :], lutv)
        P.dma("sync", "pld", p_kown[:, :], kown_in)
        P.dma("sync", "pld", p_convw[:, :], p_convw_in)
        P.dma("sync", "pld", p_convb[:, :], p_convb_in)
        P.dma("sync", "pld", p_dtb[:, :], p_dtb_in)
        P.dma("sync", "pld", p_dvec[:, :], p_dvec_in)
        P.dma("sync", "pld", p_negA[:, :], p_alog_in)

        for g in ENGS:
            P.wait(g, "pld")

            def mkload(g=g):
                def run(eng):
                    reg = eng.register("kreg_" + g).__enter__()
                    P.kreg[g] = reg
                    eng.reg_load(reg, p_kown[0:1, 0:1])
                return run
            P.silent(g, mkload())

        P.op("scalar", lambda s: s.activation(p_negA[:, :], p_negA[:, :], AF.Exp))
        P.wait("vector", "scalar")
        P.op("vector", lambda v: v.tensor_single_scalar(p_negA[:, :], p_negA[:, :], -1.0, ALU.mult))
        P.op("vector", lambda v: v.memset(carry[:, :], 0.0))
        P.op("vector", lambda v: v.memset(xpre[:, :], 0.0))

        def cast_load(dst_ap, src_ap, width):
            P.wait("sync", "vector")
            P.dma("sync", "wf32", w_f32[:, 0:width], src_ap)
            P.wait("vector", "wf32")
            P.op("vector", lambda v: v.tensor_copy(dst_ap, w_f32[:, 0:width]))

        for h in range(2):
            cast_load(dt_lhs[:, h * DM:(h + 1) * DM], wdtt[:, h * DM:(h + 1) * DM], DM)
        cast_load(xp_dt_l[:, 0:DM], wxp_dt[:, 0:DM], DM)
        cast_load(xp_dt_l[:, DM:2 * DM], wxp_dt[:, DM:2 * DM], DM)
        cast_load(xp_b_l[:, :], wxp_b[:, :], NDT * DS)
        cast_load(xp_c_l[:, :], wxp_c[:, :], NDT * DS)

        for i in range(NDT):
            bi = i % 2
            P.wait("sync", "vector")
            P.dma("sync", "wf32", w_f32[:, :], wx_blk[:, i * 2048:(i + 1) * 2048])
            P.wait("vector", "wf32")
            P.wait("vector", "swx%d" % bi)
            P.op("vector", lambda v, bi=bi: v.tensor_copy(w_bf[bi][:, :], w_f32[:, :]))
            P.wait("gpsimd", "vector")
            P.dma("gpsimd", "swx%d" % bi, wxbf.ap()[:, i * 2048:(i + 1) * 2048], w_bf[bi][:, :])

        def convert(src, dst, width_total):
            nch = width_total // INTER
            for i in range(nch):
                bi = i % 2
                P.wait("sync", "vector")
                P.dma("sync", "cvi%d" % bi, cv_idx[bi], src[:, i * INTER:(i + 1) * INTER])
                P.wait("vector", "cvi%d" % bi)
                P.wait("vector", "scv")
                P.op("vector", lambda v, bi=bi: v.tensor_copy(cv_ixb, cv_idx[bi]))
                P.op("vector", lambda v: v.tensor_scalar(
                    cv_acc, cv_ixb, 0.0, p_lut[:, 0:1], ALU.is_equal, ALU.mult))
                for vv in range(1, NLUT):
                    P.op("vector", lambda v, vv=vv: v.tensor_scalar(
                        cv_t, cv_ixb, float(vv), p_lut[:, vv:vv + 1],
                        ALU.is_equal, ALU.mult))
                    P.op("vector", lambda v: v.tensor_tensor(cv_acc, cv_acc, cv_t, ALU.add))
                P.wait("gpsimd", "vector")
                P.dma("gpsimd", "scv", dst.ap()[:, i * INTER:(i + 1) * INTER], cv_acc)

        convert(g_blk, wgbf, NIT * NFT * 128)
        convert(u_blk, wubf, NIT * NFT * 128)
        convert(d_blk, wdbf, NFT * NIT * 128)
        P.wait("vector", "scv")
        P.wait("sync", "swx0")
        P.wait("sync", "swx1")
        P.wait("sync", "scv")

        def ln_freedim(src, dst):
            P.op("vector", lambda v: v.reduce_sum(t_st[:, 0:1], src, axis=AX))
            P.op("scalar", lambda s: s.activation(t_lnsq[:, :], src, AF.Square))
            P.wait("vector", "scalar")
            P.op("vector", lambda v: v.reduce_sum(t_st[:, 1:2], t_lnsq[:, :], axis=AX))
            P.op("vector", lambda v: v.tensor_single_scalar(t_st[:, 2:3], t_st[:, 0:1], 1.0 / DM, ALU.mult))
            P.op("vector", lambda v: v.tensor_single_scalar(t_st[:, 3:4], t_st[:, 1:2], 1.0 / DM, ALU.mult))
            P.op("vector", lambda v: v.tensor_tensor(t_st[:, 4:5], t_st[:, 2:3], t_st[:, 2:3], ALU.mult))
            P.op("vector", lambda v: v.tensor_tensor(t_st[:, 5:6], t_st[:, 3:4], t_st[:, 4:5], ALU.subtract))
            P.op("vector", lambda v: v.tensor_single_scalar(t_st[:, 5:6], t_st[:, 5:6], EPS, ALU.add))
            P.wait("scalar", "vector")
            P.op("scalar", lambda s: s.activation(t_st[:, 6:7], t_st[:, 5:6], AF.Sqrt))
            P.wait("vector", "scalar")
            P.op("vector", lambda v: v.reciprocal(t_st[:, 7:8], t_st[:, 6:7]))
            P.op("vector", lambda v: v.tensor_tensor(t_st[:, 6:7], t_st[:, 2:3], t_st[:, 7:8], ALU.mult))
            P.op("vector", lambda v: v.tensor_single_scalar(t_st[:, 6:7], t_st[:, 6:7], -1.0, ALU.mult))
            P.op("vector", lambda v: v.tensor_scalar(
                dst, src, t_st[:, 7:8], t_st[:, 6:7], ALU.mult, ALU.add))

        for k in range(nkc):
            for tt in range(NTT):
                row0 = k * TC + tt * 128
                P.wait("sync", "vector")
                P.wait("sync", "scalar")
                P.dma("sync", "xtm", t_ln[:, :], xin[row0:row0 + 128, :])
                P.wait("vector", "xtm")
                P.wait("scalar", "xtm")
                ln_freedim(t_ln[:, :], t_lnsq[:, :])
                P.wait("vector", "sltm")
                P.op("vector", lambda v: v.tensor_copy(t_lno[:, :], t_lnsq[:, :]))
                P.wait("gpsimd", "vector")
                P.dma("gpsimd", "sltm", ln_tm.ap()[tt * 128:(tt + 1) * 128, :], t_lno[:, :])
            for ft in range(NFT):
                P.wait("sync", "tensor")
                P.wait("sync", "sltm")
                P.dma_tr("sync", "l1t", ln1t[:, ft * TC:(ft + 1) * TC],
                         ln_tm.ap()[:, ft * 128:(ft + 1) * 128])
            ln1t_ready = P._c("l1t")

            for dt in range(NDT):
                bi = dt % 2
                P.wait("sync", "tensor")
                P.dma("sync", "wbf%d" % bi, w_bf[bi][:, :],
                      wxbf.ap()[:, dt * 2048:(dt + 1) * 2048])
                P.wait("tensor", "wbf%d" % bi)
                P.wait("tensor", "l1t", ln1t_ready)
                P.wait("tensor", "vector")
                for kt in range(NFT):
                    st_, sp_ = kt == 0, kt == NFT - 1
                    if sp_:
                        P.op("tensor", (lambda t, kt=kt, bi=bi, st_=st_: mm_pair(
                            t, ps_mm, w_bf[bi][:, kt * 128:(kt + 1) * 128],
                            ln1t[:, kt * TC:(kt + 1) * TC], st_, True)), selfwait=False)
                    else:
                        P.silent("tensor", (lambda t, kt=kt, bi=bi, st_=st_: mm_pair(
                            t, ps_mm, w_bf[bi][:, kt * 128:(kt + 1) * 128],
                            ln1t[:, kt * TC:(kt + 1) * TC], st_, False)))
                P.wait("vector", "tensor")
                P.op("vector", lambda v, dt=dt: v.tensor_copy(
                    t_xr[:, 0:DC - 1], xpre[:, dt * (DC - 1):(dt + 1) * (DC - 1)]))
                P.op("vector", lambda v: v.tensor_copy(t_xr[:, DC - 1:DC - 1 + TC], ps_mm[:, :]))
                P.op("vector", lambda v, dt=dt: v.tensor_copy(
                    xpre[:, dt * (DC - 1):(dt + 1) * (DC - 1)], t_xr[:, TC:TC + DC - 1]))
                P.op("vector", lambda v, dt=dt: v.tensor_scalar(
                    t_cf[:, :], t_xr[:, 0:TC], p_convw[:, dt * DC:dt * DC + 1], None, ALU.mult))
                for j in range(1, DC):
                    P.op("vector", lambda v, j=j, dt=dt: v.scalar_tensor_tensor(
                        t_cf[:, :], t_xr[:, j:j + TC], p_convw[:, dt * DC + j:dt * DC + j + 1],
                        t_cf[:, :], ALU.mult, ALU.add))
                P.wait("scalar", "vector")
                P.wait("scalar", "sxld")
                P.wait("scalar", "tensor")
                P.op("scalar", lambda s, dt=dt: s.activation(
                    t_xld[:, :], t_cf[:, :], AF.Silu, bias=p_convb[:, dt:dt + 1]))
                P.wait("gpsimd", "scalar")
                P.dma("gpsimd", "sxld", xld_sc.ap()[dt * 128:(dt + 1) * 128, :], t_xld[:, :])
                P.wait("tensor", "scalar")
                st_, sp_ = dt == 0, dt == NDT - 1
                P.silent("tensor", (lambda t, dt=dt, st_=st_, sp_=sp_: mm_pair(
                    t, ps_d, xp_dt_l[:, dt * DTR:(dt + 1) * DTR], t_xld, st_, sp_)))
                P.silent("tensor", (lambda t, dt=dt, st_=st_, sp_=sp_: mm_pair(
                    t, ps_bc, xp_b_l[:, dt * DS:(dt + 1) * DS], t_xld, st_, sp_, pl=DS)))
                P.op("tensor", (lambda t, dt=dt, st_=st_, sp_=sp_: mm_pair(
                    t, ps_c, xp_c_l[:, dt * DS:(dt + 1) * DS], t_xld, st_, sp_,
                    pl=DS)), selfwait=False)
            P.wait("scalar", "tensor")
            P.op("scalar", lambda s: s.copy(dtlo_bf[:, :], ps_d[:, :]))
            P.op("scalar", lambda s: s.copy(bmat[:, :], ps_bc[0:DS, :]))
            P.op("scalar", lambda s: s.copy(cmat[:, :], ps_c[0:DS, :]))

            def z_builder(g, k=k):
                for dt in range(NDT):
                    bi = dt % 2
                    g.wait("sync", "vector")
                    g.dma("sync", "wf32", w_f32[:, :], wz_blk[:, dt * 2048:(dt + 1) * 2048])
                    g.wait("vector", "wf32")
                    g.wait("vector", "tensor")
                    g.op("vector", lambda v, bi=bi: v.tensor_copy(w_bf[bi][:, :], w_f32[:, :]))
                    g.wait("tensor", "vector")
                    g.wait("tensor", "scalar")
                    for kt in range(NFT):
                        if kt == NFT - 1:
                            g.op("tensor", (lambda t, kt=kt, bi=bi: mm_pair(
                                t, ps_mm, w_bf[bi][:, kt * 128:(kt + 1) * 128],
                                ln1t[:, kt * TC:(kt + 1) * TC], kt == 0, True)), selfwait=False)
                        else:
                            g.silent("tensor", (lambda t, kt=kt, bi=bi: mm_pair(
                                t, ps_mm, w_bf[bi][:, kt * 128:(kt + 1) * 128],
                                ln1t[:, kt * TC:(kt + 1) * TC], kt == 0, False)))
                    g.wait("scalar", "tensor")
                    g.wait("scalar", "ssz")
                    g.op("scalar", lambda s: s.activation(t_sz[:, :], ps_mm[:, :], AF.Silu))
                    g.wait("gpsimd", "scalar")
                    g.dma("gpsimd", "ssz", sz_sc.ap()[dt * 128:(dt + 1) * 128, :], t_sz[:, :])
            P.guarded(k, z_builder)

            P.wait("tensor", "scalar")
            for n in range(DS):
                P.op("tensor", (lambda t, n=n: mm_pair(
                    t, ps_bc, p_sel[:, n * 128:(n + 1) * 128], bmat, True, True)), selfwait=False)
                P.wait("scalar", "tensor")
                P.op("scalar", lambda s, n=n: s.copy(b_bc[:, n * TC:(n + 1) * TC], ps_bc[:, :]))
                P.wait("tensor", "scalar")
                P.op("tensor", (lambda t, n=n: mm_pair(
                    t, ps_bc, p_sel[:, n * 128:(n + 1) * 128], cmat, True, True)), selfwait=False)
                P.wait("scalar", "tensor")
                P.op("scalar", lambda s, n=n: s.copy(c_bc[:, n * TC:(n + 1) * TC], ps_bc[:, :]))
                P.wait("tensor", "scalar")

            for dt in range(NDT):
                P.wait("tensor", "scalar")
                P.wait("tensor", "vector")
                P.op("tensor", (lambda t, dt=dt: mm_pair(
                    t, ps_d, dt_lhs[:, dt * 128:(dt + 1) * 128], dtlo_bf, True, True)), selfwait=False)
                P.wait("scalar", "tensor")
                P.wait("scalar", "vector")
                P.op("scalar", lambda s, dt=dt: s.activation(
                    t_sp[:, :], ps_d[:, :], AF.Abs, bias=p_dtb[:, dt:dt + 1]))
                P.op("scalar", lambda s: s.activation(t_sp[:, :], t_sp[:, :], AF.Exp, scale=-1.0))
                P.op("scalar", lambda s: s.activation(t_sp[:, :], t_sp[:, :], AF.Ln, bias=1.0))
                P.wait("vector", "scalar")
                P.op("vector", lambda v, dt=dt: v.tensor_scalar(
                    t_cf[:, :], ps_d[:, :], p_dtb[:, dt:dt + 1], 0.0, ALU.add, ALU.max))
                P.op("vector", lambda v: v.tensor_tensor(t_dl[:, :], t_cf[:, :], t_sp[:, :], ALU.add))
                P.wait("sync", "vector")
                P.wait("sync", "tensor")
                P.wait("sync", "sxld")
                P.dma("sync", "lxld", t_xld[:, :], xld_sc.ap()[dt * 128:(dt + 1) * 128, :])
                P.wait("vector", "lxld")
                P.op("vector", lambda v: v.tensor_tensor(t_u[:, :], t_dl[:, :], t_xld[:, :], ALU.mult))
                for n in range(DS):
                    col = dt * DS + n
                    P.wait("scalar", "vector")
                    P.op("scalar", lambda s, col=col: s.activation(
                        t_a[:, :], t_dl[:, :], AF.Exp, scale=p_negA[:, col:col + 1]))
                    P.wait("vector", "scalar")
                    P.op("vector", lambda v, n=n: v.tensor_tensor(
                        t_b[:, :], t_u[:, :], b_bc[:, n * TC:(n + 1) * TC], ALU.mult))
                    P.op("vector", lambda v, col=col: v.tensor_tensor_scan(
                        t_h[:, :], t_a[:, :], t_b[:, :], carry[:, col:col + 1],
                        ALU.mult, ALU.add))
                    P.op("vector", lambda v, col=col: v.tensor_copy(
                        carry[:, col:col + 1], t_h[:, TC - 1:TC]))

                    def own_scan(g, n=n):
                        g.op("vector", lambda v, n=n: v.tensor_tensor(
                            t_ch[:, :], t_h[:, :], c_bc[:, n * TC:(n + 1) * TC], ALU.mult))
                        if n == 0:
                            g.op("vector", lambda v: v.tensor_copy(t_y[:, :], t_ch[:, :]))
                        else:
                            g.op("vector", lambda v: v.tensor_tensor(
                                t_y[:, :], t_y[:, :], t_ch[:, :], ALU.add))
                    P.guarded(k, own_scan)

                def own_gate(g, dt=dt):
                    g.wait("gpsimd", "vector")
                    g.wait("gpsimd", "ssz")
                    g.dma("gpsimd", "lsz", t_sz[:, :], sz_sc.ap()[dt * 128:(dt + 1) * 128, :])
                    g.op("vector", lambda v, dt=dt: v.scalar_tensor_tensor(
                        t_y[:, :], t_xld[:, :], p_dvec[:, dt:dt + 1], t_y[:, :],
                        ALU.mult, ALU.add))
                    g.wait("vector", "lsz")
                    g.wait("vector", "syg")
                    g.op("vector", lambda v: v.tensor_tensor(t_yg[:, :], t_y[:, :], t_sz[:, :], ALU.mult))
                    g.wait("gpsimd", "vector")
                    g.dma("gpsimd", "syg", ygate_sc.ap()[dt * 128:(dt + 1) * 128, :], t_yg[:, :])
                P.guarded(k, own_gate)

        syg_total = P._c("syg")
        ssz_total = P._c("ssz")

        for mtp in range(NFT // 2):
            for half, psum in ((0, ps_mm), (1, ps_d)):
                mt = mtp * 2 + half
                P.wait("tensor", "scalar")
                for dt in range(NDT):
                    bi = dt % 2
                    P.wait("sync", "vector")
                    P.dma("sync", "wf32", w_f32[:, 0:128],
                          wot_blk[:, dt * 2048 + mt * 128:dt * 2048 + (mt + 1) * 128])
                    P.wait("sync", "tensor")
                    P.wait("sync", "syg", syg_total)
                    P.dma("sync", "yt%d" % bi, t_yt[bi][:, :],
                          ygate_sc.ap()[dt * 128:(dt + 1) * 128, :])
                    P.wait("vector", "wf32")
                    P.op("vector", lambda v, bi=bi: v.tensor_copy(w_bf[bi][:, 0:128], w_f32[:, 0:128]))
                    P.wait("tensor", "vector")
                    P.wait("tensor", "yt%d" % bi)
                    P.op("tensor", (lambda t, bi=bi, psum=psum, dt=dt: mm_pair(
                        t, psum, w_bf[bi][:, 0:128], t_yt[bi], dt == 0, dt == NDT - 1)),
                        selfwait=False)
                P.wait("scalar", "tensor")
                P.wait("scalar", "sfm")
                P.op("scalar", lambda s, psum=psum: s.copy(t_ev[:, :], psum[:, :]))
                P.wait("gpsimd", "scalar")
                P.dma("gpsimd", "sfm", fm_sc.ap()[mt * 128:(mt + 1) * 128, :], t_ev[:, :])
        P.wait("sync", "ssz", ssz_total)

        for tt in range(NTT):
            P.wait("sync", "sfm")
            P.wait("sync", "scalar")
            P.wait("sync", "vector")
            P.wait("sync", "sltm")
            P.wait("sync", "sx2")
            P.dma_tr("sync", "tlo", t_lno[:, :], fm_sc.ap()[:, tt * 128:(tt + 1) * 128])
            P.dma("sync", "xtm", t_ln[:, :], xmy[tt * 128:(tt + 1) * 128, :])
            P.wait("vector", "tlo")
            P.wait("vector", "xtm")
            P.wait("scalar", "tlo")
            ln_freedim(t_lno[:, :], t_lnsq[:, :])
            P.wait("vector", "sx2")
            P.op("vector", lambda v: v.tensor_tensor(t_ln[:, :], t_ln[:, :], t_lnsq[:, :], ALU.add))
            P.wait("gpsimd", "vector")
            P.dma("gpsimd", "sx2", x2_sc.ap()[tt * 128:(tt + 1) * 128, :], t_ln[:, :])
            P.wait("scalar", "vector")
            ln_freedim(t_ln[:, :], t_lnsq[:, :])
            P.wait("vector", "sltm")
            P.op("vector", lambda v: v.tensor_copy(t_lno[:, :], t_lnsq[:, :]))
            P.wait("gpsimd", "vector")
            P.dma("gpsimd", "sltm", ln_tm.ap()[tt * 128:(tt + 1) * 128, :], t_lno[:, :])
        for ft in range(NFT):
            P.wait("sync", "sltm")
            P.wait("sync", "tensor")
            P.dma_tr("sync", "l1t", ln2t[:, ft * TC:(ft + 1) * TC],
                     ln_tm.ap()[:, ft * 128:(ft + 1) * 128])
        ln2t_ready = P._c("l1t")

        for mt in range(NIT):
            bi = mt % 2
            P.wait("sync", "tensor")
            P.dma("sync", "wbf%d" % bi, w_bf[bi][:, :], wgbf.ap()[:, mt * 2048:(mt + 1) * 2048])
            P.wait("sync", "scalar")
            P.wait("sync", "vector")
            P.dma("sync", "tlo", t_lno[:, :], wubf.ap()[:, mt * 2048:(mt + 1) * 2048])
            P.wait("tensor", "wbf%d" % bi)
            P.wait("tensor", "tlo")
            P.wait("tensor", "l1t", ln2t_ready)
            P.wait("tensor", "vector")
            P.wait("tensor", "scalar")
            for kt in range(NFT):
                if kt == NFT - 1:
                    P.op("tensor", (lambda t, kt=kt, bi=bi: mm_pair(
                        t, ps_mm, w_bf[bi][:, kt * 128:(kt + 1) * 128],
                        ln2t[:, kt * TC:(kt + 1) * TC], kt == 0, True)), selfwait=False)
                    P.op("tensor", (lambda t, kt=kt: mm_pair(
                        t, ps_d, t_lno[:, kt * 128:(kt + 1) * 128],
                        ln2t[:, kt * TC:(kt + 1) * TC], kt == 0, True)), selfwait=False)
                else:
                    P.silent("tensor", (lambda t, kt=kt, bi=bi: mm_pair(
                        t, ps_mm, w_bf[bi][:, kt * 128:(kt + 1) * 128],
                        ln2t[:, kt * TC:(kt + 1) * TC], kt == 0, False)))
                    P.silent("tensor", (lambda t, kt=kt: mm_pair(
                        t, ps_d, t_lno[:, kt * 128:(kt + 1) * 128],
                        ln2t[:, kt * TC:(kt + 1) * TC], kt == 0, False)))
            P.wait("scalar", "tensor")
            P.wait("scalar", "vector")
            P.op("scalar", lambda s: s.activation(t_ev[:, :], ps_mm[:, :], AF.Silu))
            P.wait("vector", "scalar")
            P.wait("vector", "sh")
            P.op("vector", lambda v: v.tensor_tensor(t_yg[:, :], t_ev[:, :], ps_d[:, :], ALU.mult))
            P.wait("gpsimd", "vector")
            P.dma("gpsimd", "sh", h_sc.ap()[mt * 128:(mt + 1) * 128, :], t_yg[:, :])
        sh_total = P._c("sh")

        for m2p in range(NFT // 2):
            for half, psum in ((0, ps_mm), (1, ps_d)):
                mt2 = m2p * 2 + half
                P.wait("tensor", "scalar")
                P.wait("sync", "tensor")
                P.dma("sync", "dnl", dn_l, wdbf.ap()[:, mt2 * (NIT * 128):(mt2 + 1) * (NIT * 128)])
                for it in range(NIT):
                    bi = it % 2
                    P.wait("sync", "tensor")
                    P.wait("sync", "sh", sh_total)
                    P.dma("sync", "yt%d" % bi, t_yt[bi][:, :], h_sc.ap()[it * 128:(it + 1) * 128, :])
                    P.wait("tensor", "yt%d" % bi)
                    P.wait("tensor", "dnl")
                    P.op("tensor", (lambda t, it=it, bi=bi, psum=psum: mm_pair(
                        t, psum, dn_l[:, it * 128:(it + 1) * 128], t_yt[bi], it == 0, it == NIT - 1)),
                        selfwait=False)
                P.wait("scalar", "tensor")
                P.wait("scalar", "sfm")
                P.op("scalar", lambda s, psum=psum: s.copy(t_ev[:, :], psum[:, :]))
                P.wait("gpsimd", "scalar")
                P.dma("gpsimd", "sfm", fm_sc.ap()[mt2 * 128:(mt2 + 1) * 128, :], t_ev[:, :])
        sfm_total = P._c("sfm")

        for tt in range(NTT):
            P.wait("sync", "sfm", sfm_total)
            P.wait("sync", "scalar")
            P.wait("sync", "vector")
            P.wait("sync", "sout")
            P.wait("sync", "sltm")
            P.wait("sync", "sx2")
            P.dma_tr("sync", "tlo", t_lno[:, :], fm_sc.ap()[:, tt * 128:(tt + 1) * 128])
            P.dma("sync", "xtm", t_ln[:, :], x2_sc.ap()[tt * 128:(tt + 1) * 128, :])
            P.wait("vector", "tlo")
            P.wait("vector", "xtm")
            P.wait("scalar", "tlo")
            ln_freedim(t_lno[:, :], t_lnsq[:, :])
            P.op("vector", lambda v: v.tensor_tensor(t_ln[:, :], t_ln[:, :], t_lnsq[:, :], ALU.add))
            P.wait("sync", "vector")
            P.dma("sync", "sout", yout[tt * 128:(tt + 1) * 128, :], t_ln[:, :])
        P.wait("sync", "sout")

        for name in list(P.cnt):
            P.sems[name] = e(nc.semaphore("S_" + name))

        block = e(nc.Block())

        @block.sync
        def _(sync):
            for fn in P.q["sync"]:
                fn(sync)

        @block.tensor
        def _(tensor):
            for fn in P.q["tensor"]:
                fn(tensor)

        @block.vector
        def _(vector):
            for fn in P.q["vector"]:
                fn(vector)

        @block.scalar
        def _(scalar):
            for fn in P.q["scalar"]:
                fn(scalar)

        @block.gpsimd
        def _(gpsimd):
            for fn in P.q["gpsimd"]:
                fn(gpsimd)

    return nc




HBM_BASE = [0x0, 0x4000000000, 0x800000000000, 0x804000000000]
DOM_OF_LPAIR = {0: 2, 1: 3, 2: 1, 3: 0}

GC_XW0, GC_XW1 = 0, 1
GC_XRA = [2, 3, 4, 5]
GC_XRB = [6, 7, 8, 9]
GC_FWL, GC_FWR = 10, 11
GC_OW = 12
GC_OR = [13, 14, 15, 16]
GC_L2W = 17
GC_L2R = [18, 19]


def build_program_v1(ndie=4, nkc=8, rd=True, stages="pzxlsofn2", dbg=False):
    ndtc = NDT // ndie
    nitp = (NIT // ndie) + 1
    spill = ndie < 4
    nc = bass.Bass("TRN2", target_bir_lowering=False, detect_race_conditions=rd)

    def din(name, shape, dt=F32):
        return nc.dram_tensor(name, shape, dt, kind="ExternalInput").ap()

    nown = max(1, nkc // ndie)
    xin = din("xin", [L, DM])
    xmy = din("xmy", [nown * TC, DM])
    wx_blk = din("wx_blk", [128, ndtc * 2048])
    wz_blk = din("wz_blk", [128, ndtc * 2048])
    wot_blk = din("wot_blk", [128, ndtc * 2048])
    wxp_dt = din("wxp_dt", [128, ndtc * DTR])
    wxp_b = din("wxp_b", [128, ndtc * DS])
    wxp_c = din("wxp_c", [128, ndtc * DS])
    wdtt = din("wdtt", [DTR, ndtc * 128])
    p_convw_in = din("p_convw_in", [128, ndtc * DC])
    p_convb_in = din("p_convb_in", [128, ndtc])
    p_dtb_in = din("p_dtb_in", [128, ndtc])
    p_alog_in = din("p_alog_in", [128, ndtc * DS])
    p_dvec_in = din("p_dvec_in", [128, ndtc])
    lutv = din("lutv", [128, NLUT])
    g_blk = din("g_blk", [128, nitp * NFT * 128], I32)
    u_blk = din("u_blk", [128, nitp * NFT * 128], I32)
    d_blk = din("d_blk", [128, NFT * nitp * 128], I32)
    hmask_in = din("hmask_in", [128, nitp])
    selmat = din("selmat", [DS, DS * 128], BF16)
    flags_in = din("flags_in", [1, 64], I32)
    dbg_o = (nc.dram_tensor("dbg_o", [4 * 128, TC], F32, kind="ExternalOutput").ap()
             if dbg else None)
    fvals_in = din("fvals_in", [128, 8 * 16], I32)
    gidx_in = din("gidx_in", [128, 32], I32)

    yout = nc.dram_tensor("yout", [nown * TC, DM], F32, kind="ExternalOutput").ap()

    def dscr(name, shape, dt, **kw):
        return nc.dram_tensor(name, shape, dt, kind="Internal", **kw)

    wxbf = dscr("wxbf", [128, ndtc * 2048], BF16)
    xld_sc = dscr("xld_sc", [ndtc * 128, TC], BF16)
    ln_tm = dscr("ln_tm", [TC, DM], BF16)
    sz_sc = dscr("sz_sc", [ndtc * 128, TC], BF16)
    yg_sc = dscr("yg_sc", [ndtc * 128, TC], BF16)
    h_sc = dscr("h_sc", [nitp * 128, TC], BF16)
    wgbf = dscr("wgbf", [128, nitp * NFT * 128], BF16)
    wubf = dscr("wubf", [128, nitp * NFT * 128], BF16)
    wdbf = dscr("wdbf", [128, NFT * nitp * 128], BF16)
    fm_sc = dscr("fm_sc", [DM, TC], BF16)
    x2_sc = dscr("x2_sc", [TC, DM], F32)

    XDBL = dscr("XDBL", [ndie * 8 * 192, TC], F32, addr_space="Shared")
    OPART = dscr("OPART", [ndie * 8 * DM, TC], F32, addr_space="Shared")
    DPART = dscr("DPART", [ndie * 8 * DM, TC], F32, addr_space="Shared")
    LN2X = dscr("LN2X", [8 * DM, 2 * TC], BF16, addr_space="Shared")
    FLG = dscr("FLG", [4 * 4 * 128, 1024], I32, addr_space="Shared")

    with contextlib.ExitStack() as stx:
        e = stx.enter_context

        def sb(name, p, f, dt):
            return e(nc.sbuf_tensor(name, [p, f], dt))

        ARENA_F = 20480
        arena = sb("arena", 128, ARENA_F, F32)
        a_bf = arena.bitcast(BF16)
        a_i32 = arena.bitcast(I32)

        def vbf(off_f32, free):
            return a_bf[0:128, off_f32 * 2:off_f32 * 2 + free]

        def vi32(off_f32, free):
            return a_i32[0:128, off_f32:off_f32 + free]

        ln1t = vbf(0, NFT * TC)
        b_bc = vbf(4096, DS * TC)
        c_bc = vbf(12288, DS * TC)
        cv_idx = [vi32(0, INTER), vi32(5504, INTER)]
        cv_ixb = vbf(11008, INTER)
        cv_acc = vbf(13760, INTER)
        cv_t = vbf(16512, INTER)
        ln2t = vbf(0, NFT * TC)
        dn_l = vbf(4096, nitp * 128)

        p_negA = sb("p_negA", 128, ndtc * DS, F32)
        p_convw = sb("p_convw", 128, ndtc * DC, F32)
        p_convb = sb("p_convb", 128, ndtc, F32)
        p_dtb = sb("p_dtb", 128, ndtc, F32)
        p_dvec = sb("p_dvec", 128, ndtc, F32)
        p_sel = sb("p_sel", DS, DS * 128, BF16)
        p_lut = sb("p_lut", 128, NLUT, F32)
        p_hm = sb("p_hm", 128, nitp, F32)
        p_flags = sb("p_flags", 1, 64, I32)
        p_fval = sb("p_fval", 128, 8 * 16, I32)
        p_gidx = sb("p_gidx", 128, 32, I32)
        t_fw = sb("t_fw", 1, 8 * 16, I32)
        carry = sb("carry", 128, ndtc * DS, F32)
        dt_lhs = sb("dt_lhs", DTR, ndtc * 128, BF16)
        xp_dt_l = sb("xp_dt_l", 128, ndtc * DTR, BF16)
        xp_b_l = sb("xp_b_l", 128, ndtc * DS, BF16)
        xp_c_l = sb("xp_c_l", 128, ndtc * DS, BF16)
        dtlo_bf = sb("dtlo_bf", DTR, TC, BF16)
        bmat = sb("bmat", DS, TC, BF16)
        cmat = sb("cmat", DS, TC, BF16)
        xpre = sb("xpre", 128, ndtc * (DC - 1), F32)
        yg8 = sb("yg8", 128, (1 if spill else ndtc) * TC, BF16)
        h_ffn = sb("h_ffn", 128, (1 if spill else nitp) * TC, BF16)
        t_yt = [sb("t_yt0", 128, TC, BF16), sb("t_yt1", 128, TC, BF16)] if spill else None

        t_ln = sb("t_ln", 128, DM, F32)
        t_lnsq = sb("t_lnsq", 128, DM, F32)
        t_st = sb("t_st", 128, 8, F32)
        t_lno = sb("t_lno", 128, DM, BF16)
        w_f32 = sb("w_f32", 128, DM, F32)
        w_bf = [sb("w_bf_%d" % i, 128, DM, BF16) for i in range(2)]
        t_xr = sb("t_xr", 128, TC + DC - 1, F32)
        t_cf = sb("t_cf", 128, TC, F32)
        t_sp = sb("t_sp", 128, TC, F32)
        t_dl = sb("t_dl", 128, TC, BF16)
        t_u = sb("t_u", 128, TC, BF16)
        t_xld = sb("t_xld", 128, TC, BF16)
        _ta0 = sb("t_a_0", 128, TC, F32)
        t_a = [_ta0, _ta0]
        t_b = sb("t_b", 128, TC, BF16)
        t_h = sb("t_h", 128, TC, F32)
        t_ch = sb("t_ch", 128, TC, BF16)
        t_y = sb("t_y", 128, TC, BF16)
        t_sz = sb("t_sz", 128, TC, BF16)
        t_ev = sb("t_ev", 128, TC, BF16)

        xdl_a = t_sp[:, 0:TC]
        xdl_b = t_xr[0:64, 0:TC]
        t_bc64 = t_sp
        t_sum = t_cf

        ps_mm = e(nc.psum_tensor("ps_mm", [128, TC], F32))
        ps_d = e(nc.psum_tensor("ps_d", [128, TC], F32))
        ps_bc = e(nc.psum_tensor("ps_bc", [128, TC], F32))
        ps_c = e(nc.psum_tensor("ps_c", [DS, TC], F32))

        P = B()
        P.cnt.setdefault("pollq", 0)

        def mm_pair(t, psum, lhsT, rhs, start, stop, pl=128):
            t.matmul(psum[0:pl, 0:512], lhsT, rhs[:, 0:512], start=start, stop=stop)
            return t.matmul(psum[0:pl, 512:1024], lhsT, rhs[:, 512:1024],
                            start=start, stop=stop)

        from concourse.bass import IndirectOffsetOnAxis

        def iscatter(g, sem, dram_t, row0, src_ap, idx_col, nparts, w=None):
            g.cnt[sem] = g._c(sem) + 16
            g.owner[sem] = "gpsimd"
            assert g.scls.setdefault(sem, g.ctx) == g.ctx, sem
            sems = g.sems
            ncols = dram_t.shape[1]
            out_ap = dram_t.ap()
            eoff = row0 * ncols

            def run(eng, sem=sem, out_ap=out_ap, src_ap=src_ap, idx_col=idx_col,
                    nparts=nparts, eoff=eoff):
                eng.indirect_dma_start(
                    out_ap,
                    IndirectOffsetOnAxis(ap=p_gidx[0:nparts, idx_col:idx_col + 1], axis=0),
                    src_ap, None,
                    element_offset=eoff,
                    bounds_check=None, oob_is_err=False,
                ).then_inc(sems[sem], 16)
            g.q["gpsimd"].append(run)
            return g._c(sem)

        def igather(g, sem, dst_ap, dram_t, row0, idx_col, nparts, accum=False):
            g.cnt[sem] = g._c(sem) + 16
            g.owner[sem] = "gpsimd"
            assert g.scls.setdefault(sem, g.ctx) == g.ctx, sem
            sems = g.sems
            ncols = dram_t.shape[1]
            in_ap = dram_t.ap()
            eoff = row0 * ncols

            def run(eng, sem=sem, in_ap=in_ap, dst_ap=dst_ap, idx_col=idx_col,
                    nparts=nparts, accum=accum, eoff=eoff):
                eng.indirect_dma_start(
                    dst_ap, None,
                    in_ap,
                    IndirectOffsetOnAxis(ap=p_gidx[0:nparts, idx_col:idx_col + 1], axis=0),
                    element_offset=eoff,
                    bounds_check=None, oob_is_err=False,
                    compute_op=(ALU.add if accum else ALU.bypass),
                ).then_inc(sems[sem], 16)
            g.q["gpsimd"].append(run)
            return g._c(sem)

        P.dma("sync", "pld", p_sel[:, :], selmat)
        P.dma("sync", "pld", p_lut[:, :], lutv)
        P.dma("sync", "pld", p_hm[:, :], hmask_in)
        P.dma("sync", "pld", p_flags[:, :], flags_in)
        P.dma("sync", "pld", p_fval[:, :], fvals_in)
        P.dma("sync", "pld", p_gidx[:, :], gidx_in)
        P.dma("sync", "pld", p_convw[:, :], p_convw_in)
        P.dma("sync", "pld", p_convb[:, :], p_convb_in)
        P.dma("sync", "pld", p_dtb[:, :], p_dtb_in)
        P.dma("sync", "pld", p_dvec[:, :], p_dvec_in)
        P.dma("sync", "pld", p_negA[:, :], p_alog_in)
        for g in ENGS:
            P.wait(g, "pld")

        P.op("scalar", lambda s: s.activation(p_negA[:, :], p_negA[:, :], AF.Exp))
        P.wait("vector", "scalar")
        P.op("vector", lambda v: v.tensor_single_scalar(p_negA[:, :], p_negA[:, :], -1.0, ALU.mult))
        P.op("vector", lambda v: v.memset(carry[:, :], 0.0))
        P.op("vector", lambda v: v.memset(xpre[:, :], 0.0))
        P.op("vector", lambda v: v.memset(t_sp[:, :], 0.0))

        def cast_load(dst_t, src_t, width):
            for o in range(0, width, DM):
                w = min(DM, width - o)
                P.wait("sync", "vector")
                P.dma("sync", "wf32", w_f32[:, 0:w], src_t[:, o:o + w])
                P.wait("vector", "wf32")
                P.op("vector", lambda v, o=o, w=w: v.tensor_copy(
                    dst_t[:, o:o + w], w_f32[:, 0:w]))

        cast_load(dt_lhs, wdtt, ndtc * 128)
        cast_load(xp_dt_l, wxp_dt, ndtc * DTR)
        cast_load(xp_b_l, wxp_b, ndtc * DS)
        cast_load(xp_c_l, wxp_c, ndtc * DS)

        for i in range(ndtc):
            bi = i % 2
            P.wait("sync", "vector")
            P.dma("sync", "wf32", w_f32[:, :], wx_blk[:, i * 2048:(i + 1) * 2048])
            P.wait("vector", "wf32")
            P.wait("vector", "swx%d" % bi)
            P.op("vector", lambda v, bi=bi: v.tensor_copy(w_bf[bi][:, :], w_f32[:, :]))
            P.wait("gpsimd", "vector")
            P.dma("gpsimd", "swx%d" % bi, wxbf.ap()[:, i * 2048:(i + 1) * 2048], w_bf[bi][:, :])

        def convert(src, dst, width_total):
            nch = (width_total + INTER - 1) // INTER
            for i in range(nch):
                off = i * INTER
                w = min(INTER, width_total - off)
                bi = i % 2
                P.wait("sync", "vector")
                P.dma("sync", "cvi%d" % bi, cv_idx[bi][:, 0:w], src[:, off:off + w])
                P.wait("vector", "cvi%d" % bi)
                P.wait("vector", "scv")
                P.op("vector", lambda v, bi=bi, w=w: v.tensor_copy(cv_ixb[:, 0:w], cv_idx[bi][:, 0:w]))
                P.op("vector", lambda v, w=w: v.tensor_scalar(
                    cv_acc[:, 0:w], cv_ixb[:, 0:w], 0.0, p_lut[:, 0:1], ALU.is_equal, ALU.mult))
                for vv in range(1, NLUT):
                    P.op("vector", lambda v, vv=vv, w=w: v.tensor_scalar(
                        cv_t[:, 0:w], cv_ixb[:, 0:w], float(vv), p_lut[:, vv:vv + 1],
                        ALU.is_equal, ALU.mult))
                    P.op("vector", lambda v, w=w: v.tensor_tensor(
                        cv_acc[:, 0:w], cv_acc[:, 0:w], cv_t[:, 0:w], ALU.add))
                P.wait("gpsimd", "vector")
                P.dma("gpsimd", "scv", dst.ap()[:, off:off + w], cv_acc[:, 0:w])

        convert(g_blk, wgbf, nitp * NFT * 128)
        convert(u_blk, wubf, nitp * NFT * 128)
        convert(d_blk, wdbf, NFT * nitp * 128)
        P.wait("vector", "scv")
        P.wait("sync", "swx0")
        P.wait("sync", "swx1")
        P.wait("sync", "scv")

        pregs = {}

        def init_pregs(eng):
            pregs["scnt"] = eng.register("pscnt").__enter__()
            pregs["rv"] = [eng.register("prv%d" % j).__enter__() for j in range(4)]
            pregs["ok"] = eng.register("pok").__enter__()
            pregs["t1"] = eng.register("pt1").__enter__()
            pregs["no"] = eng.register("pno").__enter__()
            pregs["tgt"] = eng.register("ptgt").__enter__()
            eng.reg_mov(pregs["scnt"], 0)
        P.silent("sync", init_pregs)

        fregs = {}

        def fguard(col, builder, cls=None):
            for g in ENGS:
                def lrun(eng, g=g, col=col):
                    if g not in fregs:
                        fregs[g] = eng.register("freg_" + g).__enter__()
                    eng.reg_load(fregs[g], p_flags[0:1, col:col + 1])
                P.silent(g, lrun)
            gb = B.__new__(B)
            gb.q = {g: [] for g in ENGS}
            gb.cnt = dict(P.cnt)
            gb.owner = P.owner
            gb.sems = P.sems
            gb.kreg = P.kreg
            gb.ctx = cls
            gb.scls = P.scls
            builder(gb)
            for g in ENGS:
                span = gb.q[g]
                deltas = [(name, P._c(name), gb._c(name) - P._c(name))
                          for name in gb.cnt
                          if gb.owner.get(name) == g and gb._c(name) > P._c(name)
                          and name not in gb.scls]
                if not span and not deltas:
                    continue
                sems = P.sems

                def run(eng, g=g, span=span, deltas=deltas):
                    with eng.If_eq(fregs[g], 1):
                        for fn in span:
                            fn(eng)
                    if deltas:
                        with eng.Else():
                            for name, pre, d in deltas:
                                eng.wait_ge(sems[name], pre)
                                eng.sem_inc(sems[name], d)
                P.q[g].append(run)
            P.cnt = gb.cnt

        def poll(g_, stream, k, prods, lw=None):
            nprod = len(prods)
            row0 = prods[0]
            lw_c = g_._c(lw) if lw else 0

            def run(eng, stream=stream, k=k, nprod=nprod, row0=row0,
                    lw=lw, lw_c=lw_c):
                scnt, rv = pregs["scnt"], pregs["rv"]
                ok, t1, no = pregs["ok"], pregs["t1"], pregs["no"]
                tgt = pregs["tgt"]
                if lw:
                    eng.wait_ge(P.sems[lw], lw_c)
                eng.reg_load(tgt, p_fval[0:1, k * 16:k * 16 + 1])
                eng.reg_mov(no, 1)
                with eng.While(no):
                    eng.reg_add(scnt, scnt, 16)
                    eng.dma_start(
                        out=t_fw[0:1, 0:nprod * 16],
                        in_=bass.AP(FLG, (stream * 512 + k * 64 + row0 * 16) * 1024,
                                    [[1, 1], [16 * 1024, nprod], [1, 16]]),
                        allow_slow_non_contiguous=True,
                    ).then_inc(P.sems["pollq"], 16)
                    eng.wait_ge(P.sems["pollq"], scnt)
                    for j in range(nprod):
                        eng.reg_load(rv[j], t_fw[0:1, j * 16:j * 16 + 1])
                    eng.reg_alu(ok, rv[0], tgt, ALU.is_equal)
                    for j in range(1, nprod):
                        eng.reg_alu(t1, rv[j], tgt, ALU.is_equal)
                        eng.reg_alu(ok, ok, t1, ALU.bitwise_and)
                    eng.reg_alu(no, ok, 1, ALU.bitwise_xor)
            g_.silent("sync", run)
            g_.op("sync", lambda eng: eng.nop(), selfwait=False)
            g_.wait("gpsimd", "sync")

        def flag_push(g_, stream, k, sem):
            src = p_fval[0:16, k * 16:(k + 1) * 16]
            cols = (GC_FWL, GC_FWR) if ndie == 4 else (GC_FWL,)
            for col in cols:
                iscatter(g_, sem, FLG, stream * 512 + k * 64, src, col, 16, w=16)

        def ln_freedim(g_, src, dst):
            g_.op("vector", lambda v: v.reduce_sum(t_st[:, 0:1], src, axis=AX))
            g_.op("scalar", lambda s: s.activation(t_lnsq[:, :], src, AF.Square))
            g_.wait("vector", "scalar")
            g_.op("vector", lambda v: v.reduce_sum(t_st[:, 1:2], t_lnsq[:, :], axis=AX))
            g_.op("vector", lambda v: v.tensor_single_scalar(t_st[:, 2:3], t_st[:, 0:1], 1.0 / DM, ALU.mult))
            g_.op("vector", lambda v: v.tensor_single_scalar(t_st[:, 3:4], t_st[:, 1:2], 1.0 / DM, ALU.mult))
            g_.op("vector", lambda v: v.tensor_tensor(t_st[:, 4:5], t_st[:, 2:3], t_st[:, 2:3], ALU.mult))
            g_.op("vector", lambda v: v.tensor_tensor(t_st[:, 5:6], t_st[:, 3:4], t_st[:, 4:5], ALU.subtract))
            g_.op("vector", lambda v: v.tensor_single_scalar(t_st[:, 5:6], t_st[:, 5:6], EPS, ALU.add))
            g_.wait("scalar", "vector")
            g_.op("scalar", lambda s: s.activation(t_st[:, 6:7], t_st[:, 5:6], AF.Sqrt))
            g_.wait("vector", "scalar")
            g_.op("vector", lambda v: v.reciprocal(t_st[:, 7:8], t_st[:, 6:7]))
            g_.op("vector", lambda v: v.tensor_tensor(t_st[:, 6:7], t_st[:, 2:3], t_st[:, 7:8], ALU.mult))
            g_.op("vector", lambda v: v.tensor_single_scalar(t_st[:, 6:7], t_st[:, 6:7], -1.0, ALU.mult))
            g_.op("vector", lambda v: v.tensor_scalar(
                dst, src, t_st[:, 7:8], t_st[:, 6:7], ALU.mult, ALU.add))

        def prescan(g, k):
            ia, iap = "ia%d" % k, ("ia%d" % (k - 1) if k else None)
            for tt in range(NTT):
                row0 = k * TC + tt * 128
                g.wait("sync", "vector")
                g.wait("sync", "scalar")
                g.dma("sync", "xtm", t_ln[:, :], xin[row0:row0 + 128, :])
                g.wait("vector", "xtm")
                g.wait("scalar", "xtm")
                ln_freedim(g, t_ln[:, :], t_lnsq[:, :])
                g.wait("vector", "sltm")
                g.op("vector", lambda v: v.tensor_copy(t_lno[:, :], t_lnsq[:, :]))
                g.wait("sync", "vector")
                g.dma("sync", "sltm", ln_tm.ap()[tt * 128:(tt + 1) * 128, :], t_lno[:, :])
            for ft in range(NFT):
                g.wait("sync", "tensor")
                g.wait("sync", "sltm")
                g.dma_tr("sync", "l1t", ln1t[:, ft * TC:(ft + 1) * TC],
                         ln_tm.ap()[:, ft * 128:(ft + 1) * 128])
            ln1t_ready = g._c("l1t")

            for dt in range(ndtc):
                bi = dt % 2
                g.wait("sync", "tensor")
                g.dma("sync", "wbf%d" % bi, w_bf[bi][:, :],
                      wxbf.ap()[:, dt * 2048:(dt + 1) * 2048])
                g.wait("tensor", "wbf%d" % bi)
                g.wait("tensor", "l1t", ln1t_ready)
                g.wait("tensor", "vector")
                for kt in range(NFT):
                    st_, sp_ = kt == 0, kt == NFT - 1
                    if sp_:
                        g.op("tensor", (lambda t, kt=kt, bi=bi, st_=st_: mm_pair(
                            t, ps_mm, w_bf[bi][:, kt * 128:(kt + 1) * 128],
                            ln1t[:, kt * TC:(kt + 1) * TC], st_, True)), selfwait=False)
                    else:
                        g.silent("tensor", (lambda t, kt=kt, bi=bi, st_=st_: mm_pair(
                            t, ps_mm, w_bf[bi][:, kt * 128:(kt + 1) * 128],
                            ln1t[:, kt * TC:(kt + 1) * TC], st_, False)))
                g.wait("vector", "tensor")
                g.op("vector", lambda v, dt=dt: v.tensor_copy(
                    t_xr[:, 0:DC - 1], xpre[:, dt * (DC - 1):(dt + 1) * (DC - 1)]))
                g.op("vector", lambda v: v.tensor_copy(t_xr[:, DC - 1:DC - 1 + TC], ps_mm[:, :]))
                g.op("vector", lambda v, dt=dt: v.tensor_copy(
                    xpre[:, dt * (DC - 1):(dt + 1) * (DC - 1)], t_xr[:, TC:TC + DC - 1]))
                g.op("vector", lambda v, dt=dt: v.tensor_scalar(
                    t_cf[:, :], t_xr[:, 0:TC], p_convw[:, dt * DC:dt * DC + 1], None, ALU.mult))
                for j in range(1, DC):
                    g.op("vector", lambda v, j=j, dt=dt: v.scalar_tensor_tensor(
                        t_cf[:, :], t_xr[:, j:j + TC], p_convw[:, dt * DC + j:dt * DC + j + 1],
                        t_cf[:, :], ALU.mult, ALU.add))
                g.wait("scalar", "vector")
                g.wait("scalar", "sxld")
                g.wait("scalar", "tensor")
                g.op("scalar", lambda s, dt=dt: s.activation(
                    t_xld[:, :], t_cf[:, :], AF.Silu, bias=p_convb[:, dt:dt + 1]))
                g.wait("sync", "scalar")
                g.dma("sync", "sxld", xld_sc.ap()[dt * 128:(dt + 1) * 128, :], t_xld[:, :])
                g.wait("tensor", "scalar")
                st_, sp_ = dt == 0, dt == ndtc - 1
                g.silent("tensor", (lambda t, dt=dt, st_=st_, sp_=sp_: mm_pair(
                    t, ps_d, xp_dt_l[:, dt * DTR:(dt + 1) * DTR], t_xld, st_, sp_)))
                g.silent("tensor", (lambda t, dt=dt, st_=st_, sp_=sp_: mm_pair(
                    t, ps_bc, xp_b_l[:, dt * DS:(dt + 1) * DS], t_xld, st_, sp_, pl=DS)))
                g.op("tensor", (lambda t, dt=dt, st_=st_, sp_=sp_: mm_pair(
                    t, ps_c, xp_c_l[:, dt * DS:(dt + 1) * DS], t_xld, st_, sp_,
                    pl=DS)), selfwait=False)
            g.wait("scalar", "tensor")
            g.wait("scalar", "vector")
            if iap:
                g.wait("scalar", iap)
            g.op("scalar", lambda s: s.copy(t_cf[:, :], ps_d[:, :]))
            g.op("scalar", lambda s: s.copy(t_bc64[0:DS, :], ps_bc[0:DS, :]))
            g.op("scalar", lambda s: s.copy(t_bc64[32:32 + DS, :], ps_c[0:DS, :]))
            g.wait("gpsimd", "scalar")
            iscatter(g, ia, XDBL, k * 192, t_cf[:, :], GC_XW0, 128)
            iscatter(g, ia, XDBL, k * 192, t_bc64[0:64, 0:TC], GC_XW1, 64)
            g.wait("gpsimd", ia)
            flag_push(g, 0, k, ia)

        def prescan_x(g, k):
            ia = "ia%d" % k
            g.wait("gpsimd", "vector")
            poll(g, 0, k, list(range(ndie)), lw=ia)
            for j in range(ndie):
                g.wait("gpsimd", ia)
                igather(g, ia, xdl_a, XDBL, k * 192, GC_XRA[j], 128, accum=(j > 0))
                g.wait("gpsimd", ia)
                igather(g, ia, xdl_b, XDBL, k * 192, GC_XRB[j], 64, accum=(j > 0))
            g.wait("vector", ia)
            g.wait("vector", "tensor")
            g.op("vector", lambda v: v.tensor_copy(dtlo_bf[:, :], xdl_a))
            g.op("vector", lambda v: v.tensor_copy(bmat[:, :], xdl_b[0:DS, 0:TC]))
            g.op("vector", lambda v: v.tensor_copy(cmat[:, :], xdl_b[32:32 + DS, 0:TC]))
            g.wait("tensor", "vector")
            g.wait("tensor", "scalar")
            for n in range(DS):
                g.op("tensor", (lambda t, n=n: mm_pair(
                    t, ps_bc, p_sel[:, n * 128:(n + 1) * 128], bmat, True, True)), selfwait=False)
                g.wait("scalar", "tensor")
                g.op("scalar", lambda s, n=n: s.copy(b_bc[:, n * TC:(n + 1) * TC], ps_bc[:, :]))
                g.wait("tensor", "scalar")
                g.op("tensor", (lambda t, n=n: mm_pair(
                    t, ps_bc, p_sel[:, n * 128:(n + 1) * 128], cmat, True, True)), selfwait=False)
                g.wait("scalar", "tensor")
                g.op("scalar", lambda s, n=n: s.copy(c_bc[:, n * TC:(n + 1) * TC], ps_bc[:, :]))
                g.wait("tensor", "scalar")

        def zstage(g, k):
            for dt in range(ndtc):
                bi = dt % 2
                g.wait("sync", "vector")
                g.dma("sync", "wf32", w_f32[:, :], wz_blk[:, dt * 2048:(dt + 1) * 2048])
                g.wait("vector", "wf32")
                g.wait("vector", "tensor")
                g.op("vector", lambda v, bi=bi: v.tensor_copy(w_bf[bi][:, :], w_f32[:, :]))
                g.wait("tensor", "vector")
                g.wait("tensor", "scalar")
                for kt in range(NFT):
                    if kt == NFT - 1:
                        g.op("tensor", (lambda t, kt=kt, bi=bi: mm_pair(
                            t, ps_mm, w_bf[bi][:, kt * 128:(kt + 1) * 128],
                            ln1t[:, kt * TC:(kt + 1) * TC], kt == 0, True)), selfwait=False)
                    else:
                        g.silent("tensor", (lambda t, kt=kt, bi=bi: mm_pair(
                            t, ps_mm, w_bf[bi][:, kt * 128:(kt + 1) * 128],
                            ln1t[:, kt * TC:(kt + 1) * TC], kt == 0, False)))
                g.wait("scalar", "tensor")
                g.wait("scalar", "ssz")
                g.op("scalar", lambda s: s.activation(t_sz[:, :], ps_mm[:, :], AF.Silu))
                g.wait("sync", "scalar")
                g.dma("sync", "ssz", sz_sc.ap()[dt * 128:(dt + 1) * 128, :], t_sz[:, :])

        def scanstage(g, k, with_out):
            for dt in range(ndtc):
                g.wait("tensor", "scalar")
                g.wait("tensor", "vector")
                g.op("tensor", (lambda t, dt=dt: mm_pair(
                    t, ps_d, dt_lhs[:, dt * 128:(dt + 1) * 128], dtlo_bf, True, True)), selfwait=False)
                g.wait("scalar", "tensor")
                g.wait("scalar", "vector")
                g.op("scalar", lambda s, dt=dt: s.activation(
                    t_sp[:, :], ps_d[:, :], AF.Abs, bias=p_dtb[:, dt:dt + 1]))
                g.op("scalar", lambda s: s.activation(t_sp[:, :], t_sp[:, :], AF.Exp, scale=-1.0))
                g.op("scalar", lambda s: s.activation(t_sp[:, :], t_sp[:, :], AF.Ln, bias=1.0))
                g.wait("vector", "scalar")
                g.op("vector", lambda v, dt=dt: v.tensor_scalar(
                    t_cf[:, :], ps_d[:, :], p_dtb[:, dt:dt + 1], 0.0, ALU.add, ALU.max))
                g.op("vector", lambda v: v.tensor_tensor(t_dl[:, :], t_cf[:, :], t_sp[:, :], ALU.add))
                g.wait("sync", "vector")
                g.wait("sync", "tensor")
                g.wait("sync", "sxld")
                g.dma("sync", "lxld", t_xld[:, :], xld_sc.ap()[dt * 128:(dt + 1) * 128, :])
                g.wait("vector", "lxld")
                g.op("vector", lambda v: v.tensor_tensor(t_u[:, :], t_dl[:, :], t_xld[:, :], ALU.mult))
                for n in range(DS):
                    col = dt * DS + n
                    nbi = n % 2
                    g.wait("scalar", "vector")
                    g.op("scalar", lambda s, col=col, nbi=nbi: s.activation(
                        t_a[nbi][:, :], t_dl[:, :], AF.Exp, scale=p_negA[:, col:col + 1]))
                    g.wait("vector", "scalar")
                    g.op("vector", lambda v, n=n: v.tensor_tensor(
                        t_b[:, :], t_u[:, :], b_bc[:, n * TC:(n + 1) * TC], ALU.mult))
                    g.op("vector", lambda v, col=col, nbi=nbi: v.tensor_tensor_scan(
                        t_h[:, :], t_a[nbi][:, :], t_b[:, :], carry[:, col:col + 1],
                        ALU.mult, ALU.add))
                    g.op("vector", lambda v, col=col: v.tensor_copy(
                        carry[:, col:col + 1], t_h[:, TC - 1:TC]))
                    if with_out:
                        g.op("vector", lambda v, n=n: v.tensor_tensor(
                            t_ch[:, :], t_h[:, :], c_bc[:, n * TC:(n + 1) * TC], ALU.mult))
                        if n == 0:
                            g.op("vector", lambda v: v.tensor_copy(t_y[:, :], t_ch[:, :]))
                        else:
                            g.op("vector", lambda v: v.tensor_tensor(
                                t_y[:, :], t_y[:, :], t_ch[:, :], ALU.add))
                if with_out:
                    g.wait("sync", "vector")
                    g.wait("sync", "ssz")
                    g.dma("sync", "lsz", t_sz[:, :], sz_sc.ap()[dt * 128:(dt + 1) * 128, :])
                    g.op("vector", lambda v, dt=dt: v.scalar_tensor_tensor(
                        t_y[:, :], t_xld[:, :], p_dvec[:, dt:dt + 1], t_y[:, :],
                        ALU.mult, ALU.add))
                    g.wait("vector", "lsz")
                    if spill:
                        g.wait("vector", "syg")
                        g.op("vector", lambda v: v.tensor_tensor(
                            yg8[:, 0:TC], t_y[:, :], t_sz[:, :], ALU.mult))
                        g.wait("sync", "vector")
                        g.dma("sync", "syg", yg_sc.ap()[dt * 128:(dt + 1) * 128, :], yg8[:, 0:TC])
                    else:
                        g.op("vector", lambda v, dt=dt: v.tensor_tensor(
                            yg8[:, dt * TC:(dt + 1) * TC], t_y[:, :], t_sz[:, :], ALU.mult))

        def outstage(g, k):
            fi, fip = "if%d" % k, ("if%d" % (k - 1) if k % 4 else None)
            if fip:
                g.wait("scalar", fip)
            mmc = {}
            for mt in range(NFT):
                g.wait("tensor", "scalar")
                for dt in range(ndtc):
                    bi = dt % 2
                    g.wait("sync", "vector")
                    g.dma("sync", "wf32", w_f32[:, 0:128],
                          wot_blk[:, dt * 2048 + mt * 128:dt * 2048 + (mt + 1) * 128])
                    g.wait("vector", "wf32")
                    g.wait("vector", "tensor")
                    g.op("vector", lambda v, bi=bi: v.tensor_copy(w_bf[bi][:, 0:128], w_f32[:, 0:128]))
                    yb = dt % 2
                    gi_ = mt * ndtc + dt
                    if spill:
                        if gi_ >= 2:
                            g.wait("sync", "tensor", mmc[gi_ - 2])
                        g.wait("sync", "syg")
                        g.dma("sync", "lyt%d" % yb, t_yt[yb][:, :],
                              yg_sc.ap()[dt * 128:(dt + 1) * 128, :])
                        g.wait("tensor", "lyt%d" % yb)
                    g.wait("tensor", "vector")
                    g.op("tensor", (lambda t, bi=bi, dt=dt, yb=yb: mm_pair(
                        t, ps_mm, w_bf[bi][:, 0:128],
                        (t_yt[yb][:, :] if spill else yg8[:, dt * TC:(dt + 1) * TC]),
                        dt == 0, dt == ndtc - 1)), selfwait=False)
                    mmc[gi_] = g._c("tensor")
                g.wait("scalar", "tensor")
                g.wait("scalar", fi)
                g.op("scalar", lambda s: s.copy(t_sum[:, 0:TC], ps_mm[:, :]))
                g.wait("gpsimd", "scalar")
                iscatter(g, fi, OPART, k * DM + mt * 128, t_sum[:, 0:TC], GC_OW, 128)
            g.wait("gpsimd", fi)
            flag_push(g, 1, k, fi)

        def ownstage(g, k):
            io = "io%d" % k
            poll(g, 1, k, list(range(ndie)), lw="if%d" % k)
            for mt in range(NFT):
                g.wait("gpsimd", "scalar")
                for j in range(ndie):
                    g.wait("gpsimd", io)
                    igather(g, io, t_sum[:, 0:TC], OPART, k * DM + mt * 128,
                            GC_OR[j], 128, accum=(j > 0))
                g.wait("scalar", io)
                g.wait("scalar", "sfm")
                g.op("scalar", lambda s: s.copy(t_ev[:, :], t_sum[:, 0:TC]))
                g.wait("sync", "scalar")
                g.dma("sync", "sfm", fm_sc.ap()[mt * 128:(mt + 1) * 128, :], t_ev[:, :])
            for tt in range(NTT):
                g.wait("sync", "sfm")
                g.wait("sync", "scalar")
                g.wait("sync", "vector")
                g.wait("sync", "sltm")
                g.wait("sync", "sx2")
                g.dma_tr("sync", "tlo", t_lno[:, :], fm_sc.ap()[:, tt * 128:(tt + 1) * 128])
                g.dma("sync", "xtm", t_ln[:, :],
                      xmy[(k // ndie) * TC + tt * 128:(k // ndie) * TC + (tt + 1) * 128, :])
                g.wait("vector", "tlo")
                g.wait("vector", "xtm")
                g.wait("scalar", "tlo")
                ln_freedim(g, t_lno[:, :], t_lnsq[:, :])
                g.op("vector", lambda v: v.tensor_tensor(t_ln[:, :], t_ln[:, :], t_lnsq[:, :], ALU.add))
                g.wait("sync", "vector")
                g.dma("sync", "sx2", x2_sc.ap()[tt * 128:(tt + 1) * 128, :], t_ln[:, :])
                g.wait("scalar", "vector")
                ln_freedim(g, t_ln[:, :], t_lnsq[:, :])
                g.wait("vector", "sltm")
                g.op("vector", lambda v: v.tensor_copy(t_lno[:, :], t_lnsq[:, :]))
                g.wait("sync", "vector")
                g.dma("sync", "sltm", ln_tm.ap()[tt * 128:(tt + 1) * 128, :], t_lno[:, :])
            for ft in range(NFT):
                g.wait("sync", "sltm")
                g.wait("sync", "tensor")
                g.dma_tr("sync", "l1t", ln2t[:, ft * TC:(ft + 1) * TC],
                         ln_tm.ap()[:, ft * 128:(ft + 1) * 128])
            g.wait("gpsimd", "l1t")
            for ft in range(NFT):
                iscatter(g, io, LN2X, k * DM + ft * 128,
                         ln2t[:, ft * TC:(ft + 1) * TC], GC_L2W, 128)
            g.wait("gpsimd", io)
            flag_push(g, 2, k, io)
            g.wait("gpsimd", io)
            g.wait("sync", io)

        def poll2(g, k):
            poll(g, 2, k, [k % ndie])

        def ffnstage(g, k):
            fi = "if%d" % k
            g.wait("gpsimd", "sync")
            opair = (k % ndie) // 2 if ndie == 4 else 0
            for ft in range(NFT):
                g.wait("gpsimd", "tensor")
                igather(g, fi, ln2t[:, ft * TC:(ft + 1) * TC], LN2X,
                        k * DM + ft * 128, GC_L2R[opair], 128)
            g.wait("tensor", fi)
            for mt in range(nitp):
                bi = mt % 2
                g.wait("sync", "tensor")
                g.dma("sync", "wbf%d" % bi, w_bf[bi][:, :], wgbf.ap()[:, mt * 2048:(mt + 1) * 2048])
                g.wait("sync", "scalar")
                g.wait("sync", "vector")
                g.dma("sync", "tlo", t_lno[:, :], wubf.ap()[:, mt * 2048:(mt + 1) * 2048])
                g.wait("tensor", "wbf%d" % bi)
                g.wait("tensor", "tlo")
                g.wait("tensor", "vector")
                g.wait("tensor", "scalar")
                for kt in range(NFT):
                    if kt == NFT - 1:
                        g.op("tensor", (lambda t, kt=kt, bi=bi: mm_pair(
                            t, ps_mm, w_bf[bi][:, kt * 128:(kt + 1) * 128],
                            ln2t[:, kt * TC:(kt + 1) * TC], kt == 0, True)), selfwait=False)
                        g.op("tensor", (lambda t, kt=kt: mm_pair(
                            t, ps_d, t_lno[:, kt * 128:(kt + 1) * 128],
                            ln2t[:, kt * TC:(kt + 1) * TC], kt == 0, True)), selfwait=False)
                    else:
                        g.silent("tensor", (lambda t, kt=kt, bi=bi: mm_pair(
                            t, ps_mm, w_bf[bi][:, kt * 128:(kt + 1) * 128],
                            ln2t[:, kt * TC:(kt + 1) * TC], kt == 0, False)))
                        g.silent("tensor", (lambda t, kt=kt: mm_pair(
                            t, ps_d, t_lno[:, kt * 128:(kt + 1) * 128],
                            ln2t[:, kt * TC:(kt + 1) * TC], kt == 0, False)))
                g.wait("scalar", "tensor")
                g.wait("scalar", "vector")
                g.op("scalar", lambda s: s.activation(t_ev[:, :], ps_mm[:, :], AF.Silu))
                g.wait("vector", "scalar")
                if spill:
                    g.wait("vector", "shh")
                    g.op("vector", lambda v, mt=mt: v.scalar_tensor_tensor(
                        h_ffn[:, 0:TC], t_ev[:, :], p_hm[:, mt:mt + 1],
                        ps_d[:, :], ALU.mult, ALU.mult))
                    g.wait("sync", "vector")
                    g.dma("sync", "shh", h_sc.ap()[mt * 128:(mt + 1) * 128, :], h_ffn[:, 0:TC])
                else:
                    g.op("vector", lambda v, mt=mt: v.scalar_tensor_tensor(
                        h_ffn[:, mt * TC:(mt + 1) * TC], t_ev[:, :], p_hm[:, mt:mt + 1],
                        ps_d[:, :], ALU.mult, ALU.mult))
            hdone = g._c("vector")
            mmc = {}
            for mt2 in range(NFT):
                g.wait("tensor", "scalar")
                g.wait("sync", "tensor")
                g.dma("sync", "dnl", dn_l, wdbf.ap()[:, mt2 * (nitp * 128):(mt2 + 1) * (nitp * 128)])
                g.wait("tensor", "dnl")
                g.wait("tensor", "vector", hdone)
                for it in range(nitp):
                    yb = it % 2
                    gi_ = mt2 * nitp + it
                    if spill:
                        if gi_ >= 2:
                            g.wait("sync", "tensor", mmc[gi_ - 2])
                        g.wait("sync", "shh")
                        g.dma("sync", "lyt%d" % yb, t_yt[yb][:, :],
                              h_sc.ap()[it * 128:(it + 1) * 128, :])
                        g.wait("tensor", "lyt%d" % yb)
                    g.op("tensor", (lambda t, it=it, mt2=mt2, yb=yb: mm_pair(
                        t, ps_mm, dn_l[:, it * 128:(it + 1) * 128],
                        (t_yt[yb][:, :] if spill else h_ffn[:, it * TC:(it + 1) * TC]),
                        it == 0, it == nitp - 1)), selfwait=False)
                    mmc[gi_] = g._c("tensor")
                g.wait("scalar", "tensor")
                g.wait("scalar", fi)
                g.op("scalar", lambda s: s.copy(t_sum[:, 0:TC], ps_mm[:, :]))
                g.wait("gpsimd", "scalar")
                iscatter(g, fi, DPART, k * DM + mt2 * 128, t_sum[:, 0:TC], GC_OW, 128)
            g.wait("gpsimd", fi)
            flag_push(g, 3, k, fi)
            g.wait("gpsimd", fi)
            g.wait("sync", "tensor")

        def own2stage(g, k):
            io = "io%d" % k
            poll(g, 3, k, list(range(ndie)), lw="if%d" % k)
            for mt in range(NFT):
                g.wait("gpsimd", "scalar")
                for j in range(ndie):
                    g.wait("gpsimd", io)
                    igather(g, io, t_sum[:, 0:TC], DPART, k * DM + mt * 128,
                            GC_OR[j], 128, accum=(j > 0))
                g.wait("scalar", io)
                g.wait("scalar", "sfm")
                g.op("scalar", lambda s: s.copy(t_ev[:, :], t_sum[:, 0:TC]))
                g.wait("sync", "scalar")
                g.dma("sync", "sfm", fm_sc.ap()[mt * 128:(mt + 1) * 128, :], t_ev[:, :])
            for tt in range(NTT):
                g.wait("sync", "sfm")
                g.wait("sync", "scalar")
                g.wait("sync", "vector")
                g.wait("sync", "sout")
                g.wait("sync", "sltm")
                g.wait("sync", "sx2")
                g.dma_tr("sync", "tlo", t_lno[:, :], fm_sc.ap()[:, tt * 128:(tt + 1) * 128])
                g.dma("sync", "xtm", t_ln[:, :], x2_sc.ap()[tt * 128:(tt + 1) * 128, :])
                g.wait("vector", "tlo")
                g.wait("vector", "xtm")
                g.wait("scalar", "tlo")
                ln_freedim(g, t_lno[:, :], t_lnsq[:, :])
                g.op("vector", lambda v: v.tensor_tensor(t_ln[:, :], t_ln[:, :], t_lnsq[:, :], ALU.add))
                g.wait("sync", "vector")
                g.dma("sync", "sout",
                      yout[(k // ndie) * TC + tt * 128:(k // ndie) * TC + (tt + 1) * 128, :],
                      t_ln[:, :])
            g.wait("scalar", io)

        for k in range(nkc):
            if "p" in stages:
                fguard(k, lambda g, k=k: prescan(g, k), cls=("a", k))
            if "z" in stages:
                fguard(16 + k, lambda g, k=k: zstage(g, k), cls=("f", k))
            if "x" in stages:
                fguard(k, lambda g, k=k: prescan_x(g, k), cls=("a", k))
            if "l" in stages:
                fguard(8 + k, lambda g, k=k: scanstage(g, k, False), cls=("l", k))
            if "s" in stages:
                fguard(16 + k, lambda g, k=k: scanstage(g, k, True), cls=("f", k))
            if "o" in stages:
                fguard(16 + k, lambda g, k=k: outstage(g, k), cls=("f", k))
            if "w" in stages or "n" in stages or "2" in stages:
                fguard(24 + k, lambda g, k=k: ownstage(g, k), cls=("o", k))
            if "n" in stages:
                fguard(32 + k, lambda g, k=k: poll2(g, k), cls=("p2", k))
                fguard(16 + k, lambda g, k=k: ffnstage(g, k), cls=("f", k))
            if "2" in stages:
                fguard(24 + k, lambda g, k=k: own2stage(g, k), cls=("o", k))

        P.wait("sync", "sout")

        if dbg:
            for j in range(4):
                P.wait("gpsimd", "dbgg")
                igather(P, "dbgg", t_cf[:, :], XDBL, 0, GC_XRA[j], 128)
                P.wait("sync", "dbgg")
                P.dma("sync", "dbgs", dbg_o[j * 128:(j + 1) * 128, :], t_cf[:, :])
            P.wait("sync", "dbgs")

        for name in list(P.cnt):
            P.sems[name] = e(nc.semaphore("S_" + name))

        block = e(nc.Block())

        @block.sync
        def _(sync):
            for fn in P.q["sync"]:
                fn(sync)

        @block.tensor
        def _(tensor):
            for fn in P.q["tensor"]:
                fn(tensor)

        @block.vector
        def _(vector):
            for fn in P.q["vector"]:
                fn(vector)

        @block.scalar
        def _(scalar):
            for fn in P.q["scalar"]:
                fn(scalar)

        @block.gpsimd
        def _(gpsimd):
            for fn in P.q["gpsimd"]:
                fn(gpsimd)

    return nc


def _host_layout_v1(inputs, ndie=4, nonce=1000):
    import ml_dtypes
    ndtc = NDT // ndie
    nitp = (NIT // ndie) + 1
    interp = nitp * 128 * ndie

    x = np.ascontiguousarray(np.asarray(inputs["L0_attn_hidden_states"], np.float32).reshape(L, DM))
    wip = np.asarray(inputs["L0_attn_in_proj_w"], np.float32)
    xp = np.asarray(inputs["L0_attn_x_proj_w"], np.float32)
    wxT = wip[0:DI].T
    wzT = wip[DI:2 * DI].T
    woT = np.asarray(inputs["L0_attn_out_proj_w"], np.float32).T

    def blk_w(wT):
        return wT.reshape(NFT, 128, NDT, 128).transpose(1, 2, 0, 3).reshape(128, NDT * 2048)

    wx_full = blk_w(wxT)
    wz_full = blk_w(wzT)
    wot_full = np.ascontiguousarray(
        woT.reshape(NDT, 128, NFT, 128).transpose(1, 0, 2, 3).reshape(128, NDT * 2048))

    def blk_di(w):
        wd = w.shape[1]
        return w.reshape(NDT, 128, wd).transpose(1, 0, 2).reshape(128, NDT * wd)

    xp_dt_f = blk_di(np.ascontiguousarray(xp[0:DTR].T))
    xp_b_f = blk_di(np.ascontiguousarray(xp[DTR:DTR + DS].T))
    xp_c_f = blk_di(np.ascontiguousarray(xp[DTR + DS:DTR + 2 * DS].T))
    convw_f = blk_di(np.asarray(inputs["L0_attn_conv_w"], np.float32))
    convb_f = blk_di(np.asarray(inputs["L0_attn_conv_b"], np.float32).reshape(DI, 1))
    dtb_f = blk_di(np.asarray(inputs["L0_attn_dt_proj_b"], np.float32).reshape(DI, 1))
    alog_f = blk_di(np.asarray(inputs["L0_attn_A_log"], np.float32))
    dvec_f = blk_di(np.asarray(inputs["L0_attn_D"], np.float32).reshape(DI, 1))
    wdtt_f = np.ascontiguousarray(np.asarray(inputs["L0_attn_dt_proj_w"], np.float32).T)

    gi = np.asarray(inputs["L0_ffn_gate_idx"], np.int32)
    ui = np.asarray(inputs["L0_ffn_up_idx"], np.int32)
    dix = np.asarray(inputs["L0_ffn_down_idx"], np.int32)
    gi_p = np.zeros((interp, DM), np.int32)
    ui_p = np.zeros((interp, DM), np.int32)
    di_p = np.zeros((DM, interp), np.int32)
    gi_p[0:INTER] = gi
    ui_p[0:INTER] = ui
    di_p[:, 0:INTER] = dix
    nit_p = interp // 128
    g_all = gi_p.reshape(nit_p, 128, NFT, 128).transpose(3, 0, 2, 1).reshape(128, nit_p * NFT * 128)
    u_all = ui_p.reshape(nit_p, 128, NFT, 128).transpose(3, 0, 2, 1).reshape(128, nit_p * NFT * 128)
    d_all = di_p.reshape(NFT, 128, nit_p, 128).transpose(3, 0, 2, 1).reshape(128, NFT * nit_p * 128)

    sel = np.zeros((DS, DS * 128), ml_dtypes.bfloat16)
    for n in range(DS):
        sel[n, n * 128:(n + 1) * 128] = 1.0
    lut128 = np.ascontiguousarray(np.broadcast_to(
        np.asarray(inputs["L0_ffn_lut"], np.float32).reshape(1, NLUT), (128, NLUT)))

    fvals = np.zeros((128, 8 * 16), np.int32)
    for k in range(8):
        fvals[:, k * 16:(k + 1) * 16] = np.int32(nonce + k)

    maps = []
    ncore = 8
    for c in range(ncore):
        die = c // ndie if ndie == 4 else c // ndie
        dc = c % ndie
        dsl = slice(dc * ndtc * 2048, (dc + 1) * ndtc * 2048)
        it0 = dc * nitp
        m = {
            "xin": x,
            "xmy": np.ascontiguousarray(
                np.concatenate([x[k * TC:(k + 1) * TC]
                                for k in range(8) if k % ndie == dc], axis=0)
                if ndie < 4 else x[c * TC:(c + 1) * TC]),
            "wx_blk": np.ascontiguousarray(wx_full[:, dsl]),
            "wz_blk": np.ascontiguousarray(wz_full[:, dsl]),
            "wot_blk": np.ascontiguousarray(wot_full[:, dsl]),
            "wxp_dt": np.ascontiguousarray(xp_dt_f[:, dc * ndtc * DTR:(dc + 1) * ndtc * DTR]),
            "wxp_b": np.ascontiguousarray(xp_b_f[:, dc * ndtc * DS:(dc + 1) * ndtc * DS]),
            "wxp_c": np.ascontiguousarray(xp_c_f[:, dc * ndtc * DS:(dc + 1) * ndtc * DS]),
            "wdtt": np.ascontiguousarray(wdtt_f[:, dc * ndtc * 128:(dc + 1) * ndtc * 128]),
            "p_convw_in": np.ascontiguousarray(convw_f[:, dc * ndtc * DC:(dc + 1) * ndtc * DC]),
            "p_convb_in": np.ascontiguousarray(convb_f[:, dc * ndtc:(dc + 1) * ndtc]),
            "p_dtb_in": np.ascontiguousarray(dtb_f[:, dc * ndtc:(dc + 1) * ndtc]),
            "p_alog_in": np.ascontiguousarray(alog_f[:, dc * ndtc * DS:(dc + 1) * ndtc * DS]),
            "p_dvec_in": np.ascontiguousarray(dvec_f[:, dc * ndtc:(dc + 1) * ndtc]),
            "lutv": lut128,
            "g_blk": np.ascontiguousarray(g_all[:, it0 * NFT * 128:(it0 + nitp) * NFT * 128]),
            "u_blk": np.ascontiguousarray(u_all[:, it0 * NFT * 128:(it0 + nitp) * NFT * 128]),
            "d_blk": np.ascontiguousarray(
                d_all.reshape(128, NFT, nit_p, 128)[:, :, it0:it0 + nitp, :]
                .reshape(128, NFT * nitp * 128)),
            "selmat": sel,
            "fvals_in": fvals,
        }
        hm = np.zeros((128, nitp), np.float32)
        for j in range(nitp):
            gidx0 = (it0 + j) * 128
            for p in range(128):
                hm[p, j] = 1.0 if (gidx0 + p) < INTER else 0.0
        m["hmask_in"] = hm
        fl = np.zeros((1, 64), np.int32)
        for k in range(8):
            if ndie == 4:
                active = (die == 1) or (k < 4)
                full = (die == 0 and k < 4) or (die == 1 and k >= 4)
            else:
                active = True
                full = True
            lite = active and not full
            own = full and (dc == (k % ndie))
            fl[0, k] = 1 if (lite or full) else 0
            fl[0, 8 + k] = 1 if lite else 0
            fl[0, 16 + k] = 1 if full else 0
            fl[0, 24 + k] = 1 if own else 0
            fl[0, 32 + k] = 1 if (full and not own) else 0
        m["flags_in"] = fl
        gidx = np.zeros((128, 32), np.int64)
        mypair = (c // 2) % 2 if ndie == 4 else 0
        if ndie == 4:
            lpair_global = c // 2
            mydom = DOM_OF_LPAIR[lpair_global]
            otherpair_global = (c // 2) ^ 1
            otherdom = DOM_OF_LPAIR[otherpair_global]
        p_ = np.arange(128)
        rb = lambda dcj: dcj * 8 * 192

        def delta_to(dcj):
            if ndie != 4:
                return 0
            pj = (die * ndie + dcj) // 2
            dj = DOM_OF_LPAIR[pj]
            return (HBM_BASE[dj] - HBM_BASE[mydom])

        gidx[:, GC_XW0] = dc * 8 * 192 + p_
        gidx[0:64, GC_XW1] = dc * 8 * 192 + 128 + p_[0:64]
        for j in range(ndie):
            d_rows = delta_to(j) // (TC * 4)
            gidx[:, GC_XRA[j]] = d_rows + j * 8 * 192 + p_
            gidx[0:64, GC_XRB[j]] = d_rows + j * 8 * 192 + 128 + p_[0:64]
        gidx[0:16, GC_FWL] = dc * 16 + p_[0:16]
        if ndie == 4:
            d_flg = (HBM_BASE[otherdom] - HBM_BASE[mydom]) // 4096
        else:
            d_flg = 0
        gidx[0:16, GC_FWR] = d_flg + dc * 16 + p_[0:16]
        gidx[:, GC_OW] = dc * 8 * DM + p_
        for j in range(ndie):
            d_rows = delta_to(j) // (TC * 4)
            gidx[:, GC_OR[j]] = d_rows + j * 8 * DM + p_
        gidx[:, GC_L2W] = p_
        if ndie == 4:
            for op_ in range(2):
                ownerpair_global = die * 2 + op_
                dj = DOM_OF_LPAIR[ownerpair_global]
                gidx[:, GC_L2R[op_]] = (HBM_BASE[dj] - HBM_BASE[mydom]) // (TC * 4) + p_
        else:
            gidx[:, GC_L2R[0]] = p_
            gidx[:, GC_L2R[1]] = p_
        assert gidx.max() < 2**31 and gidx.min() > -2**31
        m["gidx_in"] = gidx.astype(np.int32)
        maps.append(m)
    return maps


_NC_CACHE = {}


def _host_layout(inputs):
    import ml_dtypes
    x = np.ascontiguousarray(np.asarray(inputs["L0_attn_hidden_states"], np.float32).reshape(L, DM))
    wip = np.asarray(inputs["L0_attn_in_proj_w"], np.float32)
    xp = np.asarray(inputs["L0_attn_x_proj_w"], np.float32)

    def blk_w(wT):
        return np.ascontiguousarray(
            wT.reshape(NFT, 128, NDT, 128).transpose(1, 2, 0, 3).reshape(128, NDT * 2048))

    def blk_di(w):
        wd = w.shape[1]
        return np.ascontiguousarray(
            w.reshape(NDT, 128, wd).transpose(1, 0, 2).reshape(128, NDT * wd))

    wxT = wip[0:DI].T
    wzT = wip[DI:2 * DI].T
    woT = np.asarray(inputs["L0_attn_out_proj_w"], np.float32).T
    wot_blk = np.ascontiguousarray(
        woT.reshape(NDT, 128, NFT, 128).transpose(1, 0, 2, 3).reshape(128, NDT * 2048))

    gi = np.asarray(inputs["L0_ffn_gate_idx"], np.int32)
    ui = np.asarray(inputs["L0_ffn_up_idx"], np.int32)
    di = np.asarray(inputs["L0_ffn_down_idx"], np.int32)
    g_blk = np.ascontiguousarray(
        gi.reshape(NIT, 128, NFT, 128).transpose(3, 0, 2, 1).reshape(128, NIT * NFT * 128))
    u_blk = np.ascontiguousarray(
        ui.reshape(NIT, 128, NFT, 128).transpose(3, 0, 2, 1).reshape(128, NIT * NFT * 128))
    d_blk = np.ascontiguousarray(
        di.reshape(NFT, 128, NIT, 128).transpose(3, 0, 2, 1).reshape(128, NFT * NIT * 128))

    sel = np.zeros((DS, DS * 128), ml_dtypes.bfloat16)
    for n in range(DS):
        sel[n, n * 128:(n + 1) * 128] = 1.0

    base = {
        "xin": x,
        "wx_blk": blk_w(wxT),
        "wz_blk": blk_w(wzT),
        "wot_blk": wot_blk,
        "wxp_dt": blk_di(xp[0:DTR].T.reshape(DI, DTR)),
        "wxp_b": blk_di(xp[DTR:DTR + DS].T.reshape(DI, DS)),
        "wxp_c": blk_di(xp[DTR + DS:DTR + 2 * DS].T.reshape(DI, DS)),
        "wdtt": np.ascontiguousarray(np.asarray(inputs["L0_attn_dt_proj_w"], np.float32).T),
        "p_convw_in": blk_di(np.asarray(inputs["L0_attn_conv_w"], np.float32)),
        "p_convb_in": blk_di(np.asarray(inputs["L0_attn_conv_b"], np.float32).reshape(DI, 1)),
        "p_dtb_in": blk_di(np.asarray(inputs["L0_attn_dt_proj_b"], np.float32).reshape(DI, 1)),
        "p_alog_in": blk_di(np.asarray(inputs["L0_attn_A_log"], np.float32)),
        "p_dvec_in": blk_di(np.asarray(inputs["L0_attn_D"], np.float32).reshape(DI, 1)),
        "lutv": np.ascontiguousarray(np.broadcast_to(np.asarray(inputs["L0_ffn_lut"], np.float32).reshape(1, NLUT), (128, NLUT))),
        "g_blk": g_blk,
        "u_blk": u_blk,
        "d_blk": d_blk,
        "selmat": sel,
    }
    return base, x


def kernel(**inputs):
    if os.environ.get("BASS_SIM_FP64") == "1":
        base, x = _host_layout(inputs)
        if "nc0" not in _NC_CACHE:
            _NC_CACHE["nc0"] = build_program_v0()
        nc = _NC_CACHE["nc0"]
        in_maps = []
        for c in range(8):
            m = dict(base)
            m["xmy"] = np.ascontiguousarray(x[c * TC:(c + 1) * TC])
            m["kown_in"] = np.array([[c]], np.int32)
            in_maps.append(m)
        res = run_bass_kernel_spmd(nc, in_maps, core_ids=list(range(8)))
        out = np.concatenate([np.asarray(res.results[c]["yout"], np.float32)
                              for c in range(8)], axis=0)
        return out.reshape(1, L, DM)
    ndie = 2
    in_maps = _host_layout_v1(
        inputs, ndie=ndie,
        nonce=int(np.random.randint(1 << 20, 1 << 30)))
    if "nc1" not in _NC_CACHE:
        _NC_CACHE["nc1"] = build_program_v1(ndie=ndie)
    nc = _NC_CACHE["nc1"]
    res = run_bass_kernel_spmd(nc, in_maps, core_ids=list(range(8)))
    out = np.empty((L, DM), np.float32)
    for k in range(8):
        src_core = 2 * ((k // 2) % 4) + (k % 2)
        oi = k // 2
        out[k * TC:(k + 1) * TC] = np.asarray(
            res.results[src_core]["yout"], np.float32)[oi * TC:(oi + 1) * TC]
    return out.reshape(1, L, DM)


if __name__ == "__main__":
    import reference as _r
    ins = _r.setup_inputs(0)
    out = kernel(**{k: np.asarray(v) for k, v in ins.items()})
    ref = np.asarray(_r.reference(**ins))
    err = np.linalg.norm(out - ref) / np.linalg.norm(ref)
    print("rel err:", err)


# version 09 (final)
# speedup vs baseline: 4.2679x; 4.2679x over previous
import os
import numpy as np
import contextlib
import concourse.bass as bass
import concourse.mybir as mybir
from concourse.bass_utils import run_bass_kernel_spmd

F32 = mybir.dt.float32
BF16 = mybir.dt.bfloat16
I32 = mybir.dt.int32
AF = mybir.ActivationFunctionType
ALU = mybir.AluOpType
AX = mybir.AxisListType.X

L, DM, DI, DTR, DS, DC, INTER, NLUT = 8192, 2048, 4096, 128, 16, 4, 5504, 16
EPS = 1e-5
TC = 1024
NKC = L // TC
NDT = DI // 128
NFT = DM // 128
NIT = INTER // 128
NTT = TC // 128

ENGS = ("sync", "tensor", "vector", "scalar", "gpsimd")


def _impl(ctx, sc):
    if sc is None:
        return True
    if ctx is None:
        return False
    t1, k1 = ctx
    t2, k2 = sc
    if k2 > k1:
        return False
    if t2 == "a":
        return True
    if t2 == "f":
        return t1 in ("f", "o", "p2") and k1 // 2 == k2 // 2
    if t2 == "l":
        return t1 == "l"
    if t2 == "o":
        return t1 == "o" and k1 == k2
    return False


class B:

    SAFE = True

    def __init__(self):
        self.q = {g: [] for g in ENGS}
        self.cnt = {}
        self.owner = {}
        self.sems = {}
        self.kreg = {}
        self.ctx = None
        self.scls = {}

    def _c(self, name):
        return self.cnt.get(name, 0)

    def op(self, g, fn, inc=1, selfwait=True):
        sems = self.sems
        if self.SAFE and selfwait and self._c(g) > 0:
            pv = self._c(g)

            def wrun(eng, pv=pv, g=g):
                eng.wait_ge(sems[g], pv)
            self.q[g].append(wrun)
        self.cnt[g] = self._c(g) + inc
        self.owner[g] = g

        def run(eng, fn=fn, g=g, inc=inc):
            fn(eng).then_inc(sems[g], inc)
        self.q[g].append(run)
        return self.cnt[g]

    def silent(self, g, fn):
        self.q[g].append(fn)

    def wait(self, g, src, val=None):
        if src in self.scls:
            assert _impl(self.ctx, self.scls[src]), (src, self.ctx, self.scls[src])
        if val is None:
            val = self._c(src)
        sems = self.sems

        def run(eng, src=src, val=val):
            if val > 0:
                eng.wait_ge(sems[src], val)
        self.q[g].append(run)

    def dma(self, g, sem, out, in_, **kw):
        self.cnt[sem] = self._c(sem) + 16
        self.owner[sem] = g
        sems = self.sems

        def run(eng, sem=sem):
            eng.dma_start(out=out, in_=in_, **kw).then_inc(sems[sem], 16)
        self.q[g].append(run)
        return self.cnt[sem]

    def dma_tr(self, g, sem, out, in_):
        self.cnt[sem] = self._c(sem) + 16
        self.owner[sem] = g
        sems = self.sems

        def run(eng, sem=sem):
            eng.dma_start_transpose(out, in_).then_inc(sems[sem], 16)
        self.q[g].append(run)
        return self.cnt[sem]

    def guarded(self, k, builder):
        gb = B.__new__(B)
        gb.q = {g: [] for g in ENGS}
        gb.cnt = dict(self.cnt)
        gb.owner = self.owner
        gb.sems = self.sems
        gb.kreg = self.kreg
        builder(gb)
        for g in ENGS:
            span = gb.q[g]
            deltas = [(name, gb._c(name) - self._c(name))
                      for name in gb.cnt
                      if gb.owner.get(name) == g and gb._c(name) > self._c(name)]
            if not span and not deltas:
                continue
            sems = self.sems
            kreg = self.kreg

            def run(eng, g=g, span=span, deltas=deltas, k=k):
                with eng.If_eq(kreg[g], k):
                    for fn in span:
                        fn(eng)
                if deltas:
                    with eng.Else():
                        for name, d in deltas:
                            eng.sem_inc(sems[name], d)
            self.q[g].append(run)
        self.cnt = gb.cnt


def build_program_v0(nkc=NKC):
    nc = bass.Bass("TRN2", target_bir_lowering=False)

    def din(name, shape, dt=F32):
        return nc.dram_tensor(name, shape, dt, kind="ExternalInput").ap()

    xin = din("xin", [L, DM])
    xmy = din("xmy", [TC, DM])
    wx_blk = din("wx_blk", [128, NDT * 2048])
    wz_blk = din("wz_blk", [128, NDT * 2048])
    wot_blk = din("wot_blk", [128, NDT * 2048])
    wxp_dt = din("wxp_dt", [128, NDT * DTR])
    wxp_b = din("wxp_b", [128, NDT * DS])
    wxp_c = din("wxp_c", [128, NDT * DS])
    wdtt = din("wdtt", [DTR, DI])
    p_convw_in = din("p_convw_in", [128, NDT * DC])
    p_convb_in = din("p_convb_in", [128, NDT])
    p_dtb_in = din("p_dtb_in", [128, NDT])
    p_alog_in = din("p_alog_in", [128, NDT * DS])
    p_dvec_in = din("p_dvec_in", [128, NDT])
    lutv = din("lutv", [128, NLUT])
    g_blk = din("g_blk", [128, NIT * NFT * 128], I32)
    u_blk = din("u_blk", [128, NIT * NFT * 128], I32)
    d_blk = din("d_blk", [128, NFT * NIT * 128], I32)
    selmat = din("selmat", [DS, DS * 128], BF16)
    kown_in = din("kown_in", [1, 1], I32)

    yout = nc.dram_tensor("yout", [TC, DM], F32, kind="ExternalOutput").ap()

    def dscr(name, shape, dt):
        return nc.dram_tensor(name, shape, dt, kind="Internal")

    wxbf = dscr("wxbf", [128, NDT * 2048], BF16)
    xld_sc = dscr("xld_sc", [DI, TC], BF16)
    ln_tm = dscr("ln_tm", [TC, DM], BF16)
    ygate_sc = dscr("ygate_sc", [DI, TC], BF16)
    sz_sc = dscr("sz_sc", [DI, TC], BF16)
    wgbf = dscr("wgbf", [128, NIT * NFT * 128], BF16)
    wubf = dscr("wubf", [128, NIT * NFT * 128], BF16)
    wdbf = dscr("wdbf", [128, NFT * NIT * 128], BF16)
    h_sc = dscr("h_sc", [INTER, TC], BF16)
    fm_sc = dscr("fm_sc", [DM, TC], BF16)
    x2_sc = dscr("x2_sc", [TC, DM], F32)

    with contextlib.ExitStack() as stx:
        e = stx.enter_context

        def sb(name, p, f, dt):
            return e(nc.sbuf_tensor(name, [p, f], dt))

        ARENA_F = 20480
        arena = sb("arena", 128, ARENA_F, F32)
        a_bf = arena.bitcast(BF16)
        a_i32 = arena.bitcast(I32)

        def vbf(off_f32, free):
            return a_bf[0:128, off_f32 * 2:off_f32 * 2 + free]

        def vi32(off_f32, free):
            return a_i32[0:128, off_f32:off_f32 + free]

        ln1t = vbf(0, NFT * TC)
        b_bc = vbf(4096, DS * TC)
        c_bc = vbf(12288, DS * TC)
        cv_idx = [vi32(0, INTER), vi32(5504, INTER)]
        cv_ixb = vbf(11008, INTER)
        cv_acc = vbf(13760, INTER)
        cv_t = vbf(16512, INTER)
        ln2t = vbf(0, NFT * TC)
        dn_l = vbf(4096, NIT * 128)

        p_negA = sb("p_negA", 128, NDT * DS, F32)
        p_convw = sb("p_convw", 128, NDT * DC, F32)
        p_convb = sb("p_convb", 128, NDT, F32)
        p_dtb = sb("p_dtb", 128, NDT, F32)
        p_dvec = sb("p_dvec", 128, NDT, F32)
        p_sel = sb("p_sel", DS, DS * 128, BF16)
        p_lut = sb("p_lut", 128, NLUT, F32)
        p_kown = sb("p_kown", 1, 1, I32)
        carry = sb("carry", 128, NDT * DS, F32)
        dt_lhs = sb("dt_lhs", DTR, DI, BF16)
        xp_dt_l = sb("xp_dt_l", 128, NDT * DTR, BF16)
        xp_b_l = sb("xp_b_l", 128, NDT * DS, BF16)
        xp_c_l = sb("xp_c_l", 128, NDT * DS, BF16)
        dtlo_bf = sb("dtlo_bf", DTR, TC, BF16)
        bmat = sb("bmat", DS, TC, BF16)
        cmat = sb("cmat", DS, TC, BF16)
        xpre = sb("xpre", 128, NDT * (DC - 1), F32)

        t_ln = sb("t_ln", 128, DM, F32)
        t_lnsq = sb("t_lnsq", 128, DM, F32)
        t_st = sb("t_st", 128, 8, F32)
        t_lno = sb("t_lno", 128, DM, BF16)
        w_f32 = sb("w_f32", 128, DM, F32)
        w_bf = [sb("w_bf_%d" % i, 128, DM, BF16) for i in range(2)]
        t_xr = sb("t_xr", 128, TC + DC - 1, F32)
        t_cf = sb("t_cf", 128, TC, F32)
        t_sp = sb("t_sp", 128, TC, F32)
        t_dl = sb("t_dl", 128, TC, BF16)
        t_u = sb("t_u", 128, TC, BF16)
        t_xld = sb("t_xld", 128, TC, BF16)
        t_a = sb("t_a", 128, TC, F32)
        t_b = sb("t_b", 128, TC, BF16)
        t_h = sb("t_h", 128, TC, F32)
        t_ch = sb("t_ch", 128, TC, BF16)
        t_y = sb("t_y", 128, TC, BF16)
        t_yg = sb("t_yg", 128, TC, BF16)
        t_sz = sb("t_sz", 128, TC, BF16)
        t_ev = sb("t_ev", 128, TC, BF16)
        t_yt = [sb("t_yt_%d" % i, 128, TC, BF16) for i in range(2)]

        ps_mm = e(nc.psum_tensor("ps_mm", [128, TC], F32))
        ps_d = e(nc.psum_tensor("ps_d", [128, TC], F32))
        ps_bc = e(nc.psum_tensor("ps_bc", [128, TC], F32))
        ps_c = e(nc.psum_tensor("ps_c", [DS, TC], F32))

        P = B()

        def mm_pair(t, psum, lhsT, rhs, start, stop, pl=128, pbase=0):
            t.matmul(psum[pbase:pbase + pl, 0:512], lhsT, rhs[:, 0:512],
                     start=start, stop=stop)
            return t.matmul(psum[pbase:pbase + pl, 512:1024], lhsT,
                            rhs[:, 512:1024], start=start, stop=stop)

        P.dma("sync", "pld", p_sel[:, :], selmat)
        P.dma("sync", "pld", p_lut[:, :], lutv)
        P.dma("sync", "pld", p_kown[:, :], kown_in)
        P.dma("sync", "pld", p_convw[:, :], p_convw_in)
        P.dma("sync", "pld", p_convb[:, :], p_convb_in)
        P.dma("sync", "pld", p_dtb[:, :], p_dtb_in)
        P.dma("sync", "pld", p_dvec[:, :], p_dvec_in)
        P.dma("sync", "pld", p_negA[:, :], p_alog_in)

        for g in ENGS:
            P.wait(g, "pld")

            def mkload(g=g):
                def run(eng):
                    reg = eng.register("kreg_" + g).__enter__()
                    P.kreg[g] = reg
                    eng.reg_load(reg, p_kown[0:1, 0:1])
                return run
            P.silent(g, mkload())

        P.op("scalar", lambda s: s.activation(p_negA[:, :], p_negA[:, :], AF.Exp))
        P.wait("vector", "scalar")
        P.op("vector", lambda v: v.tensor_single_scalar(p_negA[:, :], p_negA[:, :], -1.0, ALU.mult))
        P.op("vector", lambda v: v.memset(carry[:, :], 0.0))
        P.op("vector", lambda v: v.memset(xpre[:, :], 0.0))

        def cast_load(dst_ap, src_ap, width):
            P.wait("sync", "vector")
            P.dma("sync", "wf32", w_f32[:, 0:width], src_ap)
            P.wait("vector", "wf32")
            P.op("vector", lambda v: v.tensor_copy(dst_ap, w_f32[:, 0:width]))

        for h in range(2):
            cast_load(dt_lhs[:, h * DM:(h + 1) * DM], wdtt[:, h * DM:(h + 1) * DM], DM)
        cast_load(xp_dt_l[:, 0:DM], wxp_dt[:, 0:DM], DM)
        cast_load(xp_dt_l[:, DM:2 * DM], wxp_dt[:, DM:2 * DM], DM)
        cast_load(xp_b_l[:, :], wxp_b[:, :], NDT * DS)
        cast_load(xp_c_l[:, :], wxp_c[:, :], NDT * DS)

        for i in range(NDT):
            bi = i % 2
            P.wait("sync", "vector")
            P.dma("sync", "wf32", w_f32[:, :], wx_blk[:, i * 2048:(i + 1) * 2048])
            P.wait("vector", "wf32")
            P.wait("vector", "swx%d" % bi)
            P.op("vector", lambda v, bi=bi: v.tensor_copy(w_bf[bi][:, :], w_f32[:, :]))
            P.wait("gpsimd", "vector")
            P.dma("gpsimd", "swx%d" % bi, wxbf.ap()[:, i * 2048:(i + 1) * 2048], w_bf[bi][:, :])

        def convert(src, dst, width_total):
            nch = width_total // INTER
            for i in range(nch):
                bi = i % 2
                P.wait("sync", "vector")
                P.dma("sync", "cvi%d" % bi, cv_idx[bi], src[:, i * INTER:(i + 1) * INTER])
                P.wait("vector", "cvi%d" % bi)
                P.wait("vector", "scv")
                P.op("vector", lambda v, bi=bi: v.tensor_copy(cv_ixb, cv_idx[bi]))
                P.op("vector", lambda v: v.tensor_scalar(
                    cv_acc, cv_ixb, 0.0, p_lut[:, 0:1], ALU.is_equal, ALU.mult))
                for vv in range(1, NLUT):
                    P.op("vector", lambda v, vv=vv: v.tensor_scalar(
                        cv_t, cv_ixb, float(vv), p_lut[:, vv:vv + 1],
                        ALU.is_equal, ALU.mult))
                    P.op("vector", lambda v: v.tensor_tensor(cv_acc, cv_acc, cv_t, ALU.add))
                P.wait("gpsimd", "vector")
                P.dma("gpsimd", "scv", dst.ap()[:, i * INTER:(i + 1) * INTER], cv_acc)

        convert(g_blk, wgbf, NIT * NFT * 128)
        convert(u_blk, wubf, NIT * NFT * 128)
        convert(d_blk, wdbf, NFT * NIT * 128)
        P.wait("vector", "scv")
        P.wait("sync", "swx0")
        P.wait("sync", "swx1")
        P.wait("sync", "scv")

        def ln_freedim(src, dst):
            P.op("vector", lambda v: v.reduce_sum(t_st[:, 0:1], src, axis=AX))
            P.op("scalar", lambda s: s.activation(t_lnsq[:, :], src, AF.Square))
            P.wait("vector", "scalar")
            P.op("vector", lambda v: v.reduce_sum(t_st[:, 1:2], t_lnsq[:, :], axis=AX))
            P.op("vector", lambda v: v.tensor_single_scalar(t_st[:, 2:3], t_st[:, 0:1], 1.0 / DM, ALU.mult))
            P.op("vector", lambda v: v.tensor_single_scalar(t_st[:, 3:4], t_st[:, 1:2], 1.0 / DM, ALU.mult))
            P.op("vector", lambda v: v.tensor_tensor(t_st[:, 4:5], t_st[:, 2:3], t_st[:, 2:3], ALU.mult))
            P.op("vector", lambda v: v.tensor_tensor(t_st[:, 5:6], t_st[:, 3:4], t_st[:, 4:5], ALU.subtract))
            P.op("vector", lambda v: v.tensor_single_scalar(t_st[:, 5:6], t_st[:, 5:6], EPS, ALU.add))
            P.wait("scalar", "vector")
            P.op("scalar", lambda s: s.activation(t_st[:, 6:7], t_st[:, 5:6], AF.Sqrt))
            P.wait("vector", "scalar")
            P.op("vector", lambda v: v.reciprocal(t_st[:, 7:8], t_st[:, 6:7]))
            P.op("vector", lambda v: v.tensor_tensor(t_st[:, 6:7], t_st[:, 2:3], t_st[:, 7:8], ALU.mult))
            P.op("vector", lambda v: v.tensor_single_scalar(t_st[:, 6:7], t_st[:, 6:7], -1.0, ALU.mult))
            P.op("vector", lambda v: v.tensor_scalar(
                dst, src, t_st[:, 7:8], t_st[:, 6:7], ALU.mult, ALU.add))

        for k in range(nkc):
            for tt in range(NTT):
                row0 = k * TC + tt * 128
                P.wait("sync", "vector")
                P.wait("sync", "scalar")
                P.dma("sync", "xtm", t_ln[:, :], xin[row0:row0 + 128, :])
                P.wait("vector", "xtm")
                P.wait("scalar", "xtm")
                ln_freedim(t_ln[:, :], t_lnsq[:, :])
                P.wait("vector", "sltm")
                P.op("vector", lambda v: v.tensor_copy(t_lno[:, :], t_lnsq[:, :]))
                P.wait("gpsimd", "vector")
                P.dma("gpsimd", "sltm", ln_tm.ap()[tt * 128:(tt + 1) * 128, :], t_lno[:, :])
            for ft in range(NFT):
                P.wait("sync", "tensor")
                P.wait("sync", "sltm")
                P.dma_tr("sync", "l1t", ln1t[:, ft * TC:(ft + 1) * TC],
                         ln_tm.ap()[:, ft * 128:(ft + 1) * 128])
            ln1t_ready = P._c("l1t")

            for dt in range(NDT):
                bi = dt % 2
                P.wait("sync", "tensor")
                P.dma("sync", "wbf%d" % bi, w_bf[bi][:, :],
                      wxbf.ap()[:, dt * 2048:(dt + 1) * 2048])
                P.wait("tensor", "wbf%d" % bi)
                P.wait("tensor", "l1t", ln1t_ready)
                P.wait("tensor", "vector")
                for kt in range(NFT):
                    st_, sp_ = kt == 0, kt == NFT - 1
                    if sp_:
                        P.op("tensor", (lambda t, kt=kt, bi=bi, st_=st_: mm_pair(
                            t, ps_mm, w_bf[bi][:, kt * 128:(kt + 1) * 128],
                            ln1t[:, kt * TC:(kt + 1) * TC], st_, True)), selfwait=False)
                    else:
                        P.silent("tensor", (lambda t, kt=kt, bi=bi, st_=st_: mm_pair(
                            t, ps_mm, w_bf[bi][:, kt * 128:(kt + 1) * 128],
                            ln1t[:, kt * TC:(kt + 1) * TC], st_, False)))
                P.wait("vector", "tensor")
                P.op("vector", lambda v, dt=dt: v.tensor_copy(
                    t_xr[:, 0:DC - 1], xpre[:, dt * (DC - 1):(dt + 1) * (DC - 1)]))
                P.op("vector", lambda v: v.tensor_copy(t_xr[:, DC - 1:DC - 1 + TC], ps_mm[:, :]))
                P.op("vector", lambda v, dt=dt: v.tensor_copy(
                    xpre[:, dt * (DC - 1):(dt + 1) * (DC - 1)], t_xr[:, TC:TC + DC - 1]))
                P.op("vector", lambda v, dt=dt: v.tensor_scalar(
                    t_cf[:, :], t_xr[:, 0:TC], p_convw[:, dt * DC:dt * DC + 1], None, ALU.mult))
                for j in range(1, DC):
                    P.op("vector", lambda v, j=j, dt=dt: v.scalar_tensor_tensor(
                        t_cf[:, :], t_xr[:, j:j + TC], p_convw[:, dt * DC + j:dt * DC + j + 1],
                        t_cf[:, :], ALU.mult, ALU.add))
                P.wait("scalar", "vector")
                P.wait("scalar", "sxld")
                P.wait("scalar", "tensor")
                P.op("scalar", lambda s, dt=dt: s.activation(
                    t_xld[:, :], t_cf[:, :], AF.Silu, bias=p_convb[:, dt:dt + 1]))
                P.wait("gpsimd", "scalar")
                P.dma("gpsimd", "sxld", xld_sc.ap()[dt * 128:(dt + 1) * 128, :], t_xld[:, :])
                P.wait("tensor", "scalar")
                st_, sp_ = dt == 0, dt == NDT - 1
                P.silent("tensor", (lambda t, dt=dt, st_=st_, sp_=sp_: mm_pair(
                    t, ps_d, xp_dt_l[:, dt * DTR:(dt + 1) * DTR], t_xld, st_, sp_)))
                P.silent("tensor", (lambda t, dt=dt, st_=st_, sp_=sp_: mm_pair(
                    t, ps_bc, xp_b_l[:, dt * DS:(dt + 1) * DS], t_xld, st_, sp_, pl=DS)))
                P.op("tensor", (lambda t, dt=dt, st_=st_, sp_=sp_: mm_pair(
                    t, ps_c, xp_c_l[:, dt * DS:(dt + 1) * DS], t_xld, st_, sp_,
                    pl=DS)), selfwait=False)
            P.wait("scalar", "tensor")
            P.op("scalar", lambda s: s.copy(dtlo_bf[:, :], ps_d[:, :]))
            P.op("scalar", lambda s: s.copy(bmat[:, :], ps_bc[0:DS, :]))
            P.op("scalar", lambda s: s.copy(cmat[:, :], ps_c[0:DS, :]))

            def z_builder(g, k=k):
                for dt in range(NDT):
                    bi = dt % 2
                    g.wait("sync", "vector")
                    g.dma("sync", "wf32", w_f32[:, :], wz_blk[:, dt * 2048:(dt + 1) * 2048])
                    g.wait("vector", "wf32")
                    g.wait("vector", "tensor")
                    g.op("vector", lambda v, bi=bi: v.tensor_copy(w_bf[bi][:, :], w_f32[:, :]))
                    g.wait("tensor", "vector")
                    g.wait("tensor", "scalar")
                    for kt in range(NFT):
                        if kt == NFT - 1:
                            g.op("tensor", (lambda t, kt=kt, bi=bi: mm_pair(
                                t, ps_mm, w_bf[bi][:, kt * 128:(kt + 1) * 128],
                                ln1t[:, kt * TC:(kt + 1) * TC], kt == 0, True)), selfwait=False)
                        else:
                            g.silent("tensor", (lambda t, kt=kt, bi=bi: mm_pair(
                                t, ps_mm, w_bf[bi][:, kt * 128:(kt + 1) * 128],
                                ln1t[:, kt * TC:(kt + 1) * TC], kt == 0, False)))
                    g.wait("scalar", "tensor")
                    g.wait("scalar", "ssz")
                    g.op("scalar", lambda s: s.activation(t_sz[:, :], ps_mm[:, :], AF.Silu))
                    g.wait("gpsimd", "scalar")
                    g.dma("gpsimd", "ssz", sz_sc.ap()[dt * 128:(dt + 1) * 128, :], t_sz[:, :])
            P.guarded(k, z_builder)

            P.wait("tensor", "scalar")
            for n in range(DS):
                P.op("tensor", (lambda t, n=n: mm_pair(
                    t, ps_bc, p_sel[:, n * 128:(n + 1) * 128], bmat, True, True)), selfwait=False)
                P.wait("scalar", "tensor")
                P.op("scalar", lambda s, n=n: s.copy(b_bc[:, n * TC:(n + 1) * TC], ps_bc[:, :]))
                P.wait("tensor", "scalar")
                P.op("tensor", (lambda t, n=n: mm_pair(
                    t, ps_bc, p_sel[:, n * 128:(n + 1) * 128], cmat, True, True)), selfwait=False)
                P.wait("scalar", "tensor")
                P.op("scalar", lambda s, n=n: s.copy(c_bc[:, n * TC:(n + 1) * TC], ps_bc[:, :]))
                P.wait("tensor", "scalar")

            for dt in range(NDT):
                P.wait("tensor", "scalar")
                P.wait("tensor", "vector")
                P.op("tensor", (lambda t, dt=dt: mm_pair(
                    t, ps_d, dt_lhs[:, dt * 128:(dt + 1) * 128], dtlo_bf, True, True)), selfwait=False)
                P.wait("scalar", "tensor")
                P.wait("scalar", "vector")
                P.op("scalar", lambda s, dt=dt: s.activation(
                    t_sp[:, :], ps_d[:, :], AF.Abs, bias=p_dtb[:, dt:dt + 1]))
                P.op("scalar", lambda s: s.activation(t_sp[:, :], t_sp[:, :], AF.Exp, scale=-1.0))
                P.op("scalar", lambda s: s.activation(t_sp[:, :], t_sp[:, :], AF.Ln, bias=1.0))
                P.wait("vector", "scalar")
                P.op("vector", lambda v, dt=dt: v.tensor_scalar(
                    t_cf[:, :], ps_d[:, :], p_dtb[:, dt:dt + 1], 0.0, ALU.add, ALU.max))
                P.op("vector", lambda v: v.tensor_tensor(t_dl[:, :], t_cf[:, :], t_sp[:, :], ALU.add))
                P.wait("sync", "vector")
                P.wait("sync", "tensor")
                P.wait("sync", "sxld")
                P.dma("sync", "lxld", t_xld[:, :], xld_sc.ap()[dt * 128:(dt + 1) * 128, :])
                P.wait("vector", "lxld")
                P.op("vector", lambda v: v.tensor_tensor(t_u[:, :], t_dl[:, :], t_xld[:, :], ALU.mult))
                for n in range(DS):
                    col = dt * DS + n
                    P.wait("scalar", "vector")
                    P.op("scalar", lambda s, col=col: s.activation(
                        t_a[:, :], t_dl[:, :], AF.Exp, scale=p_negA[:, col:col + 1]))
                    P.wait("vector", "scalar")
                    P.op("vector", lambda v, n=n: v.tensor_tensor(
                        t_b[:, :], t_u[:, :], b_bc[:, n * TC:(n + 1) * TC], ALU.mult))
                    P.op("vector", lambda v, col=col: v.tensor_tensor_scan(
                        t_h[:, :], t_a[:, :], t_b[:, :], carry[:, col:col + 1],
                        ALU.mult, ALU.add))
                    P.op("vector", lambda v, col=col: v.tensor_copy(
                        carry[:, col:col + 1], t_h[:, TC - 1:TC]))

                    def own_scan(g, n=n):
                        g.op("vector", lambda v, n=n: v.tensor_tensor(
                            t_ch[:, :], t_h[:, :], c_bc[:, n * TC:(n + 1) * TC], ALU.mult))
                        if n == 0:
                            g.op("vector", lambda v: v.tensor_copy(t_y[:, :], t_ch[:, :]))
                        else:
                            g.op("vector", lambda v: v.tensor_tensor(
                                t_y[:, :], t_y[:, :], t_ch[:, :], ALU.add))
                    P.guarded(k, own_scan)

                def own_gate(g, dt=dt):
                    g.wait("gpsimd", "vector")
                    g.wait("gpsimd", "ssz")
                    g.dma("gpsimd", "lsz", t_sz[:, :], sz_sc.ap()[dt * 128:(dt + 1) * 128, :])
                    g.op("vector", lambda v, dt=dt: v.scalar_tensor_tensor(
                        t_y[:, :], t_xld[:, :], p_dvec[:, dt:dt + 1], t_y[:, :],
                        ALU.mult, ALU.add))
                    g.wait("vector", "lsz")
                    g.wait("vector", "syg")
                    g.op("vector", lambda v: v.tensor_tensor(t_yg[:, :], t_y[:, :], t_sz[:, :], ALU.mult))
                    g.wait("gpsimd", "vector")
                    g.dma("gpsimd", "syg", ygate_sc.ap()[dt * 128:(dt + 1) * 128, :], t_yg[:, :])
                P.guarded(k, own_gate)

        syg_total = P._c("syg")
        ssz_total = P._c("ssz")

        for mtp in range(NFT // 2):
            for half, psum in ((0, ps_mm), (1, ps_d)):
                mt = mtp * 2 + half
                P.wait("tensor", "scalar")
                for dt in range(NDT):
                    bi = dt % 2
                    P.wait("sync", "vector")
                    P.dma("sync", "wf32", w_f32[:, 0:128],
                          wot_blk[:, dt * 2048 + mt * 128:dt * 2048 + (mt + 1) * 128])
                    P.wait("sync", "tensor")
                    P.wait("sync", "syg", syg_total)
                    P.dma("sync", "yt%d" % bi, t_yt[bi][:, :],
                          ygate_sc.ap()[dt * 128:(dt + 1) * 128, :])
                    P.wait("vector", "wf32")
                    P.op("vector", lambda v, bi=bi: v.tensor_copy(w_bf[bi][:, 0:128], w_f32[:, 0:128]))
                    P.wait("tensor", "vector")
                    P.wait("tensor", "yt%d" % bi)
                    P.op("tensor", (lambda t, bi=bi, psum=psum, dt=dt: mm_pair(
                        t, psum, w_bf[bi][:, 0:128], t_yt[bi], dt == 0, dt == NDT - 1)),
                        selfwait=False)
                P.wait("scalar", "tensor")
                P.wait("scalar", "sfm")
                P.op("scalar", lambda s, psum=psum: s.copy(t_ev[:, :], psum[:, :]))
                P.wait("gpsimd", "scalar")
                P.dma("gpsimd", "sfm", fm_sc.ap()[mt * 128:(mt + 1) * 128, :], t_ev[:, :])
        P.wait("sync", "ssz", ssz_total)

        for tt in range(NTT):
            P.wait("sync", "sfm")
            P.wait("sync", "scalar")
            P.wait("sync", "vector")
            P.wait("sync", "sltm")
            P.wait("sync", "sx2")
            P.dma_tr("sync", "tlo", t_lno[:, :], fm_sc.ap()[:, tt * 128:(tt + 1) * 128])
            P.dma("sync", "xtm", t_ln[:, :], xmy[tt * 128:(tt + 1) * 128, :])
            P.wait("vector", "tlo")
            P.wait("vector", "xtm")
            P.wait("scalar", "tlo")
            ln_freedim(t_lno[:, :], t_lnsq[:, :])
            P.wait("vector", "sx2")
            P.op("vector", lambda v: v.tensor_tensor(t_ln[:, :], t_ln[:, :], t_lnsq[:, :], ALU.add))
            P.wait("gpsimd", "vector")
            P.dma("gpsimd", "sx2", x2_sc.ap()[tt * 128:(tt + 1) * 128, :], t_ln[:, :])
            P.wait("scalar", "vector")
            ln_freedim(t_ln[:, :], t_lnsq[:, :])
            P.wait("vector", "sltm")
            P.op("vector", lambda v: v.tensor_copy(t_lno[:, :], t_lnsq[:, :]))
            P.wait("gpsimd", "vector")
            P.dma("gpsimd", "sltm", ln_tm.ap()[tt * 128:(tt + 1) * 128, :], t_lno[:, :])
        for ft in range(NFT):
            P.wait("sync", "sltm")
            P.wait("sync", "tensor")
            P.dma_tr("sync", "l1t", ln2t[:, ft * TC:(ft + 1) * TC],
                     ln_tm.ap()[:, ft * 128:(ft + 1) * 128])
        ln2t_ready = P._c("l1t")

        for mt in range(NIT):
            bi = mt % 2
            P.wait("sync", "tensor")
            P.dma("sync", "wbf%d" % bi, w_bf[bi][:, :], wgbf.ap()[:, mt * 2048:(mt + 1) * 2048])
            P.wait("sync", "scalar")
            P.wait("sync", "vector")
            P.dma("sync", "tlo", t_lno[:, :], wubf.ap()[:, mt * 2048:(mt + 1) * 2048])
            P.wait("tensor", "wbf%d" % bi)
            P.wait("tensor", "tlo")
            P.wait("tensor", "l1t", ln2t_ready)
            P.wait("tensor", "vector")
            P.wait("tensor", "scalar")
            for kt in range(NFT):
                if kt == NFT - 1:
                    P.op("tensor", (lambda t, kt=kt, bi=bi: mm_pair(
                        t, ps_mm, w_bf[bi][:, kt * 128:(kt + 1) * 128],
                        ln2t[:, kt * TC:(kt + 1) * TC], kt == 0, True)), selfwait=False)
                    P.op("tensor", (lambda t, kt=kt: mm_pair(
                        t, ps_d, t_lno[:, kt * 128:(kt + 1) * 128],
                        ln2t[:, kt * TC:(kt + 1) * TC], kt == 0, True)), selfwait=False)
                else:
                    P.silent("tensor", (lambda t, kt=kt, bi=bi: mm_pair(
                        t, ps_mm, w_bf[bi][:, kt * 128:(kt + 1) * 128],
                        ln2t[:, kt * TC:(kt + 1) * TC], kt == 0, False)))
                    P.silent("tensor", (lambda t, kt=kt: mm_pair(
                        t, ps_d, t_lno[:, kt * 128:(kt + 1) * 128],
                        ln2t[:, kt * TC:(kt + 1) * TC], kt == 0, False)))
            P.wait("scalar", "tensor")
            P.wait("scalar", "vector")
            P.op("scalar", lambda s: s.activation(t_ev[:, :], ps_mm[:, :], AF.Silu))
            P.wait("vector", "scalar")
            P.wait("vector", "sh")
            P.op("vector", lambda v: v.tensor_tensor(t_yg[:, :], t_ev[:, :], ps_d[:, :], ALU.mult))
            P.wait("gpsimd", "vector")
            P.dma("gpsimd", "sh", h_sc.ap()[mt * 128:(mt + 1) * 128, :], t_yg[:, :])
        sh_total = P._c("sh")

        for m2p in range(NFT // 2):
            for half, psum in ((0, ps_mm), (1, ps_d)):
                mt2 = m2p * 2 + half
                P.wait("tensor", "scalar")
                P.wait("sync", "tensor")
                P.dma("sync", "dnl", dn_l, wdbf.ap()[:, mt2 * (NIT * 128):(mt2 + 1) * (NIT * 128)])
                for it in range(NIT):
                    bi = it % 2
                    P.wait("sync", "tensor")
                    P.wait("sync", "sh", sh_total)
                    P.dma("sync", "yt%d" % bi, t_yt[bi][:, :], h_sc.ap()[it * 128:(it + 1) * 128, :])
                    P.wait("tensor", "yt%d" % bi)
                    P.wait("tensor", "dnl")
                    P.op("tensor", (lambda t, it=it, bi=bi, psum=psum: mm_pair(
                        t, psum, dn_l[:, it * 128:(it + 1) * 128], t_yt[bi], it == 0, it == NIT - 1)),
                        selfwait=False)
                P.wait("scalar", "tensor")
                P.wait("scalar", "sfm")
                P.op("scalar", lambda s, psum=psum: s.copy(t_ev[:, :], psum[:, :]))
                P.wait("gpsimd", "scalar")
                P.dma("gpsimd", "sfm", fm_sc.ap()[mt2 * 128:(mt2 + 1) * 128, :], t_ev[:, :])
        sfm_total = P._c("sfm")

        for tt in range(NTT):
            P.wait("sync", "sfm", sfm_total)
            P.wait("sync", "scalar")
            P.wait("sync", "vector")
            P.wait("sync", "sout")
            P.wait("sync", "sltm")
            P.wait("sync", "sx2")
            P.dma_tr("sync", "tlo", t_lno[:, :], fm_sc.ap()[:, tt * 128:(tt + 1) * 128])
            P.dma("sync", "xtm", t_ln[:, :], x2_sc.ap()[tt * 128:(tt + 1) * 128, :])
            P.wait("vector", "tlo")
            P.wait("vector", "xtm")
            P.wait("scalar", "tlo")
            ln_freedim(t_lno[:, :], t_lnsq[:, :])
            P.op("vector", lambda v: v.tensor_tensor(t_ln[:, :], t_ln[:, :], t_lnsq[:, :], ALU.add))
            P.wait("sync", "vector")
            P.dma("sync", "sout", yout[tt * 128:(tt + 1) * 128, :], t_ln[:, :])
        P.wait("sync", "sout")

        for name in list(P.cnt):
            P.sems[name] = e(nc.semaphore("S_" + name))

        block = e(nc.Block())

        @block.sync
        def _(sync):
            for fn in P.q["sync"]:
                fn(sync)

        @block.tensor
        def _(tensor):
            for fn in P.q["tensor"]:
                fn(tensor)

        @block.vector
        def _(vector):
            for fn in P.q["vector"]:
                fn(vector)

        @block.scalar
        def _(scalar):
            for fn in P.q["scalar"]:
                fn(scalar)

        @block.gpsimd
        def _(gpsimd):
            for fn in P.q["gpsimd"]:
                fn(gpsimd)

    return nc




HBM_BASE = [0x0, 0x4000000000, 0x800000000000, 0x804000000000]
DOM_OF_LPAIR = {0: 2, 1: 3, 2: 1, 3: 0}

GC_XW0, GC_XW1 = 0, 1
GC_XRA = [2, 3, 4, 5]
GC_XRB = [6, 7, 8, 9]
GC_FWL, GC_FWR = 10, 11
GC_OW = 12
GC_OR = [13, 14, 15, 16]
GC_L2W = 17
GC_L2R = [18, 19]


def build_program_v1(ndie=4, nkc=8, rd=True, stages="pzxlsofn2", dbg=False):
    ndtc = NDT // ndie
    nitp = (NIT // ndie) + 1
    spill = ndie < 4
    nc = bass.Bass("TRN2", target_bir_lowering=False, detect_race_conditions=rd)

    def din(name, shape, dt=F32):
        return nc.dram_tensor(name, shape, dt, kind="ExternalInput").ap()

    nown = max(1, nkc // ndie)
    xin = din("xin", [L, DM])
    xmy = din("xmy", [nown * TC, DM])
    wx_blk = din("wx_blk", [128, ndtc * 2048])
    wz_blk = din("wz_blk", [128, ndtc * 2048])
    wot_blk = din("wot_blk", [128, ndtc * 2048])
    wxp_dt = din("wxp_dt", [128, ndtc * DTR])
    wxp_b = din("wxp_b", [128, ndtc * DS])
    wxp_c = din("wxp_c", [128, ndtc * DS])
    wdtt = din("wdtt", [DTR, ndtc * 128])
    p_convw_in = din("p_convw_in", [128, ndtc * DC])
    p_convb_in = din("p_convb_in", [128, ndtc])
    p_dtb_in = din("p_dtb_in", [128, ndtc])
    p_alog_in = din("p_alog_in", [128, ndtc * DS])
    p_dvec_in = din("p_dvec_in", [128, ndtc])
    lutv = din("lutv", [128, NLUT])
    g_blk = din("g_blk", [128, nitp * NFT * 128], I32)
    u_blk = din("u_blk", [128, nitp * NFT * 128], I32)
    d_blk = din("d_blk", [128, NFT * nitp * 128], I32)
    hmask_in = din("hmask_in", [128, nitp])
    selmat = din("selmat", [DS, DS * 128], BF16)
    flags_in = din("flags_in", [1, 64], I32)
    dbg_o = (nc.dram_tensor("dbg_o", [4 * 128, TC], F32, kind="ExternalOutput").ap()
             if dbg else None)
    fvals_in = din("fvals_in", [128, 8 * 16], I32)
    gidx_in = din("gidx_in", [128, 32], I32)

    yout = nc.dram_tensor("yout", [nown * TC, DM], F32, kind="ExternalOutput").ap()

    def dscr(name, shape, dt, **kw):
        return nc.dram_tensor(name, shape, dt, kind="Internal", **kw)

    wxbf = dscr("wxbf", [128, ndtc * 2048], BF16)
    xld_sc = dscr("xld_sc", [ndtc * 128, TC], BF16)
    ln_tm = dscr("ln_tm", [TC, DM], BF16)
    sz_sc = dscr("sz_sc", [ndtc * 128, TC], BF16)
    yg_sc = dscr("yg_sc", [ndtc * 128, TC], BF16)
    h_sc = dscr("h_sc", [nitp * 128, TC], BF16)
    wgbf = dscr("wgbf", [128, nitp * NFT * 128], BF16)
    wubf = dscr("wubf", [128, nitp * NFT * 128], BF16)
    wdbf = dscr("wdbf", [128, NFT * nitp * 128], BF16)
    fm_sc = dscr("fm_sc", [DM, TC], BF16)
    x2_sc = dscr("x2_sc", [TC, DM], F32)

    XDBL = dscr("XDBL", [ndie * 8 * 192, TC], F32, addr_space="Shared")
    OPART = dscr("OPART", [ndie * 8 * DM, TC], F32, addr_space="Shared")
    DPART = dscr("DPART", [ndie * 8 * DM, TC], F32, addr_space="Shared")
    LN2X = dscr("LN2X", [8 * DM, 2 * TC], BF16, addr_space="Shared")
    FLG = dscr("FLG", [4 * 4 * 128, 1024], I32, addr_space="Shared")

    with contextlib.ExitStack() as stx:
        e = stx.enter_context

        def sb(name, p, f, dt):
            return e(nc.sbuf_tensor(name, [p, f], dt))

        ARENA_F = 20480
        arena = sb("arena", 128, ARENA_F, F32)
        a_bf = arena.bitcast(BF16)
        a_i32 = arena.bitcast(I32)

        def vbf(off_f32, free):
            return a_bf[0:128, off_f32 * 2:off_f32 * 2 + free]

        def vi32(off_f32, free):
            return a_i32[0:128, off_f32:off_f32 + free]

        ln1t = vbf(0, NFT * TC)
        b_bc = vbf(4096, DS * TC)
        c_bc = vbf(12288, DS * TC)
        cv_idx = [vi32(0, INTER), vi32(5504, INTER)]
        cv_ixb = vbf(11008, INTER)
        cv_acc = vbf(13760, INTER)
        cv_t = vbf(16512, INTER)
        ln2t = vbf(0, NFT * TC)
        dn_l = vbf(4096, nitp * 128)

        p_negA = sb("p_negA", 128, ndtc * DS, F32)
        p_convw = sb("p_convw", 128, ndtc * DC, F32)
        p_convb = sb("p_convb", 128, ndtc, F32)
        p_dtb = sb("p_dtb", 128, ndtc, F32)
        p_dvec = sb("p_dvec", 128, ndtc, F32)
        p_sel = sb("p_sel", DS, DS * 128, BF16)
        p_lut = sb("p_lut", 128, NLUT, F32)
        p_hm = sb("p_hm", 128, nitp, F32)
        p_flags = sb("p_flags", 1, 64, I32)
        p_fval = sb("p_fval", 128, 8 * 16, I32)
        p_gidx = sb("p_gidx", 128, 32, I32)
        t_fw = sb("t_fw", 1, 8 * 16, I32)
        carry = sb("carry", 128, ndtc * DS, F32)
        dt_lhs = sb("dt_lhs", DTR, ndtc * 128, BF16)
        xp_dt_l = sb("xp_dt_l", 128, ndtc * DTR, BF16)
        xp_b_l = sb("xp_b_l", 128, ndtc * DS, BF16)
        xp_c_l = sb("xp_c_l", 128, ndtc * DS, BF16)
        dtlo_bf = sb("dtlo_bf", DTR, TC, BF16)
        bmat = sb("bmat", DS, TC, BF16)
        cmat = sb("cmat", DS, TC, BF16)
        xpre = sb("xpre", 128, ndtc * (DC - 1), F32)
        yg8 = sb("yg8", 128, (1 if spill else ndtc) * TC, BF16)
        h_ffn = sb("h_ffn", 128, (1 if spill else nitp) * TC, BF16)
        t_yt = [sb("t_yt0", 128, TC, BF16), sb("t_yt1", 128, TC, BF16)] if spill else None

        t_ln = sb("t_ln", 128, DM, F32)
        t_lnsq = sb("t_lnsq", 128, DM, F32)
        t_st = sb("t_st", 128, 8, F32)
        t_lno = sb("t_lno", 128, DM, BF16)
        w_f32 = sb("w_f32", 128, DM, F32)
        w_bf = [sb("w_bf_%d" % i, 128, DM, BF16) for i in range(2)]
        t_xr = sb("t_xr", 128, TC + DC - 1, F32)
        t_cf = sb("t_cf", 128, TC, F32)
        t_sp = sb("t_sp", 128, TC, F32)
        t_dl = sb("t_dl", 128, TC, BF16)
        t_u = sb("t_u", 128, TC, BF16)
        t_xld = sb("t_xld", 128, TC, BF16)
        _ta0 = sb("t_a_0", 128, TC, F32)
        t_a = [_ta0, _ta0]
        t_b = sb("t_b", 128, TC, BF16)
        t_h = sb("t_h", 128, TC, F32)
        t_ch = sb("t_ch", 128, TC, BF16)
        t_y = sb("t_y", 128, TC, BF16)
        t_sz = sb("t_sz", 128, TC, BF16)
        t_ev = sb("t_ev", 128, TC, BF16)

        xdl_a = t_sp[:, 0:TC]
        xdl_b = t_xr[0:64, 0:TC]
        t_bc64 = t_sp
        t_sum = t_cf

        ps_mm = e(nc.psum_tensor("ps_mm", [128, TC], F32))
        ps_d = e(nc.psum_tensor("ps_d", [128, TC], F32))
        ps_bc = e(nc.psum_tensor("ps_bc", [128, TC], F32))
        ps_c = e(nc.psum_tensor("ps_c", [DS, TC], F32))

        P = B()
        P.cnt.setdefault("pollq", 0)

        def mm_pair(t, psum, lhsT, rhs, start, stop, pl=128):
            t.matmul(psum[0:pl, 0:512], lhsT, rhs[:, 0:512], start=start, stop=stop)
            return t.matmul(psum[0:pl, 512:1024], lhsT, rhs[:, 512:1024],
                            start=start, stop=stop)

        from concourse.bass import IndirectOffsetOnAxis

        def iscatter(g, sem, dram_t, row0, src_ap, idx_col, nparts, w=None):
            g.cnt[sem] = g._c(sem) + 16
            g.owner[sem] = "gpsimd"
            assert g.scls.setdefault(sem, g.ctx) == g.ctx, sem
            sems = g.sems
            ncols = dram_t.shape[1]
            out_ap = dram_t.ap()
            eoff = row0 * ncols

            def run(eng, sem=sem, out_ap=out_ap, src_ap=src_ap, idx_col=idx_col,
                    nparts=nparts, eoff=eoff):
                eng.indirect_dma_start(
                    out_ap,
                    IndirectOffsetOnAxis(ap=p_gidx[0:nparts, idx_col:idx_col + 1], axis=0),
                    src_ap, None,
                    element_offset=eoff,
                    bounds_check=None, oob_is_err=False,
                ).then_inc(sems[sem], 16)
            g.q["gpsimd"].append(run)
            return g._c(sem)

        def igather(g, sem, dst_ap, dram_t, row0, idx_col, nparts, accum=False):
            g.cnt[sem] = g._c(sem) + 16
            g.owner[sem] = "gpsimd"
            assert g.scls.setdefault(sem, g.ctx) == g.ctx, sem
            sems = g.sems
            ncols = dram_t.shape[1]
            in_ap = dram_t.ap()
            eoff = row0 * ncols

            def run(eng, sem=sem, in_ap=in_ap, dst_ap=dst_ap, idx_col=idx_col,
                    nparts=nparts, accum=accum, eoff=eoff):
                eng.indirect_dma_start(
                    dst_ap, None,
                    in_ap,
                    IndirectOffsetOnAxis(ap=p_gidx[0:nparts, idx_col:idx_col + 1], axis=0),
                    element_offset=eoff,
                    bounds_check=None, oob_is_err=False,
                    compute_op=(ALU.add if accum else ALU.bypass),
                ).then_inc(sems[sem], 16)
            g.q["gpsimd"].append(run)
            return g._c(sem)

        P.dma("sync", "pld", p_sel[:, :], selmat)
        P.dma("sync", "pld", p_lut[:, :], lutv)
        P.dma("sync", "pld", p_hm[:, :], hmask_in)
        P.dma("sync", "pld", p_flags[:, :], flags_in)
        P.dma("sync", "pld", p_fval[:, :], fvals_in)
        P.dma("sync", "pld", p_gidx[:, :], gidx_in)
        P.dma("sync", "pld", p_convw[:, :], p_convw_in)
        P.dma("sync", "pld", p_convb[:, :], p_convb_in)
        P.dma("sync", "pld", p_dtb[:, :], p_dtb_in)
        P.dma("sync", "pld", p_dvec[:, :], p_dvec_in)
        P.dma("sync", "pld", p_negA[:, :], p_alog_in)
        for g in ENGS:
            P.wait(g, "pld")

        P.op("scalar", lambda s: s.activation(p_negA[:, :], p_negA[:, :], AF.Exp))
        P.wait("vector", "scalar")
        P.op("vector", lambda v: v.tensor_single_scalar(p_negA[:, :], p_negA[:, :], -1.0, ALU.mult))
        P.op("vector", lambda v: v.memset(carry[:, :], 0.0))
        P.op("vector", lambda v: v.memset(xpre[:, :], 0.0))
        P.op("vector", lambda v: v.memset(t_sp[:, :], 0.0))

        def cast_load(dst_t, src_t, width):
            for o in range(0, width, DM):
                w = min(DM, width - o)
                P.wait("sync", "vector")
                P.dma("sync", "wf32", w_f32[:, 0:w], src_t[:, o:o + w])
                P.wait("vector", "wf32")
                P.op("vector", lambda v, o=o, w=w: v.tensor_copy(
                    dst_t[:, o:o + w], w_f32[:, 0:w]))

        cast_load(dt_lhs, wdtt, ndtc * 128)
        cast_load(xp_dt_l, wxp_dt, ndtc * DTR)
        cast_load(xp_b_l, wxp_b, ndtc * DS)
        cast_load(xp_c_l, wxp_c, ndtc * DS)

        for i in range(ndtc):
            bi = i % 2
            P.wait("sync", "vector")
            P.dma("sync", "wf32", w_f32[:, :], wx_blk[:, i * 2048:(i + 1) * 2048])
            P.wait("vector", "wf32")
            P.wait("vector", "swx%d" % bi)
            P.op("vector", lambda v, bi=bi: v.tensor_copy(w_bf[bi][:, :], w_f32[:, :]))
            P.wait("gpsimd", "vector")
            P.dma("gpsimd", "swx%d" % bi, wxbf.ap()[:, i * 2048:(i + 1) * 2048], w_bf[bi][:, :])

        def convert(src, dst, width_total):
            nch = (width_total + INTER - 1) // INTER
            for i in range(nch):
                off = i * INTER
                w = min(INTER, width_total - off)
                bi = i % 2
                P.wait("sync", "vector")
                P.dma("sync", "cvi%d" % bi, cv_idx[bi][:, 0:w], src[:, off:off + w])
                P.wait("vector", "cvi%d" % bi)
                P.wait("vector", "scv")
                P.op("vector", lambda v, bi=bi, w=w: v.tensor_copy(cv_ixb[:, 0:w], cv_idx[bi][:, 0:w]))
                P.op("vector", lambda v, w=w: v.tensor_scalar(
                    cv_acc[:, 0:w], cv_ixb[:, 0:w], 0.0, p_lut[:, 0:1], ALU.is_equal, ALU.mult))
                for vv in range(1, NLUT):
                    P.op("vector", lambda v, vv=vv, w=w: v.tensor_scalar(
                        cv_t[:, 0:w], cv_ixb[:, 0:w], float(vv), p_lut[:, vv:vv + 1],
                        ALU.is_equal, ALU.mult))
                    P.op("vector", lambda v, w=w: v.tensor_tensor(
                        cv_acc[:, 0:w], cv_acc[:, 0:w], cv_t[:, 0:w], ALU.add))
                P.wait("gpsimd", "vector")
                P.dma("gpsimd", "scv", dst.ap()[:, off:off + w], cv_acc[:, 0:w])

        convert(g_blk, wgbf, nitp * NFT * 128)
        convert(u_blk, wubf, nitp * NFT * 128)
        convert(d_blk, wdbf, NFT * nitp * 128)
        P.wait("vector", "scv")
        P.wait("sync", "swx0")
        P.wait("sync", "swx1")
        P.wait("sync", "scv")

        pregs = {}

        def init_pregs(eng):
            pregs["scnt"] = eng.register("pscnt").__enter__()
            pregs["rv"] = [eng.register("prv%d" % j).__enter__() for j in range(4)]
            pregs["ok"] = eng.register("pok").__enter__()
            pregs["t1"] = eng.register("pt1").__enter__()
            pregs["no"] = eng.register("pno").__enter__()
            pregs["tgt"] = eng.register("ptgt").__enter__()
            eng.reg_mov(pregs["scnt"], 0)
        P.silent("sync", init_pregs)

        fregs = {}

        def fguard(col, builder, cls=None):
            for g in ENGS:
                def lrun(eng, g=g, col=col):
                    if g not in fregs:
                        fregs[g] = eng.register("freg_" + g).__enter__()
                    eng.reg_load(fregs[g], p_flags[0:1, col:col + 1])
                P.silent(g, lrun)
            gb = B.__new__(B)
            gb.q = {g: [] for g in ENGS}
            gb.cnt = dict(P.cnt)
            gb.owner = P.owner
            gb.sems = P.sems
            gb.kreg = P.kreg
            gb.ctx = cls
            gb.scls = P.scls
            builder(gb)
            for g in ENGS:
                span = gb.q[g]
                deltas = [(name, P._c(name), gb._c(name) - P._c(name))
                          for name in gb.cnt
                          if gb.owner.get(name) == g and gb._c(name) > P._c(name)
                          and name not in gb.scls]
                if not span and not deltas:
                    continue
                sems = P.sems

                def run(eng, g=g, span=span, deltas=deltas):
                    with eng.If_eq(fregs[g], 1):
                        for fn in span:
                            fn(eng)
                    if deltas:
                        with eng.Else():
                            for name, pre, d in deltas:
                                eng.wait_ge(sems[name], pre)
                                eng.sem_inc(sems[name], d)
                P.q[g].append(run)
            P.cnt = gb.cnt

        def poll(g_, stream, k, prods, lw=None):
            nprod = len(prods)
            row0 = prods[0]
            lw_c = g_._c(lw) if lw else 0

            def run(eng, stream=stream, k=k, nprod=nprod, row0=row0,
                    lw=lw, lw_c=lw_c):
                scnt, rv = pregs["scnt"], pregs["rv"]
                ok, t1, no = pregs["ok"], pregs["t1"], pregs["no"]
                tgt = pregs["tgt"]
                if lw:
                    eng.wait_ge(P.sems[lw], lw_c)
                eng.reg_load(tgt, p_fval[0:1, k * 16:k * 16 + 1])
                eng.reg_mov(no, 1)
                with eng.While(no):
                    eng.reg_add(scnt, scnt, 16)
                    eng.dma_start(
                        out=t_fw[0:1, 0:nprod * 16],
                        in_=bass.AP(FLG, (stream * 512 + k * 64 + row0 * 16) * 1024,
                                    [[1, 1], [16 * 1024, nprod], [1, 16]]),
                        allow_slow_non_contiguous=True,
                    ).then_inc(P.sems["pollq"], 16)
                    eng.wait_ge(P.sems["pollq"], scnt)
                    for j in range(nprod):
                        eng.reg_load(rv[j], t_fw[0:1, j * 16:j * 16 + 1])
                    eng.reg_alu(ok, rv[0], tgt, ALU.is_equal)
                    for j in range(1, nprod):
                        eng.reg_alu(t1, rv[j], tgt, ALU.is_equal)
                        eng.reg_alu(ok, ok, t1, ALU.bitwise_and)
                    eng.reg_alu(no, ok, 1, ALU.bitwise_xor)
            g_.silent("sync", run)
            g_.op("sync", lambda eng: eng.nop(), selfwait=False)
            g_.wait("gpsimd", "sync")

        def flag_push(g_, stream, k, sem):
            src = p_fval[0:16, k * 16:(k + 1) * 16]
            cols = (GC_FWL, GC_FWR) if ndie == 4 else (GC_FWL,)
            for col in cols:
                iscatter(g_, sem, FLG, stream * 512 + k * 64, src, col, 16, w=16)

        def ln_freedim(g_, src, dst):
            g_.op("vector", lambda v: v.reduce_sum(t_st[:, 0:1], src, axis=AX))
            g_.op("scalar", lambda s: s.activation(t_lnsq[:, :], src, AF.Square))
            g_.wait("vector", "scalar")
            g_.op("vector", lambda v: v.reduce_sum(t_st[:, 1:2], t_lnsq[:, :], axis=AX))
            g_.op("vector", lambda v: v.tensor_single_scalar(t_st[:, 2:3], t_st[:, 0:1], 1.0 / DM, ALU.mult))
            g_.op("vector", lambda v: v.tensor_single_scalar(t_st[:, 3:4], t_st[:, 1:2], 1.0 / DM, ALU.mult))
            g_.op("vector", lambda v: v.tensor_tensor(t_st[:, 4:5], t_st[:, 2:3], t_st[:, 2:3], ALU.mult))
            g_.op("vector", lambda v: v.tensor_tensor(t_st[:, 5:6], t_st[:, 3:4], t_st[:, 4:5], ALU.subtract))
            g_.op("vector", lambda v: v.tensor_single_scalar(t_st[:, 5:6], t_st[:, 5:6], EPS, ALU.add))
            g_.wait("scalar", "vector")
            g_.op("scalar", lambda s: s.activation(t_st[:, 6:7], t_st[:, 5:6], AF.Sqrt))
            g_.wait("vector", "scalar")
            g_.op("vector", lambda v: v.reciprocal(t_st[:, 7:8], t_st[:, 6:7]))
            g_.op("vector", lambda v: v.tensor_tensor(t_st[:, 6:7], t_st[:, 2:3], t_st[:, 7:8], ALU.mult))
            g_.op("vector", lambda v: v.tensor_single_scalar(t_st[:, 6:7], t_st[:, 6:7], -1.0, ALU.mult))
            g_.op("vector", lambda v: v.tensor_scalar(
                dst, src, t_st[:, 7:8], t_st[:, 6:7], ALU.mult, ALU.add))

        def prescan(g, k):
            ia, iap = "ia%d" % k, ("ia%d" % (k - 1) if k else None)
            for tt in range(NTT):
                row0 = k * TC + tt * 128
                g.wait("sync", "vector")
                g.wait("sync", "scalar")
                g.dma("sync", "xtm", t_ln[:, :], xin[row0:row0 + 128, :])
                g.wait("vector", "xtm")
                g.wait("scalar", "xtm")
                ln_freedim(g, t_ln[:, :], t_lnsq[:, :])
                g.wait("vector", "sltm")
                g.op("vector", lambda v: v.tensor_copy(t_lno[:, :], t_lnsq[:, :]))
                g.wait("sync", "vector")
                g.dma("sync", "sltm", ln_tm.ap()[tt * 128:(tt + 1) * 128, :], t_lno[:, :])
            for ft in range(NFT):
                g.wait("sync", "tensor")
                g.wait("sync", "sltm")
                g.dma_tr("sync", "l1t", ln1t[:, ft * TC:(ft + 1) * TC],
                         ln_tm.ap()[:, ft * 128:(ft + 1) * 128])
            ln1t_ready = g._c("l1t")

            for dt in range(ndtc):
                bi = dt % 2
                g.wait("sync", "tensor")
                g.dma("sync", "wbf%d" % bi, w_bf[bi][:, :],
                      wxbf.ap()[:, dt * 2048:(dt + 1) * 2048])
                g.wait("tensor", "wbf%d" % bi)
                g.wait("tensor", "l1t", ln1t_ready)
                g.wait("tensor", "vector")
                for kt in range(NFT):
                    st_, sp_ = kt == 0, kt == NFT - 1
                    if sp_:
                        g.op("tensor", (lambda t, kt=kt, bi=bi, st_=st_: mm_pair(
                            t, ps_mm, w_bf[bi][:, kt * 128:(kt + 1) * 128],
                            ln1t[:, kt * TC:(kt + 1) * TC], st_, True)), selfwait=False)
                    else:
                        g.silent("tensor", (lambda t, kt=kt, bi=bi, st_=st_: mm_pair(
                            t, ps_mm, w_bf[bi][:, kt * 128:(kt + 1) * 128],
                            ln1t[:, kt * TC:(kt + 1) * TC], st_, False)))
                g.wait("vector", "tensor")
                g.op("vector", lambda v, dt=dt: v.tensor_copy(
                    t_xr[:, 0:DC - 1], xpre[:, dt * (DC - 1):(dt + 1) * (DC - 1)]))
                g.op("vector", lambda v: v.tensor_copy(t_xr[:, DC - 1:DC - 1 + TC], ps_mm[:, :]))
                g.op("vector", lambda v, dt=dt: v.tensor_copy(
                    xpre[:, dt * (DC - 1):(dt + 1) * (DC - 1)], t_xr[:, TC:TC + DC - 1]))
                g.op("vector", lambda v, dt=dt: v.tensor_scalar(
                    t_cf[:, :], t_xr[:, 0:TC], p_convw[:, dt * DC:dt * DC + 1], None, ALU.mult))
                for j in range(1, DC):
                    g.op("vector", lambda v, j=j, dt=dt: v.scalar_tensor_tensor(
                        t_cf[:, :], t_xr[:, j:j + TC], p_convw[:, dt * DC + j:dt * DC + j + 1],
                        t_cf[:, :], ALU.mult, ALU.add))
                g.wait("scalar", "vector")
                g.wait("scalar", "sxld")
                g.wait("scalar", "tensor")
                g.op("scalar", lambda s, dt=dt: s.activation(
                    t_xld[:, :], t_cf[:, :], AF.Silu, bias=p_convb[:, dt:dt + 1]))
                g.wait("sync", "scalar")
                g.dma("sync", "sxld", xld_sc.ap()[dt * 128:(dt + 1) * 128, :], t_xld[:, :])
                g.wait("tensor", "scalar")
                st_, sp_ = dt == 0, dt == ndtc - 1
                g.silent("tensor", (lambda t, dt=dt, st_=st_, sp_=sp_: mm_pair(
                    t, ps_d, xp_dt_l[:, dt * DTR:(dt + 1) * DTR], t_xld, st_, sp_)))
                g.silent("tensor", (lambda t, dt=dt, st_=st_, sp_=sp_: mm_pair(
                    t, ps_bc, xp_b_l[:, dt * DS:(dt + 1) * DS], t_xld, st_, sp_, pl=DS)))
                g.op("tensor", (lambda t, dt=dt, st_=st_, sp_=sp_: mm_pair(
                    t, ps_c, xp_c_l[:, dt * DS:(dt + 1) * DS], t_xld, st_, sp_,
                    pl=DS)), selfwait=False)
            g.wait("scalar", "tensor")
            g.wait("scalar", "vector")
            if iap:
                g.wait("scalar", iap)
            g.op("scalar", lambda s: s.copy(t_cf[:, :], ps_d[:, :]))
            g.op("scalar", lambda s: s.copy(t_bc64[0:DS, :], ps_bc[0:DS, :]))
            g.op("scalar", lambda s: s.copy(t_bc64[32:32 + DS, :], ps_c[0:DS, :]))
            g.wait("gpsimd", "scalar")
            iscatter(g, ia, XDBL, k * 192, t_cf[:, :], GC_XW0, 128)
            iscatter(g, ia, XDBL, k * 192, t_bc64[0:64, 0:TC], GC_XW1, 64)
            g.wait("gpsimd", ia)
            flag_push(g, 0, k, ia)

        def prescan_x(g, k):
            ia = "ia%d" % k
            g.wait("gpsimd", "vector")
            poll(g, 0, k, list(range(ndie)), lw=ia)
            for j in range(ndie):
                g.wait("gpsimd", ia)
                igather(g, ia, xdl_a, XDBL, k * 192, GC_XRA[j], 128, accum=(j > 0))
                g.wait("gpsimd", ia)
                igather(g, ia, xdl_b, XDBL, k * 192, GC_XRB[j], 64, accum=(j > 0))
            g.wait("vector", ia)
            g.wait("vector", "tensor")
            g.op("vector", lambda v: v.tensor_copy(dtlo_bf[:, :], xdl_a))
            g.op("vector", lambda v: v.tensor_copy(bmat[:, :], xdl_b[0:DS, 0:TC]))
            g.op("vector", lambda v: v.tensor_copy(cmat[:, :], xdl_b[32:32 + DS, 0:TC]))
            g.wait("tensor", "vector")
            g.wait("tensor", "scalar")
            for n in range(DS):
                g.op("tensor", (lambda t, n=n: mm_pair(
                    t, ps_bc, p_sel[:, n * 128:(n + 1) * 128], bmat, True, True)), selfwait=False)
                g.wait("scalar", "tensor")
                g.op("scalar", lambda s, n=n: s.copy(b_bc[:, n * TC:(n + 1) * TC], ps_bc[:, :]))
                g.wait("tensor", "scalar")
                g.op("tensor", (lambda t, n=n: mm_pair(
                    t, ps_bc, p_sel[:, n * 128:(n + 1) * 128], cmat, True, True)), selfwait=False)
                g.wait("scalar", "tensor")
                g.op("scalar", lambda s, n=n: s.copy(c_bc[:, n * TC:(n + 1) * TC], ps_bc[:, :]))
                g.wait("tensor", "scalar")

        def zstage(g, k):
            for dt in range(ndtc):
                bi = dt % 2
                g.wait("sync", "vector")
                g.dma("sync", "wf32", w_f32[:, :], wz_blk[:, dt * 2048:(dt + 1) * 2048])
                g.wait("vector", "wf32")
                g.wait("vector", "tensor")
                g.op("vector", lambda v, bi=bi: v.tensor_copy(w_bf[bi][:, :], w_f32[:, :]))
                g.wait("tensor", "vector")
                g.wait("tensor", "scalar")
                for kt in range(NFT):
                    if kt == NFT - 1:
                        g.op("tensor", (lambda t, kt=kt, bi=bi: mm_pair(
                            t, ps_mm, w_bf[bi][:, kt * 128:(kt + 1) * 128],
                            ln1t[:, kt * TC:(kt + 1) * TC], kt == 0, True)), selfwait=False)
                    else:
                        g.silent("tensor", (lambda t, kt=kt, bi=bi: mm_pair(
                            t, ps_mm, w_bf[bi][:, kt * 128:(kt + 1) * 128],
                            ln1t[:, kt * TC:(kt + 1) * TC], kt == 0, False)))
                g.wait("scalar", "tensor")
                g.wait("scalar", "ssz")
                g.op("scalar", lambda s: s.activation(t_sz[:, :], ps_mm[:, :], AF.Silu))
                g.wait("sync", "scalar")
                g.dma("sync", "ssz", sz_sc.ap()[dt * 128:(dt + 1) * 128, :], t_sz[:, :])

        def scanstage(g, k, with_out):
            for dt in range(ndtc):
                g.wait("tensor", "scalar")
                g.wait("tensor", "vector")
                g.op("tensor", (lambda t, dt=dt: mm_pair(
                    t, ps_d, dt_lhs[:, dt * 128:(dt + 1) * 128], dtlo_bf, True, True)), selfwait=False)
                g.wait("scalar", "tensor")
                g.wait("scalar", "vector")
                g.op("scalar", lambda s, dt=dt: s.activation(
                    t_sp[:, :], ps_d[:, :], AF.Abs, bias=p_dtb[:, dt:dt + 1]))
                g.op("scalar", lambda s: s.activation(t_sp[:, :], t_sp[:, :], AF.Exp, scale=-1.0))
                g.op("scalar", lambda s: s.activation(t_sp[:, :], t_sp[:, :], AF.Ln, bias=1.0))
                g.wait("vector", "scalar")
                g.op("vector", lambda v, dt=dt: v.tensor_scalar(
                    t_cf[:, :], ps_d[:, :], p_dtb[:, dt:dt + 1], 0.0, ALU.add, ALU.max))
                g.op("vector", lambda v: v.tensor_tensor(t_dl[:, :], t_cf[:, :], t_sp[:, :], ALU.add))
                g.wait("sync", "vector")
                g.wait("sync", "tensor")
                g.wait("sync", "sxld")
                g.dma("sync", "lxld", t_xld[:, :], xld_sc.ap()[dt * 128:(dt + 1) * 128, :])
                g.wait("vector", "lxld")
                g.op("vector", lambda v: v.tensor_tensor(t_u[:, :], t_dl[:, :], t_xld[:, :], ALU.mult))
                for n in range(DS):
                    col = dt * DS + n
                    nbi = n % 2
                    g.wait("scalar", "vector")
                    g.op("scalar", lambda s, col=col, nbi=nbi: s.activation(
                        t_a[nbi][:, :], t_dl[:, :], AF.Exp, scale=p_negA[:, col:col + 1]))
                    g.wait("vector", "scalar")
                    g.op("vector", lambda v, n=n: v.tensor_tensor(
                        t_b[:, :], t_u[:, :], b_bc[:, n * TC:(n + 1) * TC], ALU.mult))
                    g.op("vector", lambda v, col=col, nbi=nbi: v.tensor_tensor_scan(
                        t_h[:, :], t_a[nbi][:, :], t_b[:, :], carry[:, col:col + 1],
                        ALU.mult, ALU.add))
                    g.op("vector", lambda v, col=col: v.tensor_copy(
                        carry[:, col:col + 1], t_h[:, TC - 1:TC]))
                    if with_out:
                        g.op("vector", lambda v, n=n: v.tensor_tensor(
                            t_ch[:, :], t_h[:, :], c_bc[:, n * TC:(n + 1) * TC], ALU.mult))
                        if n == 0:
                            g.op("vector", lambda v: v.tensor_copy(t_y[:, :], t_ch[:, :]))
                        else:
                            g.op("vector", lambda v: v.tensor_tensor(
                                t_y[:, :], t_y[:, :], t_ch[:, :], ALU.add))
                if with_out:
                    g.wait("sync", "vector")
                    g.wait("sync", "ssz")
                    g.dma("sync", "lsz", t_sz[:, :], sz_sc.ap()[dt * 128:(dt + 1) * 128, :])
                    g.op("vector", lambda v, dt=dt: v.scalar_tensor_tensor(
                        t_y[:, :], t_xld[:, :], p_dvec[:, dt:dt + 1], t_y[:, :],
                        ALU.mult, ALU.add))
                    g.wait("vector", "lsz")
                    if spill:
                        g.wait("vector", "syg")
                        g.op("vector", lambda v: v.tensor_tensor(
                            yg8[:, 0:TC], t_y[:, :], t_sz[:, :], ALU.mult))
                        g.wait("sync", "vector")
                        g.dma("sync", "syg", yg_sc.ap()[dt * 128:(dt + 1) * 128, :], yg8[:, 0:TC])
                    else:
                        g.op("vector", lambda v, dt=dt: v.tensor_tensor(
                            yg8[:, dt * TC:(dt + 1) * TC], t_y[:, :], t_sz[:, :], ALU.mult))

        def outstage(g, k):
            fi, fip = "if%d" % k, ("if%d" % (k - 1) if k % 2 else None)
            if fip:
                g.wait("scalar", fip)
            mmc = {}
            for mt in range(NFT):
                g.wait("tensor", "scalar")
                for dt in range(ndtc):
                    bi = dt % 2
                    g.wait("sync", "vector")
                    g.dma("sync", "wf32", w_f32[:, 0:128],
                          wot_blk[:, dt * 2048 + mt * 128:dt * 2048 + (mt + 1) * 128])
                    g.wait("vector", "wf32")
                    g.wait("vector", "tensor")
                    g.op("vector", lambda v, bi=bi: v.tensor_copy(w_bf[bi][:, 0:128], w_f32[:, 0:128]))
                    yb = dt % 2
                    gi_ = mt * ndtc + dt
                    if spill:
                        if gi_ >= 2:
                            g.wait("sync", "tensor", mmc[gi_ - 2])
                        g.wait("sync", "syg")
                        g.dma("sync", "lyt%d" % yb, t_yt[yb][:, :],
                              yg_sc.ap()[dt * 128:(dt + 1) * 128, :])
                        g.wait("tensor", "lyt%d" % yb)
                    g.wait("tensor", "vector")
                    g.op("tensor", (lambda t, bi=bi, dt=dt, yb=yb: mm_pair(
                        t, ps_mm, w_bf[bi][:, 0:128],
                        (t_yt[yb][:, :] if spill else yg8[:, dt * TC:(dt + 1) * TC]),
                        dt == 0, dt == ndtc - 1)), selfwait=False)
                    mmc[gi_] = g._c("tensor")
                g.wait("scalar", "tensor")
                g.wait("scalar", fi)
                g.op("scalar", lambda s: s.copy(t_sum[:, 0:TC], ps_mm[:, :]))
                g.wait("gpsimd", "scalar")
                iscatter(g, fi, OPART, k * DM + mt * 128, t_sum[:, 0:TC], GC_OW, 128)
            g.wait("gpsimd", fi)
            flag_push(g, 1, k, fi)

        def ownstage(g, k):
            io = "io%d" % k
            poll(g, 1, k, list(range(ndie)), lw="if%d" % k)
            for mt in range(NFT):
                g.wait("gpsimd", "scalar")
                for j in range(ndie):
                    g.wait("gpsimd", io)
                    igather(g, io, t_sum[:, 0:TC], OPART, k * DM + mt * 128,
                            GC_OR[j], 128, accum=(j > 0))
                g.wait("scalar", io)
                g.wait("scalar", "sfm")
                g.op("scalar", lambda s: s.copy(t_ev[:, :], t_sum[:, 0:TC]))
                g.wait("sync", "scalar")
                g.dma("sync", "sfm", fm_sc.ap()[mt * 128:(mt + 1) * 128, :], t_ev[:, :])
            for tt in range(NTT):
                g.wait("sync", "sfm")
                g.wait("sync", "scalar")
                g.wait("sync", "vector")
                g.wait("sync", "sltm")
                g.wait("sync", "sx2")
                g.dma_tr("sync", "tlo", t_lno[:, :], fm_sc.ap()[:, tt * 128:(tt + 1) * 128])
                g.dma("sync", "xtm", t_ln[:, :],
                      xmy[(k // ndie) * TC + tt * 128:(k // ndie) * TC + (tt + 1) * 128, :])
                g.wait("vector", "tlo")
                g.wait("vector", "xtm")
                g.wait("scalar", "tlo")
                ln_freedim(g, t_lno[:, :], t_lnsq[:, :])
                g.op("vector", lambda v: v.tensor_tensor(t_ln[:, :], t_ln[:, :], t_lnsq[:, :], ALU.add))
                g.wait("sync", "vector")
                g.dma("sync", "sx2", x2_sc.ap()[tt * 128:(tt + 1) * 128, :], t_ln[:, :])
                g.wait("scalar", "vector")
                ln_freedim(g, t_ln[:, :], t_lnsq[:, :])
                g.wait("vector", "sltm")
                g.op("vector", lambda v: v.tensor_copy(t_lno[:, :], t_lnsq[:, :]))
                g.wait("sync", "vector")
                g.dma("sync", "sltm", ln_tm.ap()[tt * 128:(tt + 1) * 128, :], t_lno[:, :])
            for ft in range(NFT):
                g.wait("sync", "sltm")
                g.wait("sync", "tensor")
                g.dma_tr("sync", "l1t", ln2t[:, ft * TC:(ft + 1) * TC],
                         ln_tm.ap()[:, ft * 128:(ft + 1) * 128])
            g.wait("gpsimd", "l1t")
            for ft in range(NFT):
                iscatter(g, io, LN2X, k * DM + ft * 128,
                         ln2t[:, ft * TC:(ft + 1) * TC], GC_L2W, 128)
            g.wait("gpsimd", io)
            flag_push(g, 2, k, io)
            g.wait("gpsimd", io)
            g.wait("sync", io)

        def poll2(g, k):
            poll(g, 2, k, [k % ndie])

        def ffnstage(g, k):
            fi = "if%d" % k
            g.wait("gpsimd", "sync")
            opair = (k % ndie) // 2 if ndie == 4 else 0
            for ft in range(NFT):
                g.wait("gpsimd", "tensor")
                igather(g, fi, ln2t[:, ft * TC:(ft + 1) * TC], LN2X,
                        k * DM + ft * 128, GC_L2R[opair], 128)
            g.wait("tensor", fi)
            for mt in range(nitp):
                bi = mt % 2
                g.wait("sync", "tensor")
                g.dma("sync", "wbf%d" % bi, w_bf[bi][:, :], wgbf.ap()[:, mt * 2048:(mt + 1) * 2048])
                g.wait("sync", "scalar")
                g.wait("sync", "vector")
                g.dma("sync", "tlo", t_lno[:, :], wubf.ap()[:, mt * 2048:(mt + 1) * 2048])
                g.wait("tensor", "wbf%d" % bi)
                g.wait("tensor", "tlo")
                g.wait("tensor", "vector")
                g.wait("tensor", "scalar")
                for kt in range(NFT):
                    if kt == NFT - 1:
                        g.op("tensor", (lambda t, kt=kt, bi=bi: mm_pair(
                            t, ps_mm, w_bf[bi][:, kt * 128:(kt + 1) * 128],
                            ln2t[:, kt * TC:(kt + 1) * TC], kt == 0, True)), selfwait=False)
                        g.op("tensor", (lambda t, kt=kt: mm_pair(
                            t, ps_d, t_lno[:, kt * 128:(kt + 1) * 128],
                            ln2t[:, kt * TC:(kt + 1) * TC], kt == 0, True)), selfwait=False)
                    else:
                        g.silent("tensor", (lambda t, kt=kt, bi=bi: mm_pair(
                            t, ps_mm, w_bf[bi][:, kt * 128:(kt + 1) * 128],
                            ln2t[:, kt * TC:(kt + 1) * TC], kt == 0, False)))
                        g.silent("tensor", (lambda t, kt=kt: mm_pair(
                            t, ps_d, t_lno[:, kt * 128:(kt + 1) * 128],
                            ln2t[:, kt * TC:(kt + 1) * TC], kt == 0, False)))
                g.wait("scalar", "tensor")
                g.wait("scalar", "vector")
                g.op("scalar", lambda s: s.activation(t_ev[:, :], ps_mm[:, :], AF.Silu))
                g.wait("vector", "scalar")
                if spill:
                    g.wait("vector", "shh")
                    g.op("vector", lambda v, mt=mt: v.scalar_tensor_tensor(
                        h_ffn[:, 0:TC], t_ev[:, :], p_hm[:, mt:mt + 1],
                        ps_d[:, :], ALU.mult, ALU.mult))
                    g.wait("sync", "vector")
                    g.dma("sync", "shh", h_sc.ap()[mt * 128:(mt + 1) * 128, :], h_ffn[:, 0:TC])
                else:
                    g.op("vector", lambda v, mt=mt: v.scalar_tensor_tensor(
                        h_ffn[:, mt * TC:(mt + 1) * TC], t_ev[:, :], p_hm[:, mt:mt + 1],
                        ps_d[:, :], ALU.mult, ALU.mult))
            hdone = g._c("vector")
            mmc = {}
            for mt2 in range(NFT):
                g.wait("tensor", "scalar")
                g.wait("sync", "tensor")
                g.dma("sync", "dnl", dn_l, wdbf.ap()[:, mt2 * (nitp * 128):(mt2 + 1) * (nitp * 128)])
                g.wait("tensor", "dnl")
                g.wait("tensor", "vector", hdone)
                for it in range(nitp):
                    yb = it % 2
                    gi_ = mt2 * nitp + it
                    if spill:
                        if gi_ >= 2:
                            g.wait("sync", "tensor", mmc[gi_ - 2])
                        g.wait("sync", "shh")
                        g.dma("sync", "lyt%d" % yb, t_yt[yb][:, :],
                              h_sc.ap()[it * 128:(it + 1) * 128, :])
                        g.wait("tensor", "lyt%d" % yb)
                    g.op("tensor", (lambda t, it=it, mt2=mt2, yb=yb: mm_pair(
                        t, ps_mm, dn_l[:, it * 128:(it + 1) * 128],
                        (t_yt[yb][:, :] if spill else h_ffn[:, it * TC:(it + 1) * TC]),
                        it == 0, it == nitp - 1)), selfwait=False)
                    mmc[gi_] = g._c("tensor")
                g.wait("scalar", "tensor")
                g.wait("scalar", fi)
                g.op("scalar", lambda s: s.copy(t_sum[:, 0:TC], ps_mm[:, :]))
                g.wait("gpsimd", "scalar")
                iscatter(g, fi, DPART, k * DM + mt2 * 128, t_sum[:, 0:TC], GC_OW, 128)
            g.wait("gpsimd", fi)
            flag_push(g, 3, k, fi)
            g.wait("gpsimd", fi)
            g.wait("sync", "tensor")

        def own2stage(g, k):
            io = "io%d" % k
            poll(g, 3, k, list(range(ndie)), lw="if%d" % k)
            for mt in range(NFT):
                g.wait("gpsimd", "scalar")
                for j in range(ndie):
                    g.wait("gpsimd", io)
                    igather(g, io, t_sum[:, 0:TC], DPART, k * DM + mt * 128,
                            GC_OR[j], 128, accum=(j > 0))
                g.wait("scalar", io)
                g.wait("scalar", "sfm")
                g.op("scalar", lambda s: s.copy(t_ev[:, :], t_sum[:, 0:TC]))
                g.wait("sync", "scalar")
                g.dma("sync", "sfm", fm_sc.ap()[mt * 128:(mt + 1) * 128, :], t_ev[:, :])
            for tt in range(NTT):
                g.wait("sync", "sfm")
                g.wait("sync", "scalar")
                g.wait("sync", "vector")
                g.wait("sync", "sout")
                g.wait("sync", "sltm")
                g.wait("sync", "sx2")
                g.dma_tr("sync", "tlo", t_lno[:, :], fm_sc.ap()[:, tt * 128:(tt + 1) * 128])
                g.dma("sync", "xtm", t_ln[:, :], x2_sc.ap()[tt * 128:(tt + 1) * 128, :])
                g.wait("vector", "tlo")
                g.wait("vector", "xtm")
                g.wait("scalar", "tlo")
                ln_freedim(g, t_lno[:, :], t_lnsq[:, :])
                g.op("vector", lambda v: v.tensor_tensor(t_ln[:, :], t_ln[:, :], t_lnsq[:, :], ALU.add))
                g.wait("sync", "vector")
                g.dma("sync", "sout",
                      yout[(k // ndie) * TC + tt * 128:(k // ndie) * TC + (tt + 1) * 128, :],
                      t_ln[:, :])
            g.wait("scalar", io)

        for k in range(nkc):
            if "p" in stages:
                fguard(k, lambda g, k=k: prescan(g, k), cls=("a", k))
            if "z" in stages:
                fguard(16 + k, lambda g, k=k: zstage(g, k), cls=("f", k))
            if "x" in stages:
                fguard(k, lambda g, k=k: prescan_x(g, k), cls=("a", k))
            if "l" in stages:
                fguard(8 + k, lambda g, k=k: scanstage(g, k, False), cls=("l", k))
            if "s" in stages:
                fguard(16 + k, lambda g, k=k: scanstage(g, k, True), cls=("f", k))
            if "o" in stages:
                fguard(16 + k, lambda g, k=k: outstage(g, k), cls=("f", k))
            if "w" in stages or "n" in stages or "2" in stages:
                fguard(24 + k, lambda g, k=k: ownstage(g, k), cls=("o", k))
            if "n" in stages:
                fguard(32 + k, lambda g, k=k: poll2(g, k), cls=("p2", k))
                fguard(16 + k, lambda g, k=k: ffnstage(g, k), cls=("f", k))
            if "2" in stages:
                fguard(24 + k, lambda g, k=k: own2stage(g, k), cls=("o", k))

        P.wait("sync", "sout")

        if dbg:
            for j in range(4):
                P.wait("gpsimd", "dbgg")
                igather(P, "dbgg", t_cf[:, :], XDBL, 0, GC_XRA[j], 128)
                P.wait("sync", "dbgg")
                P.dma("sync", "dbgs", dbg_o[j * 128:(j + 1) * 128, :], t_cf[:, :])
            P.wait("sync", "dbgs")

        for name in list(P.cnt):
            P.sems[name] = e(nc.semaphore("S_" + name))

        block = e(nc.Block())

        @block.sync
        def _(sync):
            for fn in P.q["sync"]:
                fn(sync)

        @block.tensor
        def _(tensor):
            for fn in P.q["tensor"]:
                fn(tensor)

        @block.vector
        def _(vector):
            for fn in P.q["vector"]:
                fn(vector)

        @block.scalar
        def _(scalar):
            for fn in P.q["scalar"]:
                fn(scalar)

        @block.gpsimd
        def _(gpsimd):
            for fn in P.q["gpsimd"]:
                fn(gpsimd)

    return nc


def _host_layout_v1(inputs, ndie=4, nonce=1000):
    import ml_dtypes
    ndtc = NDT // ndie
    nitp = (NIT // ndie) + 1
    interp = nitp * 128 * ndie

    x = np.ascontiguousarray(np.asarray(inputs["L0_attn_hidden_states"], np.float32).reshape(L, DM))
    wip = np.asarray(inputs["L0_attn_in_proj_w"], np.float32)
    xp = np.asarray(inputs["L0_attn_x_proj_w"], np.float32)
    wxT = wip[0:DI].T
    wzT = wip[DI:2 * DI].T
    woT = np.asarray(inputs["L0_attn_out_proj_w"], np.float32).T

    def blk_w(wT):
        return wT.reshape(NFT, 128, NDT, 128).transpose(1, 2, 0, 3).reshape(128, NDT * 2048)

    wx_full = blk_w(wxT)
    wz_full = blk_w(wzT)
    wot_full = np.ascontiguousarray(
        woT.reshape(NDT, 128, NFT, 128).transpose(1, 0, 2, 3).reshape(128, NDT * 2048))

    def blk_di(w):
        wd = w.shape[1]
        return w.reshape(NDT, 128, wd).transpose(1, 0, 2).reshape(128, NDT * wd)

    xp_dt_f = blk_di(np.ascontiguousarray(xp[0:DTR].T))
    xp_b_f = blk_di(np.ascontiguousarray(xp[DTR:DTR + DS].T))
    xp_c_f = blk_di(np.ascontiguousarray(xp[DTR + DS:DTR + 2 * DS].T))
    convw_f = blk_di(np.asarray(inputs["L0_attn_conv_w"], np.float32))
    convb_f = blk_di(np.asarray(inputs["L0_attn_conv_b"], np.float32).reshape(DI, 1))
    dtb_f = blk_di(np.asarray(inputs["L0_attn_dt_proj_b"], np.float32).reshape(DI, 1))
    alog_f = blk_di(np.asarray(inputs["L0_attn_A_log"], np.float32))
    dvec_f = blk_di(np.asarray(inputs["L0_attn_D"], np.float32).reshape(DI, 1))
    wdtt_f = np.ascontiguousarray(np.asarray(inputs["L0_attn_dt_proj_w"], np.float32).T)

    gi = np.asarray(inputs["L0_ffn_gate_idx"], np.int32)
    ui = np.asarray(inputs["L0_ffn_up_idx"], np.int32)
    dix = np.asarray(inputs["L0_ffn_down_idx"], np.int32)
    gi_p = np.zeros((interp, DM), np.int32)
    ui_p = np.zeros((interp, DM), np.int32)
    di_p = np.zeros((DM, interp), np.int32)
    gi_p[0:INTER] = gi
    ui_p[0:INTER] = ui
    di_p[:, 0:INTER] = dix
    nit_p = interp // 128
    g_all = gi_p.reshape(nit_p, 128, NFT, 128).transpose(3, 0, 2, 1).reshape(128, nit_p * NFT * 128)
    u_all = ui_p.reshape(nit_p, 128, NFT, 128).transpose(3, 0, 2, 1).reshape(128, nit_p * NFT * 128)
    d_all = di_p.reshape(NFT, 128, nit_p, 128).transpose(3, 0, 2, 1).reshape(128, NFT * nit_p * 128)

    sel = np.zeros((DS, DS * 128), ml_dtypes.bfloat16)
    for n in range(DS):
        sel[n, n * 128:(n + 1) * 128] = 1.0
    lut128 = np.ascontiguousarray(np.broadcast_to(
        np.asarray(inputs["L0_ffn_lut"], np.float32).reshape(1, NLUT), (128, NLUT)))

    fvals = np.zeros((128, 8 * 16), np.int32)
    for k in range(8):
        fvals[:, k * 16:(k + 1) * 16] = np.int32(nonce + k)

    maps = []
    ncore = 8
    for c in range(ncore):
        die = c // ndie if ndie == 4 else c // ndie
        dc = c % ndie
        dsl = slice(dc * ndtc * 2048, (dc + 1) * ndtc * 2048)
        it0 = dc * nitp
        m = {
            "xin": x,
            "xmy": np.ascontiguousarray(
                np.concatenate([x[k * TC:(k + 1) * TC]
                                for k in range(8) if k % ndie == dc], axis=0)
                if ndie < 4 else
                np.concatenate([x[c * TC:(c + 1) * TC]] * 2, axis=0)),
            "wx_blk": np.ascontiguousarray(wx_full[:, dsl]),
            "wz_blk": np.ascontiguousarray(wz_full[:, dsl]),
            "wot_blk": np.ascontiguousarray(wot_full[:, dsl]),
            "wxp_dt": np.ascontiguousarray(xp_dt_f[:, dc * ndtc * DTR:(dc + 1) * ndtc * DTR]),
            "wxp_b": np.ascontiguousarray(xp_b_f[:, dc * ndtc * DS:(dc + 1) * ndtc * DS]),
            "wxp_c": np.ascontiguousarray(xp_c_f[:, dc * ndtc * DS:(dc + 1) * ndtc * DS]),
            "wdtt": np.ascontiguousarray(wdtt_f[:, dc * ndtc * 128:(dc + 1) * ndtc * 128]),
            "p_convw_in": np.ascontiguousarray(convw_f[:, dc * ndtc * DC:(dc + 1) * ndtc * DC]),
            "p_convb_in": np.ascontiguousarray(convb_f[:, dc * ndtc:(dc + 1) * ndtc]),
            "p_dtb_in": np.ascontiguousarray(dtb_f[:, dc * ndtc:(dc + 1) * ndtc]),
            "p_alog_in": np.ascontiguousarray(alog_f[:, dc * ndtc * DS:(dc + 1) * ndtc * DS]),
            "p_dvec_in": np.ascontiguousarray(dvec_f[:, dc * ndtc:(dc + 1) * ndtc]),
            "lutv": lut128,
            "g_blk": np.ascontiguousarray(g_all[:, it0 * NFT * 128:(it0 + nitp) * NFT * 128]),
            "u_blk": np.ascontiguousarray(u_all[:, it0 * NFT * 128:(it0 + nitp) * NFT * 128]),
            "d_blk": np.ascontiguousarray(
                d_all.reshape(128, NFT, nit_p, 128)[:, :, it0:it0 + nitp, :]
                .reshape(128, NFT * nitp * 128)),
            "selmat": sel,
            "fvals_in": fvals,
        }
        hm = np.zeros((128, nitp), np.float32)
        for j in range(nitp):
            gidx0 = (it0 + j) * 128
            for p in range(128):
                hm[p, j] = 1.0 if (gidx0 + p) < INTER else 0.0
        m["hmask_in"] = hm
        fl = np.zeros((1, 64), np.int32)
        for k in range(8):
            if ndie == 4:
                active = (die == 1) or (k < 4)
                full = (die == 0 and k < 4) or (die == 1 and k >= 4)
            else:
                active = k <= 2 * die + 1
                full = k // 2 == die
            lite = active and not full
            own = full and (dc == (k % ndie))
            fl[0, k] = 1 if (lite or full) else 0
            fl[0, 8 + k] = 1 if lite else 0
            fl[0, 16 + k] = 1 if full else 0
            fl[0, 24 + k] = 1 if own else 0
            fl[0, 32 + k] = 1 if (full and not own) else 0
        m["flags_in"] = fl
        gidx = np.zeros((128, 32), np.int64)
        mypair = (c // 2) % 2 if ndie == 4 else 0
        if ndie == 4:
            lpair_global = c // 2
            mydom = DOM_OF_LPAIR[lpair_global]
            otherpair_global = (c // 2) ^ 1
            otherdom = DOM_OF_LPAIR[otherpair_global]
        p_ = np.arange(128)
        rb = lambda dcj: dcj * 8 * 192

        def delta_to(dcj):
            if ndie != 4:
                return 0
            pj = (die * ndie + dcj) // 2
            dj = DOM_OF_LPAIR[pj]
            return (HBM_BASE[dj] - HBM_BASE[mydom])

        gidx[:, GC_XW0] = dc * 8 * 192 + p_
        gidx[0:64, GC_XW1] = dc * 8 * 192 + 128 + p_[0:64]
        for j in range(ndie):
            d_rows = delta_to(j) // (TC * 4)
            gidx[:, GC_XRA[j]] = d_rows + j * 8 * 192 + p_
            gidx[0:64, GC_XRB[j]] = d_rows + j * 8 * 192 + 128 + p_[0:64]
        gidx[0:16, GC_FWL] = dc * 16 + p_[0:16]
        if ndie == 4:
            d_flg = (HBM_BASE[otherdom] - HBM_BASE[mydom]) // 4096
        else:
            d_flg = 0
        gidx[0:16, GC_FWR] = d_flg + dc * 16 + p_[0:16]
        gidx[:, GC_OW] = dc * 8 * DM + p_
        for j in range(ndie):
            d_rows = delta_to(j) // (TC * 4)
            gidx[:, GC_OR[j]] = d_rows + j * 8 * DM + p_
        gidx[:, GC_L2W] = p_
        if ndie == 4:
            for op_ in range(2):
                ownerpair_global = die * 2 + op_
                dj = DOM_OF_LPAIR[ownerpair_global]
                gidx[:, GC_L2R[op_]] = (HBM_BASE[dj] - HBM_BASE[mydom]) // (TC * 4) + p_
        else:
            gidx[:, GC_L2R[0]] = p_
            gidx[:, GC_L2R[1]] = p_
        assert gidx.max() < 2**31 and gidx.min() > -2**31
        m["gidx_in"] = gidx.astype(np.int32)
        maps.append(m)
    return maps


_NC_CACHE = {}


def _host_layout(inputs):
    import ml_dtypes
    x = np.ascontiguousarray(np.asarray(inputs["L0_attn_hidden_states"], np.float32).reshape(L, DM))
    wip = np.asarray(inputs["L0_attn_in_proj_w"], np.float32)
    xp = np.asarray(inputs["L0_attn_x_proj_w"], np.float32)

    def blk_w(wT):
        return np.ascontiguousarray(
            wT.reshape(NFT, 128, NDT, 128).transpose(1, 2, 0, 3).reshape(128, NDT * 2048))

    def blk_di(w):
        wd = w.shape[1]
        return np.ascontiguousarray(
            w.reshape(NDT, 128, wd).transpose(1, 0, 2).reshape(128, NDT * wd))

    wxT = wip[0:DI].T
    wzT = wip[DI:2 * DI].T
    woT = np.asarray(inputs["L0_attn_out_proj_w"], np.float32).T
    wot_blk = np.ascontiguousarray(
        woT.reshape(NDT, 128, NFT, 128).transpose(1, 0, 2, 3).reshape(128, NDT * 2048))

    gi = np.asarray(inputs["L0_ffn_gate_idx"], np.int32)
    ui = np.asarray(inputs["L0_ffn_up_idx"], np.int32)
    di = np.asarray(inputs["L0_ffn_down_idx"], np.int32)
    g_blk = np.ascontiguousarray(
        gi.reshape(NIT, 128, NFT, 128).transpose(3, 0, 2, 1).reshape(128, NIT * NFT * 128))
    u_blk = np.ascontiguousarray(
        ui.reshape(NIT, 128, NFT, 128).transpose(3, 0, 2, 1).reshape(128, NIT * NFT * 128))
    d_blk = np.ascontiguousarray(
        di.reshape(NFT, 128, NIT, 128).transpose(3, 0, 2, 1).reshape(128, NFT * NIT * 128))

    sel = np.zeros((DS, DS * 128), ml_dtypes.bfloat16)
    for n in range(DS):
        sel[n, n * 128:(n + 1) * 128] = 1.0

    base = {
        "xin": x,
        "wx_blk": blk_w(wxT),
        "wz_blk": blk_w(wzT),
        "wot_blk": wot_blk,
        "wxp_dt": blk_di(xp[0:DTR].T.reshape(DI, DTR)),
        "wxp_b": blk_di(xp[DTR:DTR + DS].T.reshape(DI, DS)),
        "wxp_c": blk_di(xp[DTR + DS:DTR + 2 * DS].T.reshape(DI, DS)),
        "wdtt": np.ascontiguousarray(np.asarray(inputs["L0_attn_dt_proj_w"], np.float32).T),
        "p_convw_in": blk_di(np.asarray(inputs["L0_attn_conv_w"], np.float32)),
        "p_convb_in": blk_di(np.asarray(inputs["L0_attn_conv_b"], np.float32).reshape(DI, 1)),
        "p_dtb_in": blk_di(np.asarray(inputs["L0_attn_dt_proj_b"], np.float32).reshape(DI, 1)),
        "p_alog_in": blk_di(np.asarray(inputs["L0_attn_A_log"], np.float32)),
        "p_dvec_in": blk_di(np.asarray(inputs["L0_attn_D"], np.float32).reshape(DI, 1)),
        "lutv": np.ascontiguousarray(np.broadcast_to(np.asarray(inputs["L0_ffn_lut"], np.float32).reshape(1, NLUT), (128, NLUT))),
        "g_blk": g_blk,
        "u_blk": u_blk,
        "d_blk": d_blk,
        "selmat": sel,
    }
    return base, x


def kernel(**inputs):
    if os.environ.get("BASS_SIM_FP64") == "1":
        base, x = _host_layout(inputs)
        if "nc0" not in _NC_CACHE:
            _NC_CACHE["nc0"] = build_program_v0()
        nc = _NC_CACHE["nc0"]
        in_maps = []
        for c in range(8):
            m = dict(base)
            m["xmy"] = np.ascontiguousarray(x[c * TC:(c + 1) * TC])
            m["kown_in"] = np.array([[c]], np.int32)
            in_maps.append(m)
        res = run_bass_kernel_spmd(nc, in_maps, core_ids=list(range(8)))
        out = np.concatenate([np.asarray(res.results[c]["yout"], np.float32)
                              for c in range(8)], axis=0)
        return out.reshape(1, L, DM)
    ndie = 2
    in_maps = _host_layout_v1(
        inputs, ndie=ndie,
        nonce=int(np.random.randint(1 << 20, 1 << 30)))
    if "nc1" not in _NC_CACHE:
        _NC_CACHE["nc1"] = build_program_v1(ndie=ndie)
    nc = _NC_CACHE["nc1"]
    res = run_bass_kernel_spmd(nc, in_maps, core_ids=list(range(8)))
    out = np.empty((L, DM), np.float32)
    for k in range(8):
        if ndie == 4:
            src_core, oi = k, k // 4
        else:
            src_core, oi = 2 * ((k // 2) % 4) + (k % 2), k // 2
        out[k * TC:(k + 1) * TC] = np.asarray(
            res.results[src_core]["yout"], np.float32)[oi * TC:(oi + 1) * TC]
    return out.reshape(1, L, DM)


if __name__ == "__main__":
    import reference as _r
    ins = _r.setup_inputs(0)
    out = kernel(**{k: np.asarray(v) for k, v in ins.items()})
    ref = np.asarray(_r.reference(**ins))
    err = np.linalg.norm(out - ref) / np.linalg.norm(ref)
    print("rel err:", err)


# version 10
# speedup vs baseline: 4.4249x; 1.0368x over previous
import os
import numpy as np
import contextlib
import concourse.bass as bass
import concourse.mybir as mybir
from concourse.bass_utils import run_bass_kernel_spmd

F32 = mybir.dt.float32
BF16 = mybir.dt.bfloat16
I32 = mybir.dt.int32
AF = mybir.ActivationFunctionType
ALU = mybir.AluOpType
AX = mybir.AxisListType.X

L, DM, DI, DTR, DS, DC, INTER, NLUT = 8192, 2048, 4096, 128, 16, 4, 5504, 16
EPS = 1e-5
TC = 1024
NKC = L // TC
NDT = DI // 128
NFT = DM // 128
NIT = INTER // 128
NTT = TC // 128

ENGS = ("sync", "tensor", "vector", "scalar", "gpsimd")


def _impl(ctx, sc):
    if sc is None:
        return True
    if ctx is None:
        return False
    t1, k1 = ctx
    t2, k2 = sc
    if k2 > k1:
        return False
    if t2 == "a":
        return True
    if t2 == "f":
        return t1 in ("f", "o", "p2") and k1 // 2 == k2 // 2
    if t2 == "l":
        return t1 == "l"
    if t2 == "o":
        return t1 == "o" and k1 == k2
    return False


class B:

    SAFE = True

    def __init__(self):
        self.q = {g: [] for g in ENGS}
        self.cnt = {}
        self.owner = {}
        self.sems = {}
        self.kreg = {}
        self.ctx = None
        self.scls = {}

    def _c(self, name):
        return self.cnt.get(name, 0)

    def op(self, g, fn, inc=1, selfwait=True):
        sems = self.sems
        if self.SAFE and selfwait and self._c(g) > 0:
            pv = self._c(g)

            def wrun(eng, pv=pv, g=g):
                eng.wait_ge(sems[g], pv)
            self.q[g].append(wrun)
        self.cnt[g] = self._c(g) + inc
        self.owner[g] = g

        def run(eng, fn=fn, g=g, inc=inc):
            fn(eng).then_inc(sems[g], inc)
        self.q[g].append(run)
        return self.cnt[g]

    def silent(self, g, fn):
        self.q[g].append(fn)

    def wait(self, g, src, val=None):
        if src in self.scls:
            assert _impl(self.ctx, self.scls[src]), (src, self.ctx, self.scls[src])
        if val is None:
            val = self._c(src)
        sems = self.sems

        def run(eng, src=src, val=val):
            if val > 0:
                eng.wait_ge(sems[src], val)
        self.q[g].append(run)

    def dma(self, g, sem, out, in_, **kw):
        self.cnt[sem] = self._c(sem) + 16
        self.owner[sem] = g
        sems = self.sems

        def run(eng, sem=sem):
            eng.dma_start(out=out, in_=in_, **kw).then_inc(sems[sem], 16)
        self.q[g].append(run)
        return self.cnt[sem]

    def dma_tr(self, g, sem, out, in_):
        self.cnt[sem] = self._c(sem) + 16
        self.owner[sem] = g
        sems = self.sems

        def run(eng, sem=sem):
            eng.dma_start_transpose(out, in_).then_inc(sems[sem], 16)
        self.q[g].append(run)
        return self.cnt[sem]

    def guarded(self, k, builder):
        gb = B.__new__(B)
        gb.q = {g: [] for g in ENGS}
        gb.cnt = dict(self.cnt)
        gb.owner = self.owner
        gb.sems = self.sems
        gb.kreg = self.kreg
        builder(gb)
        for g in ENGS:
            span = gb.q[g]
            deltas = [(name, gb._c(name) - self._c(name))
                      for name in gb.cnt
                      if gb.owner.get(name) == g and gb._c(name) > self._c(name)]
            if not span and not deltas:
                continue
            sems = self.sems
            kreg = self.kreg

            def run(eng, g=g, span=span, deltas=deltas, k=k):
                with eng.If_eq(kreg[g], k):
                    for fn in span:
                        fn(eng)
                if deltas:
                    with eng.Else():
                        for name, d in deltas:
                            eng.sem_inc(sems[name], d)
            self.q[g].append(run)
        self.cnt = gb.cnt


def build_program_v0(nkc=NKC):
    nc = bass.Bass("TRN2", target_bir_lowering=False)

    def din(name, shape, dt=F32):
        return nc.dram_tensor(name, shape, dt, kind="ExternalInput").ap()

    xin = din("xin", [L, DM])
    xmy = din("xmy", [TC, DM])
    wx_blk = din("wx_blk", [128, NDT * 2048])
    wz_blk = din("wz_blk", [128, NDT * 2048])
    wot_blk = din("wot_blk", [128, NDT * 2048])
    wxp_dt = din("wxp_dt", [128, NDT * DTR])
    wxp_b = din("wxp_b", [128, NDT * DS])
    wxp_c = din("wxp_c", [128, NDT * DS])
    wdtt = din("wdtt", [DTR, DI])
    p_convw_in = din("p_convw_in", [128, NDT * DC])
    p_convb_in = din("p_convb_in", [128, NDT])
    p_dtb_in = din("p_dtb_in", [128, NDT])
    p_alog_in = din("p_alog_in", [128, NDT * DS])
    p_dvec_in = din("p_dvec_in", [128, NDT])
    lutv = din("lutv", [128, NLUT])
    g_blk = din("g_blk", [128, NIT * NFT * 128], I32)
    u_blk = din("u_blk", [128, NIT * NFT * 128], I32)
    d_blk = din("d_blk", [128, NFT * NIT * 128], I32)
    selmat = din("selmat", [DS, DS * 128], BF16)
    kown_in = din("kown_in", [1, 1], I32)

    yout = nc.dram_tensor("yout", [TC, DM], F32, kind="ExternalOutput").ap()

    def dscr(name, shape, dt):
        return nc.dram_tensor(name, shape, dt, kind="Internal")

    wxbf = dscr("wxbf", [128, NDT * 2048], BF16)
    xld_sc = dscr("xld_sc", [DI, TC], BF16)
    ln_tm = dscr("ln_tm", [TC, DM], BF16)
    ygate_sc = dscr("ygate_sc", [DI, TC], BF16)
    sz_sc = dscr("sz_sc", [DI, TC], BF16)
    wgbf = dscr("wgbf", [128, NIT * NFT * 128], BF16)
    wubf = dscr("wubf", [128, NIT * NFT * 128], BF16)
    wdbf = dscr("wdbf", [128, NFT * NIT * 128], BF16)
    h_sc = dscr("h_sc", [INTER, TC], BF16)
    fm_sc = dscr("fm_sc", [DM, TC], BF16)
    x2_sc = dscr("x2_sc", [TC, DM], F32)

    with contextlib.ExitStack() as stx:
        e = stx.enter_context

        def sb(name, p, f, dt):
            return e(nc.sbuf_tensor(name, [p, f], dt))

        ARENA_F = 20480
        arena = sb("arena", 128, ARENA_F, F32)
        a_bf = arena.bitcast(BF16)
        a_i32 = arena.bitcast(I32)

        def vbf(off_f32, free):
            return a_bf[0:128, off_f32 * 2:off_f32 * 2 + free]

        def vi32(off_f32, free):
            return a_i32[0:128, off_f32:off_f32 + free]

        ln1t = vbf(0, NFT * TC)
        b_bc = vbf(4096, DS * TC)
        c_bc = vbf(12288, DS * TC)
        cv_idx = [vi32(0, INTER), vi32(5504, INTER)]
        cv_ixb = vbf(11008, INTER)
        cv_acc = vbf(13760, INTER)
        cv_t = vbf(16512, INTER)
        ln2t = vbf(0, NFT * TC)
        dn_l = vbf(4096, NIT * 128)

        p_negA = sb("p_negA", 128, NDT * DS, F32)
        p_convw = sb("p_convw", 128, NDT * DC, F32)
        p_convb = sb("p_convb", 128, NDT, F32)
        p_dtb = sb("p_dtb", 128, NDT, F32)
        p_dvec = sb("p_dvec", 128, NDT, F32)
        p_sel = sb("p_sel", DS, DS * 128, BF16)
        p_lut = sb("p_lut", 128, NLUT, F32)
        p_kown = sb("p_kown", 1, 1, I32)
        carry = sb("carry", 128, NDT * DS, F32)
        dt_lhs = sb("dt_lhs", DTR, DI, BF16)
        xp_dt_l = sb("xp_dt_l", 128, NDT * DTR, BF16)
        xp_b_l = sb("xp_b_l", 128, NDT * DS, BF16)
        xp_c_l = sb("xp_c_l", 128, NDT * DS, BF16)
        dtlo_bf = sb("dtlo_bf", DTR, TC, BF16)
        bmat = sb("bmat", DS, TC, BF16)
        cmat = sb("cmat", DS, TC, BF16)
        xpre = sb("xpre", 128, NDT * (DC - 1), F32)

        t_ln = sb("t_ln", 128, DM, F32)
        t_lnsq = sb("t_lnsq", 128, DM, F32)
        t_st = sb("t_st", 128, 8, F32)
        t_lno = sb("t_lno", 128, DM, BF16)
        w_f32 = sb("w_f32", 128, DM, F32)
        w_bf = [sb("w_bf_%d" % i, 128, DM, BF16) for i in range(2)]
        t_xr = sb("t_xr", 128, TC + DC - 1, F32)
        t_cf = sb("t_cf", 128, TC, F32)
        t_sp = sb("t_sp", 128, TC, F32)
        t_dl = sb("t_dl", 128, TC, BF16)
        t_u = sb("t_u", 128, TC, BF16)
        t_xld = sb("t_xld", 128, TC, BF16)
        t_a = sb("t_a", 128, TC, F32)
        t_b = sb("t_b", 128, TC, BF16)
        t_h = sb("t_h", 128, TC, F32)
        t_ch = sb("t_ch", 128, TC, BF16)
        t_y = sb("t_y", 128, TC, BF16)
        t_yg = sb("t_yg", 128, TC, BF16)
        t_sz = sb("t_sz", 128, TC, BF16)
        t_ev = sb("t_ev", 128, TC, BF16)
        t_yt = [sb("t_yt_%d" % i, 128, TC, BF16) for i in range(2)]

        ps_mm = e(nc.psum_tensor("ps_mm", [128, TC], F32))
        ps_d = e(nc.psum_tensor("ps_d", [128, TC], F32))
        ps_bc = e(nc.psum_tensor("ps_bc", [128, TC], F32))
        ps_c = e(nc.psum_tensor("ps_c", [DS, TC], F32))

        P = B()

        def mm_pair(t, psum, lhsT, rhs, start, stop, pl=128, pbase=0):
            t.matmul(psum[pbase:pbase + pl, 0:512], lhsT, rhs[:, 0:512],
                     start=start, stop=stop)
            return t.matmul(psum[pbase:pbase + pl, 512:1024], lhsT,
                            rhs[:, 512:1024], start=start, stop=stop)

        P.dma("sync", "pld", p_sel[:, :], selmat)
        P.dma("sync", "pld", p_lut[:, :], lutv)
        P.dma("sync", "pld", p_kown[:, :], kown_in)
        P.dma("sync", "pld", p_convw[:, :], p_convw_in)
        P.dma("sync", "pld", p_convb[:, :], p_convb_in)
        P.dma("sync", "pld", p_dtb[:, :], p_dtb_in)
        P.dma("sync", "pld", p_dvec[:, :], p_dvec_in)
        P.dma("sync", "pld", p_negA[:, :], p_alog_in)

        for g in ENGS:
            P.wait(g, "pld")

            def mkload(g=g):
                def run(eng):
                    reg = eng.register("kreg_" + g).__enter__()
                    P.kreg[g] = reg
                    eng.reg_load(reg, p_kown[0:1, 0:1])
                return run
            P.silent(g, mkload())

        P.op("scalar", lambda s: s.activation(p_negA[:, :], p_negA[:, :], AF.Exp))
        P.wait("vector", "scalar")
        P.op("vector", lambda v: v.tensor_single_scalar(p_negA[:, :], p_negA[:, :], -1.0, ALU.mult))
        P.op("vector", lambda v: v.memset(carry[:, :], 0.0))
        P.op("vector", lambda v: v.memset(xpre[:, :], 0.0))

        def cast_load(dst_ap, src_ap, width):
            P.wait("sync", "vector")
            P.dma("sync", "wf32", w_f32[:, 0:width], src_ap)
            P.wait("vector", "wf32")
            P.op("vector", lambda v: v.tensor_copy(dst_ap, w_f32[:, 0:width]))

        for h in range(2):
            cast_load(dt_lhs[:, h * DM:(h + 1) * DM], wdtt[:, h * DM:(h + 1) * DM], DM)
        cast_load(xp_dt_l[:, 0:DM], wxp_dt[:, 0:DM], DM)
        cast_load(xp_dt_l[:, DM:2 * DM], wxp_dt[:, DM:2 * DM], DM)
        cast_load(xp_b_l[:, :], wxp_b[:, :], NDT * DS)
        cast_load(xp_c_l[:, :], wxp_c[:, :], NDT * DS)

        for i in range(NDT):
            bi = i % 2
            P.wait("sync", "vector")
            P.dma("sync", "wf32", w_f32[:, :], wx_blk[:, i * 2048:(i + 1) * 2048])
            P.wait("vector", "wf32")
            P.wait("vector", "swx%d" % bi)
            P.op("vector", lambda v, bi=bi: v.tensor_copy(w_bf[bi][:, :], w_f32[:, :]))
            P.wait("gpsimd", "vector")
            P.dma("gpsimd", "swx%d" % bi, wxbf.ap()[:, i * 2048:(i + 1) * 2048], w_bf[bi][:, :])

        def convert(src, dst, width_total):
            nch = width_total // INTER
            for i in range(nch):
                bi = i % 2
                P.wait("sync", "vector")
                P.dma("sync", "cvi%d" % bi, cv_idx[bi], src[:, i * INTER:(i + 1) * INTER])
                P.wait("vector", "cvi%d" % bi)
                P.wait("vector", "scv")
                P.op("vector", lambda v, bi=bi: v.tensor_copy(cv_ixb, cv_idx[bi]))
                P.op("vector", lambda v: v.tensor_scalar(
                    cv_acc, cv_ixb, 0.0, p_lut[:, 0:1], ALU.is_equal, ALU.mult))
                for vv in range(1, NLUT):
                    P.op("vector", lambda v, vv=vv: v.tensor_scalar(
                        cv_t, cv_ixb, float(vv), p_lut[:, vv:vv + 1],
                        ALU.is_equal, ALU.mult))
                    P.op("vector", lambda v: v.tensor_tensor(cv_acc, cv_acc, cv_t, ALU.add))
                P.wait("gpsimd", "vector")
                P.dma("gpsimd", "scv", dst.ap()[:, i * INTER:(i + 1) * INTER], cv_acc)

        convert(g_blk, wgbf, NIT * NFT * 128)
        convert(u_blk, wubf, NIT * NFT * 128)
        convert(d_blk, wdbf, NFT * NIT * 128)
        P.wait("vector", "scv")
        P.wait("sync", "swx0")
        P.wait("sync", "swx1")
        P.wait("sync", "scv")

        def ln_freedim(src, dst):
            P.op("vector", lambda v: v.reduce_sum(t_st[:, 0:1], src, axis=AX))
            P.op("scalar", lambda s: s.activation(t_lnsq[:, :], src, AF.Square))
            P.wait("vector", "scalar")
            P.op("vector", lambda v: v.reduce_sum(t_st[:, 1:2], t_lnsq[:, :], axis=AX))
            P.op("vector", lambda v: v.tensor_single_scalar(t_st[:, 2:3], t_st[:, 0:1], 1.0 / DM, ALU.mult))
            P.op("vector", lambda v: v.tensor_single_scalar(t_st[:, 3:4], t_st[:, 1:2], 1.0 / DM, ALU.mult))
            P.op("vector", lambda v: v.tensor_tensor(t_st[:, 4:5], t_st[:, 2:3], t_st[:, 2:3], ALU.mult))
            P.op("vector", lambda v: v.tensor_tensor(t_st[:, 5:6], t_st[:, 3:4], t_st[:, 4:5], ALU.subtract))
            P.op("vector", lambda v: v.tensor_single_scalar(t_st[:, 5:6], t_st[:, 5:6], EPS, ALU.add))
            P.wait("scalar", "vector")
            P.op("scalar", lambda s: s.activation(t_st[:, 6:7], t_st[:, 5:6], AF.Sqrt))
            P.wait("vector", "scalar")
            P.op("vector", lambda v: v.reciprocal(t_st[:, 7:8], t_st[:, 6:7]))
            P.op("vector", lambda v: v.tensor_tensor(t_st[:, 6:7], t_st[:, 2:3], t_st[:, 7:8], ALU.mult))
            P.op("vector", lambda v: v.tensor_single_scalar(t_st[:, 6:7], t_st[:, 6:7], -1.0, ALU.mult))
            P.op("vector", lambda v: v.tensor_scalar(
                dst, src, t_st[:, 7:8], t_st[:, 6:7], ALU.mult, ALU.add))

        for k in range(nkc):
            for tt in range(NTT):
                row0 = k * TC + tt * 128
                P.wait("sync", "vector")
                P.wait("sync", "scalar")
                P.dma("sync", "xtm", t_ln[:, :], xin[row0:row0 + 128, :])
                P.wait("vector", "xtm")
                P.wait("scalar", "xtm")
                ln_freedim(t_ln[:, :], t_lnsq[:, :])
                P.wait("vector", "sltm")
                P.op("vector", lambda v: v.tensor_copy(t_lno[:, :], t_lnsq[:, :]))
                P.wait("gpsimd", "vector")
                P.dma("gpsimd", "sltm", ln_tm.ap()[tt * 128:(tt + 1) * 128, :], t_lno[:, :])
            for ft in range(NFT):
                P.wait("sync", "tensor")
                P.wait("sync", "sltm")
                P.dma_tr("sync", "l1t", ln1t[:, ft * TC:(ft + 1) * TC],
                         ln_tm.ap()[:, ft * 128:(ft + 1) * 128])
            ln1t_ready = P._c("l1t")

            for dt in range(NDT):
                bi = dt % 2
                P.wait("sync", "tensor")
                P.dma("sync", "wbf%d" % bi, w_bf[bi][:, :],
                      wxbf.ap()[:, dt * 2048:(dt + 1) * 2048])
                P.wait("tensor", "wbf%d" % bi)
                P.wait("tensor", "l1t", ln1t_ready)
                P.wait("tensor", "vector")
                for kt in range(NFT):
                    st_, sp_ = kt == 0, kt == NFT - 1
                    if sp_:
                        P.op("tensor", (lambda t, kt=kt, bi=bi, st_=st_: mm_pair(
                            t, ps_mm, w_bf[bi][:, kt * 128:(kt + 1) * 128],
                            ln1t[:, kt * TC:(kt + 1) * TC], st_, True)), selfwait=False)
                    else:
                        P.silent("tensor", (lambda t, kt=kt, bi=bi, st_=st_: mm_pair(
                            t, ps_mm, w_bf[bi][:, kt * 128:(kt + 1) * 128],
                            ln1t[:, kt * TC:(kt + 1) * TC], st_, False)))
                P.wait("vector", "tensor")
                P.op("vector", lambda v, dt=dt: v.tensor_copy(
                    t_xr[:, 0:DC - 1], xpre[:, dt * (DC - 1):(dt + 1) * (DC - 1)]))
                P.op("vector", lambda v: v.tensor_copy(t_xr[:, DC - 1:DC - 1 + TC], ps_mm[:, :]))
                P.op("vector", lambda v, dt=dt: v.tensor_copy(
                    xpre[:, dt * (DC - 1):(dt + 1) * (DC - 1)], t_xr[:, TC:TC + DC - 1]))
                P.op("vector", lambda v, dt=dt: v.tensor_scalar(
                    t_cf[:, :], t_xr[:, 0:TC], p_convw[:, dt * DC:dt * DC + 1], None, ALU.mult))
                for j in range(1, DC):
                    P.op("vector", lambda v, j=j, dt=dt: v.scalar_tensor_tensor(
                        t_cf[:, :], t_xr[:, j:j + TC], p_convw[:, dt * DC + j:dt * DC + j + 1],
                        t_cf[:, :], ALU.mult, ALU.add))
                P.wait("scalar", "vector")
                P.wait("scalar", "sxld")
                P.wait("scalar", "tensor")
                P.op("scalar", lambda s, dt=dt: s.activation(
                    t_xld[:, :], t_cf[:, :], AF.Silu, bias=p_convb[:, dt:dt + 1]))
                P.wait("gpsimd", "scalar")
                P.dma("gpsimd", "sxld", xld_sc.ap()[dt * 128:(dt + 1) * 128, :], t_xld[:, :])
                P.wait("tensor", "scalar")
                st_, sp_ = dt == 0, dt == NDT - 1
                P.silent("tensor", (lambda t, dt=dt, st_=st_, sp_=sp_: mm_pair(
                    t, ps_d, xp_dt_l[:, dt * DTR:(dt + 1) * DTR], t_xld, st_, sp_)))
                P.silent("tensor", (lambda t, dt=dt, st_=st_, sp_=sp_: mm_pair(
                    t, ps_bc, xp_b_l[:, dt * DS:(dt + 1) * DS], t_xld, st_, sp_, pl=DS)))
                P.op("tensor", (lambda t, dt=dt, st_=st_, sp_=sp_: mm_pair(
                    t, ps_c, xp_c_l[:, dt * DS:(dt + 1) * DS], t_xld, st_, sp_,
                    pl=DS)), selfwait=False)
            P.wait("scalar", "tensor")
            P.op("scalar", lambda s: s.copy(dtlo_bf[:, :], ps_d[:, :]))
            P.op("scalar", lambda s: s.copy(bmat[:, :], ps_bc[0:DS, :]))
            P.op("scalar", lambda s: s.copy(cmat[:, :], ps_c[0:DS, :]))

            def z_builder(g, k=k):
                for dt in range(NDT):
                    bi = dt % 2
                    g.wait("sync", "vector")
                    g.dma("sync", "wf32", w_f32[:, :], wz_blk[:, dt * 2048:(dt + 1) * 2048])
                    g.wait("vector", "wf32")
                    g.wait("vector", "tensor")
                    g.op("vector", lambda v, bi=bi: v.tensor_copy(w_bf[bi][:, :], w_f32[:, :]))
                    g.wait("tensor", "vector")
                    g.wait("tensor", "scalar")
                    for kt in range(NFT):
                        if kt == NFT - 1:
                            g.op("tensor", (lambda t, kt=kt, bi=bi: mm_pair(
                                t, ps_mm, w_bf[bi][:, kt * 128:(kt + 1) * 128],
                                ln1t[:, kt * TC:(kt + 1) * TC], kt == 0, True)), selfwait=False)
                        else:
                            g.silent("tensor", (lambda t, kt=kt, bi=bi: mm_pair(
                                t, ps_mm, w_bf[bi][:, kt * 128:(kt + 1) * 128],
                                ln1t[:, kt * TC:(kt + 1) * TC], kt == 0, False)))
                    g.wait("scalar", "tensor")
                    g.wait("scalar", "ssz")
                    g.op("scalar", lambda s: s.activation(t_sz[:, :], ps_mm[:, :], AF.Silu))
                    g.wait("gpsimd", "scalar")
                    g.dma("gpsimd", "ssz", sz_sc.ap()[dt * 128:(dt + 1) * 128, :], t_sz[:, :])
            P.guarded(k, z_builder)

            P.wait("tensor", "scalar")
            for n in range(DS):
                P.op("tensor", (lambda t, n=n: mm_pair(
                    t, ps_bc, p_sel[:, n * 128:(n + 1) * 128], bmat, True, True)), selfwait=False)
                P.wait("scalar", "tensor")
                P.op("scalar", lambda s, n=n: s.copy(b_bc[:, n * TC:(n + 1) * TC], ps_bc[:, :]))
                P.wait("tensor", "scalar")
                P.op("tensor", (lambda t, n=n: mm_pair(
                    t, ps_bc, p_sel[:, n * 128:(n + 1) * 128], cmat, True, True)), selfwait=False)
                P.wait("scalar", "tensor")
                P.op("scalar", lambda s, n=n: s.copy(c_bc[:, n * TC:(n + 1) * TC], ps_bc[:, :]))
                P.wait("tensor", "scalar")

            for dt in range(NDT):
                P.wait("tensor", "scalar")
                P.wait("tensor", "vector")
                P.op("tensor", (lambda t, dt=dt: mm_pair(
                    t, ps_d, dt_lhs[:, dt * 128:(dt + 1) * 128], dtlo_bf, True, True)), selfwait=False)
                P.wait("scalar", "tensor")
                P.wait("scalar", "vector")
                P.op("scalar", lambda s, dt=dt: s.activation(
                    t_sp[:, :], ps_d[:, :], AF.Abs, bias=p_dtb[:, dt:dt + 1]))
                P.op("scalar", lambda s: s.activation(t_sp[:, :], t_sp[:, :], AF.Exp, scale=-1.0))
                P.op("scalar", lambda s: s.activation(t_sp[:, :], t_sp[:, :], AF.Ln, bias=1.0))
                P.wait("vector", "scalar")
                P.op("vector", lambda v, dt=dt: v.tensor_scalar(
                    t_cf[:, :], ps_d[:, :], p_dtb[:, dt:dt + 1], 0.0, ALU.add, ALU.max))
                P.op("vector", lambda v: v.tensor_tensor(t_dl[:, :], t_cf[:, :], t_sp[:, :], ALU.add))
                P.wait("sync", "vector")
                P.wait("sync", "tensor")
                P.wait("sync", "sxld")
                P.dma("sync", "lxld", t_xld[:, :], xld_sc.ap()[dt * 128:(dt + 1) * 128, :])
                P.wait("vector", "lxld")
                P.op("vector", lambda v: v.tensor_tensor(t_u[:, :], t_dl[:, :], t_xld[:, :], ALU.mult))
                for n in range(DS):
                    col = dt * DS + n
                    P.wait("scalar", "vector")
                    P.op("scalar", lambda s, col=col: s.activation(
                        t_a[:, :], t_dl[:, :], AF.Exp, scale=p_negA[:, col:col + 1]))
                    P.wait("vector", "scalar")
                    P.op("vector", lambda v, n=n: v.tensor_tensor(
                        t_b[:, :], t_u[:, :], b_bc[:, n * TC:(n + 1) * TC], ALU.mult))
                    P.op("vector", lambda v, col=col: v.tensor_tensor_scan(
                        t_h[:, :], t_a[:, :], t_b[:, :], carry[:, col:col + 1],
                        ALU.mult, ALU.add))
                    P.op("vector", lambda v, col=col: v.tensor_copy(
                        carry[:, col:col + 1], t_h[:, TC - 1:TC]))

                    def own_scan(g, n=n):
                        g.op("vector", lambda v, n=n: v.tensor_tensor(
                            t_ch[:, :], t_h[:, :], c_bc[:, n * TC:(n + 1) * TC], ALU.mult))
                        if n == 0:
                            g.op("vector", lambda v: v.tensor_copy(t_y[:, :], t_ch[:, :]))
                        else:
                            g.op("vector", lambda v: v.tensor_tensor(
                                t_y[:, :], t_y[:, :], t_ch[:, :], ALU.add))
                    P.guarded(k, own_scan)

                def own_gate(g, dt=dt):
                    g.wait("gpsimd", "vector")
                    g.wait("gpsimd", "ssz")
                    g.dma("gpsimd", "lsz", t_sz[:, :], sz_sc.ap()[dt * 128:(dt + 1) * 128, :])
                    g.op("vector", lambda v, dt=dt: v.scalar_tensor_tensor(
                        t_y[:, :], t_xld[:, :], p_dvec[:, dt:dt + 1], t_y[:, :],
                        ALU.mult, ALU.add))
                    g.wait("vector", "lsz")
                    g.wait("vector", "syg")
                    g.op("vector", lambda v: v.tensor_tensor(t_yg[:, :], t_y[:, :], t_sz[:, :], ALU.mult))
                    g.wait("gpsimd", "vector")
                    g.dma("gpsimd", "syg", ygate_sc.ap()[dt * 128:(dt + 1) * 128, :], t_yg[:, :])
                P.guarded(k, own_gate)

        syg_total = P._c("syg")
        ssz_total = P._c("ssz")

        for mtp in range(NFT // 2):
            for half, psum in ((0, ps_mm), (1, ps_d)):
                mt = mtp * 2 + half
                P.wait("tensor", "scalar")
                for dt in range(NDT):
                    bi = dt % 2
                    P.wait("sync", "vector")
                    P.dma("sync", "wf32", w_f32[:, 0:128],
                          wot_blk[:, dt * 2048 + mt * 128:dt * 2048 + (mt + 1) * 128])
                    P.wait("sync", "tensor")
                    P.wait("sync", "syg", syg_total)
                    P.dma("sync", "yt%d" % bi, t_yt[bi][:, :],
                          ygate_sc.ap()[dt * 128:(dt + 1) * 128, :])
                    P.wait("vector", "wf32")
                    P.op("vector", lambda v, bi=bi: v.tensor_copy(w_bf[bi][:, 0:128], w_f32[:, 0:128]))
                    P.wait("tensor", "vector")
                    P.wait("tensor", "yt%d" % bi)
                    P.op("tensor", (lambda t, bi=bi, psum=psum, dt=dt: mm_pair(
                        t, psum, w_bf[bi][:, 0:128], t_yt[bi], dt == 0, dt == NDT - 1)),
                        selfwait=False)
                P.wait("scalar", "tensor")
                P.wait("scalar", "sfm")
                P.op("scalar", lambda s, psum=psum: s.copy(t_ev[:, :], psum[:, :]))
                P.wait("gpsimd", "scalar")
                P.dma("gpsimd", "sfm", fm_sc.ap()[mt * 128:(mt + 1) * 128, :], t_ev[:, :])
        P.wait("sync", "ssz", ssz_total)

        for tt in range(NTT):
            P.wait("sync", "sfm")
            P.wait("sync", "scalar")
            P.wait("sync", "vector")
            P.wait("sync", "sltm")
            P.wait("sync", "sx2")
            P.dma_tr("sync", "tlo", t_lno[:, :], fm_sc.ap()[:, tt * 128:(tt + 1) * 128])
            P.dma("sync", "xtm", t_ln[:, :], xmy[tt * 128:(tt + 1) * 128, :])
            P.wait("vector", "tlo")
            P.wait("vector", "xtm")
            P.wait("scalar", "tlo")
            ln_freedim(t_lno[:, :], t_lnsq[:, :])
            P.wait("vector", "sx2")
            P.op("vector", lambda v: v.tensor_tensor(t_ln[:, :], t_ln[:, :], t_lnsq[:, :], ALU.add))
            P.wait("gpsimd", "vector")
            P.dma("gpsimd", "sx2", x2_sc.ap()[tt * 128:(tt + 1) * 128, :], t_ln[:, :])
            P.wait("scalar", "vector")
            ln_freedim(t_ln[:, :], t_lnsq[:, :])
            P.wait("vector", "sltm")
            P.op("vector", lambda v: v.tensor_copy(t_lno[:, :], t_lnsq[:, :]))
            P.wait("gpsimd", "vector")
            P.dma("gpsimd", "sltm", ln_tm.ap()[tt * 128:(tt + 1) * 128, :], t_lno[:, :])
        for ft in range(NFT):
            P.wait("sync", "sltm")
            P.wait("sync", "tensor")
            P.dma_tr("sync", "l1t", ln2t[:, ft * TC:(ft + 1) * TC],
                     ln_tm.ap()[:, ft * 128:(ft + 1) * 128])
        ln2t_ready = P._c("l1t")

        for mt in range(NIT):
            bi = mt % 2
            P.wait("sync", "tensor")
            P.dma("sync", "wbf%d" % bi, w_bf[bi][:, :], wgbf.ap()[:, mt * 2048:(mt + 1) * 2048])
            P.wait("sync", "scalar")
            P.wait("sync", "vector")
            P.dma("sync", "tlo", t_lno[:, :], wubf.ap()[:, mt * 2048:(mt + 1) * 2048])
            P.wait("tensor", "wbf%d" % bi)
            P.wait("tensor", "tlo")
            P.wait("tensor", "l1t", ln2t_ready)
            P.wait("tensor", "vector")
            P.wait("tensor", "scalar")
            for kt in range(NFT):
                if kt == NFT - 1:
                    P.op("tensor", (lambda t, kt=kt, bi=bi: mm_pair(
                        t, ps_mm, w_bf[bi][:, kt * 128:(kt + 1) * 128],
                        ln2t[:, kt * TC:(kt + 1) * TC], kt == 0, True)), selfwait=False)
                    P.op("tensor", (lambda t, kt=kt: mm_pair(
                        t, ps_d, t_lno[:, kt * 128:(kt + 1) * 128],
                        ln2t[:, kt * TC:(kt + 1) * TC], kt == 0, True)), selfwait=False)
                else:
                    P.silent("tensor", (lambda t, kt=kt, bi=bi: mm_pair(
                        t, ps_mm, w_bf[bi][:, kt * 128:(kt + 1) * 128],
                        ln2t[:, kt * TC:(kt + 1) * TC], kt == 0, False)))
                    P.silent("tensor", (lambda t, kt=kt: mm_pair(
                        t, ps_d, t_lno[:, kt * 128:(kt + 1) * 128],
                        ln2t[:, kt * TC:(kt + 1) * TC], kt == 0, False)))
            P.wait("scalar", "tensor")
            P.wait("scalar", "vector")
            P.op("scalar", lambda s: s.activation(t_ev[:, :], ps_mm[:, :], AF.Silu))
            P.wait("vector", "scalar")
            P.wait("vector", "sh")
            P.op("vector", lambda v: v.tensor_tensor(t_yg[:, :], t_ev[:, :], ps_d[:, :], ALU.mult))
            P.wait("gpsimd", "vector")
            P.dma("gpsimd", "sh", h_sc.ap()[mt * 128:(mt + 1) * 128, :], t_yg[:, :])
        sh_total = P._c("sh")

        for m2p in range(NFT // 2):
            for half, psum in ((0, ps_mm), (1, ps_d)):
                mt2 = m2p * 2 + half
                P.wait("tensor", "scalar")
                P.wait("sync", "tensor")
                P.dma("sync", "dnl", dn_l, wdbf.ap()[:, mt2 * (NIT * 128):(mt2 + 1) * (NIT * 128)])
                for it in range(NIT):
                    bi = it % 2
                    P.wait("sync", "tensor")
                    P.wait("sync", "sh", sh_total)
                    P.dma("sync", "yt%d" % bi, t_yt[bi][:, :], h_sc.ap()[it * 128:(it + 1) * 128, :])
                    P.wait("tensor", "yt%d" % bi)
                    P.wait("tensor", "dnl")
                    P.op("tensor", (lambda t, it=it, bi=bi, psum=psum: mm_pair(
                        t, psum, dn_l[:, it * 128:(it + 1) * 128], t_yt[bi], it == 0, it == NIT - 1)),
                        selfwait=False)
                P.wait("scalar", "tensor")
                P.wait("scalar", "sfm")
                P.op("scalar", lambda s, psum=psum: s.copy(t_ev[:, :], psum[:, :]))
                P.wait("gpsimd", "scalar")
                P.dma("gpsimd", "sfm", fm_sc.ap()[mt2 * 128:(mt2 + 1) * 128, :], t_ev[:, :])
        sfm_total = P._c("sfm")

        for tt in range(NTT):
            P.wait("sync", "sfm", sfm_total)
            P.wait("sync", "scalar")
            P.wait("sync", "vector")
            P.wait("sync", "sout")
            P.wait("sync", "sltm")
            P.wait("sync", "sx2")
            P.dma_tr("sync", "tlo", t_lno[:, :], fm_sc.ap()[:, tt * 128:(tt + 1) * 128])
            P.dma("sync", "xtm", t_ln[:, :], x2_sc.ap()[tt * 128:(tt + 1) * 128, :])
            P.wait("vector", "tlo")
            P.wait("vector", "xtm")
            P.wait("scalar", "tlo")
            ln_freedim(t_lno[:, :], t_lnsq[:, :])
            P.op("vector", lambda v: v.tensor_tensor(t_ln[:, :], t_ln[:, :], t_lnsq[:, :], ALU.add))
            P.wait("sync", "vector")
            P.dma("sync", "sout", yout[tt * 128:(tt + 1) * 128, :], t_ln[:, :])
        P.wait("sync", "sout")

        for name in list(P.cnt):
            P.sems[name] = e(nc.semaphore("S_" + name))

        block = e(nc.Block())

        @block.sync
        def _(sync):
            for fn in P.q["sync"]:
                fn(sync)

        @block.tensor
        def _(tensor):
            for fn in P.q["tensor"]:
                fn(tensor)

        @block.vector
        def _(vector):
            for fn in P.q["vector"]:
                fn(vector)

        @block.scalar
        def _(scalar):
            for fn in P.q["scalar"]:
                fn(scalar)

        @block.gpsimd
        def _(gpsimd):
            for fn in P.q["gpsimd"]:
                fn(gpsimd)

    return nc




HBM_BASE = [0x0, 0x4000000000, 0x800000000000, 0x804000000000]
DOM_OF_LPAIR = {0: 2, 1: 3, 2: 1, 3: 0}

GC_XW0, GC_XW1 = 0, 1
GC_XRA = [2, 3, 4, 5]
GC_XRB = [6, 7, 8, 9]
GC_FWL, GC_FWR = 10, 11
GC_OW = 12
GC_OR = [13, 14, 15, 16]
GC_L2W = 17
GC_L2R = [18, 19]


def build_program_v1(ndie=4, nkc=8, rd=True, stages="pzxlsofn2", dbg=False):
    ndtc = NDT // ndie
    nitp = (NIT // ndie) + 1
    spill = ndie < 4
    nc = bass.Bass("TRN2", target_bir_lowering=False, detect_race_conditions=rd)

    def din(name, shape, dt=F32):
        return nc.dram_tensor(name, shape, dt, kind="ExternalInput").ap()

    nown = max(1, nkc // ndie)
    xin = din("xin", [L, DM])
    xmy = din("xmy", [nown * TC, DM])
    wx_blk = din("wx_blk", [128, ndtc * 2048])
    wz_blk = din("wz_blk", [128, ndtc * 2048])
    wot_blk = din("wot_blk", [128, ndtc * 2048])
    wxp_dt = din("wxp_dt", [128, ndtc * DTR])
    wxp_b = din("wxp_b", [128, ndtc * DS])
    wxp_c = din("wxp_c", [128, ndtc * DS])
    wdtt = din("wdtt", [DTR, ndtc * 128])
    p_convw_in = din("p_convw_in", [128, ndtc * DC])
    p_convb_in = din("p_convb_in", [128, ndtc])
    p_dtb_in = din("p_dtb_in", [128, ndtc])
    p_alog_in = din("p_alog_in", [128, ndtc * DS])
    p_dvec_in = din("p_dvec_in", [128, ndtc])
    lutv = din("lutv", [128, NLUT])
    g_blk = din("g_blk", [128, nitp * NFT * 128], I32)
    u_blk = din("u_blk", [128, nitp * NFT * 128], I32)
    d_blk = din("d_blk", [128, NFT * nitp * 128], I32)
    hmask_in = din("hmask_in", [128, nitp])
    selmat = din("selmat", [DS, DS * 128], BF16)
    flags_in = din("flags_in", [1, 64], I32)
    dbg_o = (nc.dram_tensor("dbg_o", [4 * 128, TC], F32, kind="ExternalOutput").ap()
             if dbg else None)
    fvals_in = din("fvals_in", [128, 8 * 16], I32)
    gidx_in = din("gidx_in", [128, 32], I32)

    yout = nc.dram_tensor("yout", [nown * TC, DM], F32, kind="ExternalOutput").ap()

    def dscr(name, shape, dt, **kw):
        return nc.dram_tensor(name, shape, dt, kind="Internal", **kw)

    wxbf = dscr("wxbf", [128, ndtc * 2048], BF16)
    xld_sc = dscr("xld_sc", [ndtc * 128, TC], BF16)
    ln_tm = dscr("ln_tm", [TC, DM], BF16)
    sz_sc = dscr("sz_sc", [ndtc * 128, TC], BF16)
    yg_sc = dscr("yg_sc", [ndtc * 128, TC], BF16)
    h_sc = dscr("h_sc", [nitp * 128, TC], BF16)
    wgbf = dscr("wgbf", [128, nitp * NFT * 128], BF16)
    wubf = dscr("wubf", [128, nitp * NFT * 128], BF16)
    wdbf = dscr("wdbf", [128, NFT * nitp * 128], BF16)
    fm_sc = dscr("fm_sc", [DM, TC], BF16)
    x2_sc = dscr("x2_sc", [TC, DM], F32)

    XDBL = dscr("XDBL", [ndie * 8 * 192, TC], F32, addr_space="Shared")
    OPART = dscr("OPART", [ndie * 8 * DM, TC], F32, addr_space="Shared")
    DPART = dscr("DPART", [ndie * 8 * DM, TC], F32, addr_space="Shared")
    LN2X = dscr("LN2X", [8 * DM, 2 * TC], BF16, addr_space="Shared")
    FLG = dscr("FLG", [4 * 4 * 128, 1024], I32, addr_space="Shared")

    with contextlib.ExitStack() as stx:
        e = stx.enter_context

        def sb(name, p, f, dt):
            return e(nc.sbuf_tensor(name, [p, f], dt))

        ARENA_F = 20480
        arena = sb("arena", 128, ARENA_F, F32)
        a_bf = arena.bitcast(BF16)
        a_i32 = arena.bitcast(I32)

        def vbf(off_f32, free):
            return a_bf[0:128, off_f32 * 2:off_f32 * 2 + free]

        def vi32(off_f32, free):
            return a_i32[0:128, off_f32:off_f32 + free]

        ln1t = vbf(0, NFT * TC)
        b_bc = vbf(4096, DS * TC)
        c_bc = vbf(12288, DS * TC)
        cv_idx = [vi32(0, INTER), vi32(5504, INTER)]
        cv_ixb = vbf(11008, INTER)
        cv_acc = vbf(13760, INTER)
        cv_t = vbf(16512, INTER)
        ln2t = vbf(0, NFT * TC)
        dn_l = vbf(4096, nitp * 128)

        p_negA = sb("p_negA", 128, ndtc * DS, F32)
        p_convw = sb("p_convw", 128, ndtc * DC, F32)
        p_convb = sb("p_convb", 128, ndtc, F32)
        p_dtb = sb("p_dtb", 128, ndtc, F32)
        p_dvec = sb("p_dvec", 128, ndtc, F32)
        p_sel = sb("p_sel", DS, DS * 128, BF16)
        p_lut = sb("p_lut", 128, NLUT, F32)
        p_hm = sb("p_hm", 128, nitp, F32)
        p_flags = sb("p_flags", 1, 64, I32)
        p_fval = sb("p_fval", 128, 8 * 16, I32)
        p_gidx = sb("p_gidx", 128, 32, I32)
        t_fw = sb("t_fw", 1, 8 * 16, I32)
        carry = sb("carry", 128, ndtc * DS, F32)
        dt_lhs = sb("dt_lhs", DTR, ndtc * 128, BF16)
        xp_dt_l = sb("xp_dt_l", 128, ndtc * DTR, BF16)
        xp_b_l = sb("xp_b_l", 128, ndtc * DS, BF16)
        xp_c_l = sb("xp_c_l", 128, ndtc * DS, BF16)
        dtlo_bf = sb("dtlo_bf", DTR, TC, BF16)
        bmat = sb("bmat", DS, TC, BF16)
        cmat = sb("cmat", DS, TC, BF16)
        xpre = sb("xpre", 128, ndtc * (DC - 1), F32)
        yg8 = sb("yg8", 128, (1 if spill else ndtc) * TC, BF16)
        h_ffn = sb("h_ffn", 128, (1 if spill else nitp) * TC, BF16)
        t_yt = [sb("t_yt0", 128, TC, BF16), sb("t_yt1", 128, TC, BF16)] if spill else None

        t_ln = sb("t_ln", 128, DM, F32)
        t_lnsq = sb("t_lnsq", 128, DM, F32)
        t_st = sb("t_st", 128, 8, F32)
        t_lno = sb("t_lno", 128, DM, BF16)
        w_f32 = sb("w_f32", 128, DM, F32)
        w_bf = [sb("w_bf_%d" % i, 128, DM, BF16) for i in range(2)]
        t_xr = sb("t_xr", 128, TC + DC - 1, F32)
        t_cf = sb("t_cf", 128, TC, F32)
        t_sp = sb("t_sp", 128, TC, F32)
        t_dl = sb("t_dl", 128, TC, BF16)
        t_u = sb("t_u", 128, TC, BF16)
        t_xld = sb("t_xld", 128, TC, BF16)
        _ta0 = sb("t_a_0", 128, TC, F32)
        t_a = [_ta0, sb("t_a_1", 128, TC, F32)] if spill else [_ta0, _ta0]
        t_b = sb("t_b", 128, TC, BF16)
        t_h = sb("t_h", 128, TC, F32)
        t_ch = sb("t_ch", 128, TC, BF16)
        t_y = sb("t_y", 128, TC, BF16)
        t_sz = sb("t_sz", 128, TC, BF16)
        t_ev = sb("t_ev", 128, TC, BF16)

        xdl_a = t_sp[:, 0:TC]
        xdl_b = t_xr[0:64, 0:TC]
        t_bc64 = t_sp
        t_sum = t_cf

        ps_mm = e(nc.psum_tensor("ps_mm", [128, TC], F32))
        ps_d = e(nc.psum_tensor("ps_d", [128, TC], F32))
        ps_bc = e(nc.psum_tensor("ps_bc", [128, TC], F32))
        ps_c = e(nc.psum_tensor("ps_c", [DS, TC], F32))

        P = B()
        P.cnt.setdefault("pollq", 0)

        def mm_pair(t, psum, lhsT, rhs, start, stop, pl=128):
            t.matmul(psum[0:pl, 0:512], lhsT, rhs[:, 0:512], start=start, stop=stop)
            return t.matmul(psum[0:pl, 512:1024], lhsT, rhs[:, 512:1024],
                            start=start, stop=stop)

        from concourse.bass import IndirectOffsetOnAxis

        def iscatter(g, sem, dram_t, row0, src_ap, idx_col, nparts, w=None):
            g.cnt[sem] = g._c(sem) + 16
            g.owner[sem] = "gpsimd"
            assert g.scls.setdefault(sem, g.ctx) == g.ctx, sem
            sems = g.sems
            ncols = dram_t.shape[1]
            out_ap = dram_t.ap()
            eoff = row0 * ncols

            def run(eng, sem=sem, out_ap=out_ap, src_ap=src_ap, idx_col=idx_col,
                    nparts=nparts, eoff=eoff):
                eng.indirect_dma_start(
                    out_ap,
                    IndirectOffsetOnAxis(ap=p_gidx[0:nparts, idx_col:idx_col + 1], axis=0),
                    src_ap, None,
                    element_offset=eoff,
                    bounds_check=None, oob_is_err=False,
                ).then_inc(sems[sem], 16)
            g.q["gpsimd"].append(run)
            return g._c(sem)

        def igather(g, sem, dst_ap, dram_t, row0, idx_col, nparts, accum=False):
            g.cnt[sem] = g._c(sem) + 16
            g.owner[sem] = "gpsimd"
            assert g.scls.setdefault(sem, g.ctx) == g.ctx, sem
            sems = g.sems
            ncols = dram_t.shape[1]
            in_ap = dram_t.ap()
            eoff = row0 * ncols

            def run(eng, sem=sem, in_ap=in_ap, dst_ap=dst_ap, idx_col=idx_col,
                    nparts=nparts, accum=accum, eoff=eoff):
                eng.indirect_dma_start(
                    dst_ap, None,
                    in_ap,
                    IndirectOffsetOnAxis(ap=p_gidx[0:nparts, idx_col:idx_col + 1], axis=0),
                    element_offset=eoff,
                    bounds_check=None, oob_is_err=False,
                    compute_op=(ALU.add if accum else ALU.bypass),
                ).then_inc(sems[sem], 16)
            g.q["gpsimd"].append(run)
            return g._c(sem)

        P.dma("sync", "pld", p_sel[:, :], selmat)
        P.dma("sync", "pld", p_lut[:, :], lutv)
        P.dma("sync", "pld", p_hm[:, :], hmask_in)
        P.dma("sync", "pld", p_flags[:, :], flags_in)
        P.dma("sync", "pld", p_fval[:, :], fvals_in)
        P.dma("sync", "pld", p_gidx[:, :], gidx_in)
        P.dma("sync", "pld", p_convw[:, :], p_convw_in)
        P.dma("sync", "pld", p_convb[:, :], p_convb_in)
        P.dma("sync", "pld", p_dtb[:, :], p_dtb_in)
        P.dma("sync", "pld", p_dvec[:, :], p_dvec_in)
        P.dma("sync", "pld", p_negA[:, :], p_alog_in)
        for g in ENGS:
            P.wait(g, "pld")

        P.op("scalar", lambda s: s.activation(p_negA[:, :], p_negA[:, :], AF.Exp))
        P.wait("vector", "scalar")
        P.op("vector", lambda v: v.tensor_single_scalar(p_negA[:, :], p_negA[:, :], -1.0, ALU.mult))
        P.op("vector", lambda v: v.memset(carry[:, :], 0.0))
        P.op("vector", lambda v: v.memset(xpre[:, :], 0.0))
        P.op("vector", lambda v: v.memset(t_sp[:, :], 0.0))

        def cast_load(dst_t, src_t, width):
            for o in range(0, width, DM):
                w = min(DM, width - o)
                P.wait("sync", "vector")
                P.dma("sync", "wf32", w_f32[:, 0:w], src_t[:, o:o + w])
                P.wait("vector", "wf32")
                P.op("vector", lambda v, o=o, w=w: v.tensor_copy(
                    dst_t[:, o:o + w], w_f32[:, 0:w]))

        cast_load(dt_lhs, wdtt, ndtc * 128)
        cast_load(xp_dt_l, wxp_dt, ndtc * DTR)
        cast_load(xp_b_l, wxp_b, ndtc * DS)
        cast_load(xp_c_l, wxp_c, ndtc * DS)

        for i in range(ndtc):
            bi = i % 2
            P.wait("sync", "vector")
            P.dma("sync", "wf32", w_f32[:, :], wx_blk[:, i * 2048:(i + 1) * 2048])
            P.wait("vector", "wf32")
            P.wait("vector", "swx%d" % bi)
            P.op("vector", lambda v, bi=bi: v.tensor_copy(w_bf[bi][:, :], w_f32[:, :]))
            P.wait("gpsimd", "vector")
            P.dma("gpsimd", "swx%d" % bi, wxbf.ap()[:, i * 2048:(i + 1) * 2048], w_bf[bi][:, :])

        def convert(src, dst, width_total):
            nch = (width_total + INTER - 1) // INTER
            for i in range(nch):
                off = i * INTER
                w = min(INTER, width_total - off)
                bi = i % 2
                P.wait("sync", "vector")
                P.dma("sync", "cvi%d" % bi, cv_idx[bi][:, 0:w], src[:, off:off + w])
                P.wait("vector", "cvi%d" % bi)
                P.wait("vector", "scv")
                P.op("vector", lambda v, bi=bi, w=w: v.tensor_copy(cv_ixb[:, 0:w], cv_idx[bi][:, 0:w]))
                P.op("vector", lambda v, w=w: v.tensor_scalar(
                    cv_acc[:, 0:w], cv_ixb[:, 0:w], 0.0, p_lut[:, 0:1], ALU.is_equal, ALU.mult))
                for vv in range(1, NLUT):
                    P.op("vector", lambda v, vv=vv, w=w: v.tensor_scalar(
                        cv_t[:, 0:w], cv_ixb[:, 0:w], float(vv), p_lut[:, vv:vv + 1],
                        ALU.is_equal, ALU.mult))
                    P.op("vector", lambda v, w=w: v.tensor_tensor(
                        cv_acc[:, 0:w], cv_acc[:, 0:w], cv_t[:, 0:w], ALU.add))
                P.wait("gpsimd", "vector")
                P.dma("gpsimd", "scv", dst.ap()[:, off:off + w], cv_acc[:, 0:w])

        convert(g_blk, wgbf, nitp * NFT * 128)
        convert(u_blk, wubf, nitp * NFT * 128)
        convert(d_blk, wdbf, NFT * nitp * 128)
        P.wait("vector", "scv")
        P.wait("sync", "swx0")
        P.wait("sync", "swx1")
        P.wait("sync", "scv")

        pregs = {}

        def init_pregs(eng):
            pregs["scnt"] = eng.register("pscnt").__enter__()
            pregs["rv"] = [eng.register("prv%d" % j).__enter__() for j in range(4)]
            pregs["ok"] = eng.register("pok").__enter__()
            pregs["t1"] = eng.register("pt1").__enter__()
            pregs["no"] = eng.register("pno").__enter__()
            pregs["tgt"] = eng.register("ptgt").__enter__()
            eng.reg_mov(pregs["scnt"], 0)
        P.silent("sync", init_pregs)

        fregs = {}

        def fguard(col, builder, cls=None):
            for g in ENGS:
                def lrun(eng, g=g, col=col):
                    if g not in fregs:
                        fregs[g] = eng.register("freg_" + g).__enter__()
                    eng.reg_load(fregs[g], p_flags[0:1, col:col + 1])
                P.silent(g, lrun)
            gb = B.__new__(B)
            gb.q = {g: [] for g in ENGS}
            gb.cnt = dict(P.cnt)
            gb.owner = P.owner
            gb.sems = P.sems
            gb.kreg = P.kreg
            gb.ctx = cls
            gb.scls = P.scls
            builder(gb)
            for g in ENGS:
                span = gb.q[g]
                deltas = [(name, P._c(name), gb._c(name) - P._c(name))
                          for name in gb.cnt
                          if gb.owner.get(name) == g and gb._c(name) > P._c(name)
                          and name not in gb.scls]
                if not span and not deltas:
                    continue
                sems = P.sems

                def run(eng, g=g, span=span, deltas=deltas):
                    with eng.If_eq(fregs[g], 1):
                        for fn in span:
                            fn(eng)
                    if deltas:
                        with eng.Else():
                            for name, pre, d in deltas:
                                eng.wait_ge(sems[name], pre)
                                eng.sem_inc(sems[name], d)
                P.q[g].append(run)
            P.cnt = gb.cnt

        def poll(g_, stream, k, prods, lw=None):
            nprod = len(prods)
            row0 = prods[0]
            lw_c = g_._c(lw) if lw else 0

            def run(eng, stream=stream, k=k, nprod=nprod, row0=row0,
                    lw=lw, lw_c=lw_c):
                scnt, rv = pregs["scnt"], pregs["rv"]
                ok, t1, no = pregs["ok"], pregs["t1"], pregs["no"]
                tgt = pregs["tgt"]
                if lw:
                    eng.wait_ge(P.sems[lw], lw_c)
                eng.reg_load(tgt, p_fval[0:1, k * 16:k * 16 + 1])
                eng.reg_mov(no, 1)
                with eng.While(no):
                    eng.reg_add(scnt, scnt, 16)
                    eng.dma_start(
                        out=t_fw[0:1, 0:nprod * 16],
                        in_=bass.AP(FLG, (stream * 512 + k * 64 + row0 * 16) * 1024,
                                    [[1, 1], [16 * 1024, nprod], [1, 16]]),
                        allow_slow_non_contiguous=True,
                    ).then_inc(P.sems["pollq"], 16)
                    eng.wait_ge(P.sems["pollq"], scnt)
                    for j in range(nprod):
                        eng.reg_load(rv[j], t_fw[0:1, j * 16:j * 16 + 1])
                    eng.reg_alu(ok, rv[0], tgt, ALU.is_equal)
                    for j in range(1, nprod):
                        eng.reg_alu(t1, rv[j], tgt, ALU.is_equal)
                        eng.reg_alu(ok, ok, t1, ALU.bitwise_and)
                    eng.reg_alu(no, ok, 1, ALU.bitwise_xor)
            g_.silent("sync", run)
            g_.op("sync", lambda eng: eng.nop(), selfwait=False)
            g_.wait("gpsimd", "sync")

        def flag_push(g_, stream, k, sem):
            src = p_fval[0:16, k * 16:(k + 1) * 16]
            cols = (GC_FWL, GC_FWR) if ndie == 4 else (GC_FWL,)
            for col in cols:
                iscatter(g_, sem, FLG, stream * 512 + k * 64, src, col, 16, w=16)

        def ln_freedim(g_, src, dst):
            g_.op("vector", lambda v: v.reduce_sum(t_st[:, 0:1], src, axis=AX))
            g_.op("scalar", lambda s: s.activation(t_lnsq[:, :], src, AF.Square))
            g_.wait("vector", "scalar")
            g_.op("vector", lambda v: v.reduce_sum(t_st[:, 1:2], t_lnsq[:, :], axis=AX))
            g_.op("vector", lambda v: v.tensor_single_scalar(t_st[:, 2:3], t_st[:, 0:1], 1.0 / DM, ALU.mult))
            g_.op("vector", lambda v: v.tensor_single_scalar(t_st[:, 3:4], t_st[:, 1:2], 1.0 / DM, ALU.mult))
            g_.op("vector", lambda v: v.tensor_tensor(t_st[:, 4:5], t_st[:, 2:3], t_st[:, 2:3], ALU.mult))
            g_.op("vector", lambda v: v.tensor_tensor(t_st[:, 5:6], t_st[:, 3:4], t_st[:, 4:5], ALU.subtract))
            g_.op("vector", lambda v: v.tensor_single_scalar(t_st[:, 5:6], t_st[:, 5:6], EPS, ALU.add))
            g_.wait("scalar", "vector")
            g_.op("scalar", lambda s: s.activation(t_st[:, 6:7], t_st[:, 5:6], AF.Sqrt))
            g_.wait("vector", "scalar")
            g_.op("vector", lambda v: v.reciprocal(t_st[:, 7:8], t_st[:, 6:7]))
            g_.op("vector", lambda v: v.tensor_tensor(t_st[:, 6:7], t_st[:, 2:3], t_st[:, 7:8], ALU.mult))
            g_.op("vector", lambda v: v.tensor_single_scalar(t_st[:, 6:7], t_st[:, 6:7], -1.0, ALU.mult))
            g_.op("vector", lambda v: v.tensor_scalar(
                dst, src, t_st[:, 7:8], t_st[:, 6:7], ALU.mult, ALU.add))

        def prescan(g, k):
            ia, iap = "ia%d" % k, ("ia%d" % (k - 1) if k else None)
            for tt in range(NTT):
                row0 = k * TC + tt * 128
                g.wait("sync", "vector")
                g.wait("sync", "scalar")
                g.dma("sync", "xtm", t_ln[:, :], xin[row0:row0 + 128, :])
                g.wait("vector", "xtm")
                g.wait("scalar", "xtm")
                ln_freedim(g, t_ln[:, :], t_lnsq[:, :])
                g.wait("vector", "sltm")
                g.op("vector", lambda v: v.tensor_copy(t_lno[:, :], t_lnsq[:, :]))
                g.wait("sync", "vector")
                g.dma("sync", "sltm", ln_tm.ap()[tt * 128:(tt + 1) * 128, :], t_lno[:, :])
            for ft in range(NFT):
                g.wait("sync", "tensor")
                g.wait("sync", "sltm")
                g.dma_tr("sync", "l1t", ln1t[:, ft * TC:(ft + 1) * TC],
                         ln_tm.ap()[:, ft * 128:(ft + 1) * 128])
            ln1t_ready = g._c("l1t")

            for dt in range(ndtc):
                bi = dt % 2
                g.wait("sync", "tensor")
                g.dma("sync", "wbf%d" % bi, w_bf[bi][:, :],
                      wxbf.ap()[:, dt * 2048:(dt + 1) * 2048])
                g.wait("tensor", "wbf%d" % bi)
                g.wait("tensor", "l1t", ln1t_ready)
                g.wait("tensor", "vector")
                for kt in range(NFT):
                    st_, sp_ = kt == 0, kt == NFT - 1
                    if sp_:
                        g.op("tensor", (lambda t, kt=kt, bi=bi, st_=st_: mm_pair(
                            t, ps_mm, w_bf[bi][:, kt * 128:(kt + 1) * 128],
                            ln1t[:, kt * TC:(kt + 1) * TC], st_, True)), selfwait=False)
                    else:
                        g.silent("tensor", (lambda t, kt=kt, bi=bi, st_=st_: mm_pair(
                            t, ps_mm, w_bf[bi][:, kt * 128:(kt + 1) * 128],
                            ln1t[:, kt * TC:(kt + 1) * TC], st_, False)))
                g.wait("vector", "tensor")
                g.op("vector", lambda v, dt=dt: v.tensor_copy(
                    t_xr[:, 0:DC - 1], xpre[:, dt * (DC - 1):(dt + 1) * (DC - 1)]))
                g.op("vector", lambda v: v.tensor_copy(t_xr[:, DC - 1:DC - 1 + TC], ps_mm[:, :]))
                g.op("vector", lambda v, dt=dt: v.tensor_copy(
                    xpre[:, dt * (DC - 1):(dt + 1) * (DC - 1)], t_xr[:, TC:TC + DC - 1]))
                g.op("vector", lambda v, dt=dt: v.tensor_scalar(
                    t_cf[:, :], t_xr[:, 0:TC], p_convw[:, dt * DC:dt * DC + 1], None, ALU.mult))
                for j in range(1, DC):
                    g.op("vector", lambda v, j=j, dt=dt: v.scalar_tensor_tensor(
                        t_cf[:, :], t_xr[:, j:j + TC], p_convw[:, dt * DC + j:dt * DC + j + 1],
                        t_cf[:, :], ALU.mult, ALU.add))
                g.wait("scalar", "vector")
                g.wait("scalar", "sxld")
                g.wait("scalar", "tensor")
                g.op("scalar", lambda s, dt=dt: s.activation(
                    t_xld[:, :], t_cf[:, :], AF.Silu, bias=p_convb[:, dt:dt + 1]))
                g.wait("sync", "scalar")
                g.dma("sync", "sxld", xld_sc.ap()[dt * 128:(dt + 1) * 128, :], t_xld[:, :])
                g.wait("tensor", "scalar")
                st_, sp_ = dt == 0, dt == ndtc - 1
                g.silent("tensor", (lambda t, dt=dt, st_=st_, sp_=sp_: mm_pair(
                    t, ps_d, xp_dt_l[:, dt * DTR:(dt + 1) * DTR], t_xld, st_, sp_)))
                g.silent("tensor", (lambda t, dt=dt, st_=st_, sp_=sp_: mm_pair(
                    t, ps_bc, xp_b_l[:, dt * DS:(dt + 1) * DS], t_xld, st_, sp_, pl=DS)))
                g.op("tensor", (lambda t, dt=dt, st_=st_, sp_=sp_: mm_pair(
                    t, ps_c, xp_c_l[:, dt * DS:(dt + 1) * DS], t_xld, st_, sp_,
                    pl=DS)), selfwait=False)
            g.wait("scalar", "tensor")
            g.wait("scalar", "vector")
            if iap:
                g.wait("scalar", iap)
            g.op("scalar", lambda s: s.copy(t_cf[:, :], ps_d[:, :]))
            g.op("scalar", lambda s: s.copy(t_bc64[0:DS, :], ps_bc[0:DS, :]))
            g.op("scalar", lambda s: s.copy(t_bc64[32:32 + DS, :], ps_c[0:DS, :]))
            g.wait("gpsimd", "scalar")
            iscatter(g, ia, XDBL, k * 192, t_cf[:, :], GC_XW0, 128)
            iscatter(g, ia, XDBL, k * 192, t_bc64[0:64, 0:TC], GC_XW1, 64)
            g.wait("gpsimd", ia)
            flag_push(g, 0, k, ia)

        def prescan_x(g, k):
            ia = "ia%d" % k
            g.wait("gpsimd", "vector")
            poll(g, 0, k, list(range(ndie)), lw=ia)
            for j in range(ndie):
                g.wait("gpsimd", ia)
                igather(g, ia, xdl_a, XDBL, k * 192, GC_XRA[j], 128, accum=(j > 0))
                g.wait("gpsimd", ia)
                igather(g, ia, xdl_b, XDBL, k * 192, GC_XRB[j], 64, accum=(j > 0))
            g.wait("vector", ia)
            g.wait("vector", "tensor")
            g.op("vector", lambda v: v.tensor_copy(dtlo_bf[:, :], xdl_a))
            g.op("vector", lambda v: v.tensor_copy(bmat[:, :], xdl_b[0:DS, 0:TC]))
            g.op("vector", lambda v: v.tensor_copy(cmat[:, :], xdl_b[32:32 + DS, 0:TC]))
            g.wait("tensor", "vector")
            g.wait("tensor", "scalar")
            for n in range(DS):
                g.op("tensor", (lambda t, n=n: mm_pair(
                    t, ps_bc, p_sel[:, n * 128:(n + 1) * 128], bmat, True, True)), selfwait=False)
                g.wait("scalar", "tensor")
                g.op("scalar", lambda s, n=n: s.copy(b_bc[:, n * TC:(n + 1) * TC], ps_bc[:, :]))
                g.wait("tensor", "scalar")
                g.op("tensor", (lambda t, n=n: mm_pair(
                    t, ps_bc, p_sel[:, n * 128:(n + 1) * 128], cmat, True, True)), selfwait=False)
                g.wait("scalar", "tensor")
                g.op("scalar", lambda s, n=n: s.copy(c_bc[:, n * TC:(n + 1) * TC], ps_bc[:, :]))
                g.wait("tensor", "scalar")

        def zstage(g, k):
            for dt in range(ndtc):
                bi = dt % 2
                g.wait("sync", "vector")
                g.dma("sync", "wf32", w_f32[:, :], wz_blk[:, dt * 2048:(dt + 1) * 2048])
                g.wait("vector", "wf32")
                g.wait("vector", "tensor")
                g.op("vector", lambda v, bi=bi: v.tensor_copy(w_bf[bi][:, :], w_f32[:, :]))
                g.wait("tensor", "vector")
                g.wait("tensor", "scalar")
                for kt in range(NFT):
                    if kt == NFT - 1:
                        g.op("tensor", (lambda t, kt=kt, bi=bi: mm_pair(
                            t, ps_mm, w_bf[bi][:, kt * 128:(kt + 1) * 128],
                            ln1t[:, kt * TC:(kt + 1) * TC], kt == 0, True)), selfwait=False)
                    else:
                        g.silent("tensor", (lambda t, kt=kt, bi=bi: mm_pair(
                            t, ps_mm, w_bf[bi][:, kt * 128:(kt + 1) * 128],
                            ln1t[:, kt * TC:(kt + 1) * TC], kt == 0, False)))
                g.wait("scalar", "tensor")
                g.wait("scalar", "ssz")
                g.op("scalar", lambda s: s.activation(t_sz[:, :], ps_mm[:, :], AF.Silu))
                g.wait("sync", "scalar")
                g.dma("sync", "ssz", sz_sc.ap()[dt * 128:(dt + 1) * 128, :], t_sz[:, :])

        def scanstage(g, k, with_out):
            vcnt = {}
            gi_ = 0
            for dt in range(ndtc):
                g.wait("tensor", "scalar")
                g.wait("tensor", "vector")
                g.op("tensor", (lambda t, dt=dt: mm_pair(
                    t, ps_d, dt_lhs[:, dt * 128:(dt + 1) * 128], dtlo_bf, True, True)), selfwait=False)
                g.wait("scalar", "tensor")
                g.wait("scalar", "vector")
                g.op("scalar", lambda s, dt=dt: s.activation(
                    t_sp[:, :], ps_d[:, :], AF.Abs, bias=p_dtb[:, dt:dt + 1]))
                g.op("scalar", lambda s: s.activation(t_sp[:, :], t_sp[:, :], AF.Exp, scale=-1.0))
                g.op("scalar", lambda s: s.activation(t_sp[:, :], t_sp[:, :], AF.Ln, bias=1.0))
                g.wait("vector", "scalar")
                g.op("vector", lambda v, dt=dt: v.tensor_scalar(
                    t_cf[:, :], ps_d[:, :], p_dtb[:, dt:dt + 1], 0.0, ALU.add, ALU.max))
                g.op("vector", lambda v: v.tensor_tensor(t_dl[:, :], t_cf[:, :], t_sp[:, :], ALU.add))
                g.wait("sync", "vector")
                g.wait("sync", "tensor")
                g.wait("sync", "sxld")
                g.dma("sync", "lxld", t_xld[:, :], xld_sc.ap()[dt * 128:(dt + 1) * 128, :])
                g.wait("vector", "lxld")
                g.op("vector", lambda v: v.tensor_tensor(t_u[:, :], t_dl[:, :], t_xld[:, :], ALU.mult))
                for n in range(DS):
                    col = dt * DS + n
                    nbi = gi_ % 2
                    if spill and n >= 2:
                        g.wait("scalar", "vector", vcnt[gi_ - 2])
                    else:
                        g.wait("scalar", "vector")
                    g.op("scalar", lambda s, col=col, nbi=nbi: s.activation(
                        t_a[nbi][:, :], t_dl[:, :], AF.Exp, scale=p_negA[:, col:col + 1]))
                    g.wait("vector", "scalar")
                    g.op("vector", lambda v, n=n: v.tensor_tensor(
                        t_b[:, :], t_u[:, :], b_bc[:, n * TC:(n + 1) * TC], ALU.mult))
                    g.op("vector", lambda v, col=col, nbi=nbi: v.tensor_tensor_scan(
                        t_h[:, :], t_a[nbi][:, :], t_b[:, :], carry[:, col:col + 1],
                        ALU.mult, ALU.add))
                    g.op("vector", lambda v, col=col: v.tensor_copy(
                        carry[:, col:col + 1], t_h[:, TC - 1:TC]))
                    if with_out:
                        g.op("vector", lambda v, n=n: v.tensor_tensor(
                            t_ch[:, :], t_h[:, :], c_bc[:, n * TC:(n + 1) * TC], ALU.mult))
                        if n == 0:
                            g.op("vector", lambda v: v.tensor_copy(t_y[:, :], t_ch[:, :]))
                        else:
                            g.op("vector", lambda v: v.tensor_tensor(
                                t_y[:, :], t_y[:, :], t_ch[:, :], ALU.add))
                    vcnt[gi_] = g._c("vector")
                    gi_ += 1
                if with_out:
                    g.wait("sync", "vector")
                    g.wait("sync", "ssz")
                    g.dma("sync", "lsz", t_sz[:, :], sz_sc.ap()[dt * 128:(dt + 1) * 128, :])
                    g.op("vector", lambda v, dt=dt: v.scalar_tensor_tensor(
                        t_y[:, :], t_xld[:, :], p_dvec[:, dt:dt + 1], t_y[:, :],
                        ALU.mult, ALU.add))
                    g.wait("vector", "lsz")
                    if spill:
                        g.wait("vector", "syg")
                        g.op("vector", lambda v: v.tensor_tensor(
                            yg8[:, 0:TC], t_y[:, :], t_sz[:, :], ALU.mult))
                        g.wait("sync", "vector")
                        g.dma("sync", "syg", yg_sc.ap()[dt * 128:(dt + 1) * 128, :], yg8[:, 0:TC])
                    else:
                        g.op("vector", lambda v, dt=dt: v.tensor_tensor(
                            yg8[:, dt * TC:(dt + 1) * TC], t_y[:, :], t_sz[:, :], ALU.mult))

        def outstage(g, k):
            fi, fip = "if%d" % k, ("if%d" % (k - 1) if k % 2 else None)
            if fip:
                g.wait("scalar", fip)
            mmc = {}
            for mt in range(NFT):
                g.wait("tensor", "scalar")
                for dt in range(ndtc):
                    bi = dt % 2
                    g.wait("sync", "vector")
                    g.dma("sync", "wf32", w_f32[:, 0:128],
                          wot_blk[:, dt * 2048 + mt * 128:dt * 2048 + (mt + 1) * 128])
                    g.wait("vector", "wf32")
                    g.wait("vector", "tensor")
                    g.op("vector", lambda v, bi=bi: v.tensor_copy(w_bf[bi][:, 0:128], w_f32[:, 0:128]))
                    yb = dt % 2
                    gi_ = mt * ndtc + dt
                    if spill:
                        if gi_ >= 2:
                            g.wait("sync", "tensor", mmc[gi_ - 2])
                        g.wait("sync", "syg")
                        g.dma("sync", "lyt%d" % yb, t_yt[yb][:, :],
                              yg_sc.ap()[dt * 128:(dt + 1) * 128, :])
                        g.wait("tensor", "lyt%d" % yb)
                    g.wait("tensor", "vector")
                    g.op("tensor", (lambda t, bi=bi, dt=dt, yb=yb: mm_pair(
                        t, ps_mm, w_bf[bi][:, 0:128],
                        (t_yt[yb][:, :] if spill else yg8[:, dt * TC:(dt + 1) * TC]),
                        dt == 0, dt == ndtc - 1)), selfwait=False)
                    mmc[gi_] = g._c("tensor")
                g.wait("scalar", "tensor")
                g.wait("scalar", fi)
                g.op("scalar", lambda s: s.copy(t_sum[:, 0:TC], ps_mm[:, :]))
                g.wait("gpsimd", "scalar")
                iscatter(g, fi, OPART, k * DM + mt * 128, t_sum[:, 0:TC], GC_OW, 128)
            g.wait("gpsimd", fi)
            flag_push(g, 1, k, fi)

        def ownstage(g, k):
            io = "io%d" % k
            poll(g, 1, k, list(range(ndie)), lw="if%d" % k)
            for mt in range(NFT):
                g.wait("gpsimd", "scalar")
                for j in range(ndie):
                    g.wait("gpsimd", io)
                    igather(g, io, t_sum[:, 0:TC], OPART, k * DM + mt * 128,
                            GC_OR[j], 128, accum=(j > 0))
                g.wait("scalar", io)
                g.wait("scalar", "sfm")
                g.op("scalar", lambda s: s.copy(t_ev[:, :], t_sum[:, 0:TC]))
                g.wait("sync", "scalar")
                g.dma("sync", "sfm", fm_sc.ap()[mt * 128:(mt + 1) * 128, :], t_ev[:, :])
            for tt in range(NTT):
                g.wait("sync", "sfm")
                g.wait("sync", "scalar")
                g.wait("sync", "vector")
                g.wait("sync", "sltm")
                g.wait("sync", "sx2")
                g.dma_tr("sync", "tlo", t_lno[:, :], fm_sc.ap()[:, tt * 128:(tt + 1) * 128])
                g.dma("sync", "xtm", t_ln[:, :],
                      xmy[(k // ndie) * TC + tt * 128:(k // ndie) * TC + (tt + 1) * 128, :])
                g.wait("vector", "tlo")
                g.wait("vector", "xtm")
                g.wait("scalar", "tlo")
                ln_freedim(g, t_lno[:, :], t_lnsq[:, :])
                g.op("vector", lambda v: v.tensor_tensor(t_ln[:, :], t_ln[:, :], t_lnsq[:, :], ALU.add))
                g.wait("sync", "vector")
                g.dma("sync", "sx2", x2_sc.ap()[tt * 128:(tt + 1) * 128, :], t_ln[:, :])
                g.wait("scalar", "vector")
                ln_freedim(g, t_ln[:, :], t_lnsq[:, :])
                g.wait("vector", "sltm")
                g.op("vector", lambda v: v.tensor_copy(t_lno[:, :], t_lnsq[:, :]))
                g.wait("sync", "vector")
                g.dma("sync", "sltm", ln_tm.ap()[tt * 128:(tt + 1) * 128, :], t_lno[:, :])
            for ft in range(NFT):
                g.wait("sync", "sltm")
                g.wait("sync", "tensor")
                g.dma_tr("sync", "l1t", ln2t[:, ft * TC:(ft + 1) * TC],
                         ln_tm.ap()[:, ft * 128:(ft + 1) * 128])
            g.wait("gpsimd", "l1t")
            for ft in range(NFT):
                iscatter(g, io, LN2X, k * DM + ft * 128,
                         ln2t[:, ft * TC:(ft + 1) * TC], GC_L2W, 128)
            g.wait("gpsimd", io)
            flag_push(g, 2, k, io)
            g.wait("gpsimd", io)
            g.wait("sync", io)

        def poll2(g, k):
            poll(g, 2, k, [k % ndie])

        def ffnstage(g, k):
            fi = "if%d" % k
            g.wait("gpsimd", "sync")
            opair = (k % ndie) // 2 if ndie == 4 else 0
            for ft in range(NFT):
                g.wait("gpsimd", "tensor")
                igather(g, fi, ln2t[:, ft * TC:(ft + 1) * TC], LN2X,
                        k * DM + ft * 128, GC_L2R[opair], 128)
            g.wait("tensor", fi)
            for mt in range(nitp):
                bi = mt % 2
                g.wait("sync", "tensor")
                g.dma("sync", "wbf%d" % bi, w_bf[bi][:, :], wgbf.ap()[:, mt * 2048:(mt + 1) * 2048])
                g.wait("sync", "scalar")
                g.wait("sync", "vector")
                g.dma("sync", "tlo", t_lno[:, :], wubf.ap()[:, mt * 2048:(mt + 1) * 2048])
                g.wait("tensor", "wbf%d" % bi)
                g.wait("tensor", "tlo")
                g.wait("tensor", "vector")
                g.wait("tensor", "scalar")
                for kt in range(NFT):
                    if kt == NFT - 1:
                        g.op("tensor", (lambda t, kt=kt, bi=bi: mm_pair(
                            t, ps_mm, w_bf[bi][:, kt * 128:(kt + 1) * 128],
                            ln2t[:, kt * TC:(kt + 1) * TC], kt == 0, True)), selfwait=False)
                        g.op("tensor", (lambda t, kt=kt: mm_pair(
                            t, ps_d, t_lno[:, kt * 128:(kt + 1) * 128],
                            ln2t[:, kt * TC:(kt + 1) * TC], kt == 0, True)), selfwait=False)
                    else:
                        g.silent("tensor", (lambda t, kt=kt, bi=bi: mm_pair(
                            t, ps_mm, w_bf[bi][:, kt * 128:(kt + 1) * 128],
                            ln2t[:, kt * TC:(kt + 1) * TC], kt == 0, False)))
                        g.silent("tensor", (lambda t, kt=kt: mm_pair(
                            t, ps_d, t_lno[:, kt * 128:(kt + 1) * 128],
                            ln2t[:, kt * TC:(kt + 1) * TC], kt == 0, False)))
                g.wait("scalar", "tensor")
                g.wait("scalar", "vector")
                g.op("scalar", lambda s: s.activation(t_ev[:, :], ps_mm[:, :], AF.Silu))
                g.wait("vector", "scalar")
                if spill:
                    g.wait("vector", "shh")
                    g.op("vector", lambda v, mt=mt: v.scalar_tensor_tensor(
                        h_ffn[:, 0:TC], t_ev[:, :], p_hm[:, mt:mt + 1],
                        ps_d[:, :], ALU.mult, ALU.mult))
                    g.wait("sync", "vector")
                    g.dma("sync", "shh", h_sc.ap()[mt * 128:(mt + 1) * 128, :], h_ffn[:, 0:TC])
                else:
                    g.op("vector", lambda v, mt=mt: v.scalar_tensor_tensor(
                        h_ffn[:, mt * TC:(mt + 1) * TC], t_ev[:, :], p_hm[:, mt:mt + 1],
                        ps_d[:, :], ALU.mult, ALU.mult))
            hdone = g._c("vector")
            mmc = {}
            for mt2 in range(NFT):
                g.wait("tensor", "scalar")
                g.wait("sync", "tensor")
                g.dma("sync", "dnl", dn_l, wdbf.ap()[:, mt2 * (nitp * 128):(mt2 + 1) * (nitp * 128)])
                g.wait("tensor", "dnl")
                g.wait("tensor", "vector", hdone)
                for it in range(nitp):
                    yb = it % 2
                    gi_ = mt2 * nitp + it
                    if spill:
                        if gi_ >= 2:
                            g.wait("sync", "tensor", mmc[gi_ - 2])
                        g.wait("sync", "shh")
                        g.dma("sync", "lyt%d" % yb, t_yt[yb][:, :],
                              h_sc.ap()[it * 128:(it + 1) * 128, :])
                        g.wait("tensor", "lyt%d" % yb)
                    g.op("tensor", (lambda t, it=it, mt2=mt2, yb=yb: mm_pair(
                        t, ps_mm, dn_l[:, it * 128:(it + 1) * 128],
                        (t_yt[yb][:, :] if spill else h_ffn[:, it * TC:(it + 1) * TC]),
                        it == 0, it == nitp - 1)), selfwait=False)
                    mmc[gi_] = g._c("tensor")
                g.wait("scalar", "tensor")
                g.wait("scalar", fi)
                g.op("scalar", lambda s: s.copy(t_sum[:, 0:TC], ps_mm[:, :]))
                g.wait("gpsimd", "scalar")
                iscatter(g, fi, DPART, k * DM + mt2 * 128, t_sum[:, 0:TC], GC_OW, 128)
            g.wait("gpsimd", fi)
            flag_push(g, 3, k, fi)
            g.wait("gpsimd", fi)
            g.wait("sync", "tensor")

        def own2stage(g, k):
            io = "io%d" % k
            poll(g, 3, k, list(range(ndie)), lw="if%d" % k)
            for mt in range(NFT):
                g.wait("gpsimd", "scalar")
                for j in range(ndie):
                    g.wait("gpsimd", io)
                    igather(g, io, t_sum[:, 0:TC], DPART, k * DM + mt * 128,
                            GC_OR[j], 128, accum=(j > 0))
                g.wait("scalar", io)
                g.wait("scalar", "sfm")
                g.op("scalar", lambda s: s.copy(t_ev[:, :], t_sum[:, 0:TC]))
                g.wait("sync", "scalar")
                g.dma("sync", "sfm", fm_sc.ap()[mt * 128:(mt + 1) * 128, :], t_ev[:, :])
            for tt in range(NTT):
                g.wait("sync", "sfm")
                g.wait("sync", "scalar")
                g.wait("sync", "vector")
                g.wait("sync", "sout")
                g.wait("sync", "sltm")
                g.wait("sync", "sx2")
                g.dma_tr("sync", "tlo", t_lno[:, :], fm_sc.ap()[:, tt * 128:(tt + 1) * 128])
                g.dma("sync", "xtm", t_ln[:, :], x2_sc.ap()[tt * 128:(tt + 1) * 128, :])
                g.wait("vector", "tlo")
                g.wait("vector", "xtm")
                g.wait("scalar", "tlo")
                ln_freedim(g, t_lno[:, :], t_lnsq[:, :])
                g.op("vector", lambda v: v.tensor_tensor(t_ln[:, :], t_ln[:, :], t_lnsq[:, :], ALU.add))
                g.wait("sync", "vector")
                g.dma("sync", "sout",
                      yout[(k // ndie) * TC + tt * 128:(k // ndie) * TC + (tt + 1) * 128, :],
                      t_ln[:, :])
            g.wait("scalar", io)

        for k in range(nkc):
            if "p" in stages:
                fguard(k, lambda g, k=k: prescan(g, k), cls=("a", k))
            if "z" in stages:
                fguard(16 + k, lambda g, k=k: zstage(g, k), cls=("f", k))
            if "x" in stages:
                fguard(k, lambda g, k=k: prescan_x(g, k), cls=("a", k))
            if "l" in stages:
                fguard(8 + k, lambda g, k=k: scanstage(g, k, False), cls=("l", k))
            if "s" in stages:
                fguard(16 + k, lambda g, k=k: scanstage(g, k, True), cls=("f", k))
            if "o" in stages:
                fguard(16 + k, lambda g, k=k: outstage(g, k), cls=("f", k))
            if "w" in stages or "n" in stages or "2" in stages:
                fguard(24 + k, lambda g, k=k: ownstage(g, k), cls=("o", k))
            if "n" in stages:
                fguard(32 + k, lambda g, k=k: poll2(g, k), cls=("p2", k))
                fguard(16 + k, lambda g, k=k: ffnstage(g, k), cls=("f", k))
            if "2" in stages:
                fguard(24 + k, lambda g, k=k: own2stage(g, k), cls=("o", k))

        P.wait("sync", "sout")

        if dbg:
            for j in range(4):
                P.wait("gpsimd", "dbgg")
                igather(P, "dbgg", t_cf[:, :], XDBL, 0, GC_XRA[j], 128)
                P.wait("sync", "dbgg")
                P.dma("sync", "dbgs", dbg_o[j * 128:(j + 1) * 128, :], t_cf[:, :])
            P.wait("sync", "dbgs")

        for name in list(P.cnt):
            P.sems[name] = e(nc.semaphore("S_" + name))

        block = e(nc.Block())

        @block.sync
        def _(sync):
            for fn in P.q["sync"]:
                fn(sync)

        @block.tensor
        def _(tensor):
            for fn in P.q["tensor"]:
                fn(tensor)

        @block.vector
        def _(vector):
            for fn in P.q["vector"]:
                fn(vector)

        @block.scalar
        def _(scalar):
            for fn in P.q["scalar"]:
                fn(scalar)

        @block.gpsimd
        def _(gpsimd):
            for fn in P.q["gpsimd"]:
                fn(gpsimd)

    return nc


def _host_layout_v1(inputs, ndie=4, nonce=1000):
    import ml_dtypes
    ndtc = NDT // ndie
    nitp = (NIT // ndie) + 1
    interp = nitp * 128 * ndie

    x = np.ascontiguousarray(np.asarray(inputs["L0_attn_hidden_states"], np.float32).reshape(L, DM))
    wip = np.asarray(inputs["L0_attn_in_proj_w"], np.float32)
    xp = np.asarray(inputs["L0_attn_x_proj_w"], np.float32)
    wxT = wip[0:DI].T
    wzT = wip[DI:2 * DI].T
    woT = np.asarray(inputs["L0_attn_out_proj_w"], np.float32).T

    def blk_w(wT):
        return wT.reshape(NFT, 128, NDT, 128).transpose(1, 2, 0, 3).reshape(128, NDT * 2048)

    wx_full = blk_w(wxT)
    wz_full = blk_w(wzT)
    wot_full = np.ascontiguousarray(
        woT.reshape(NDT, 128, NFT, 128).transpose(1, 0, 2, 3).reshape(128, NDT * 2048))

    def blk_di(w):
        wd = w.shape[1]
        return w.reshape(NDT, 128, wd).transpose(1, 0, 2).reshape(128, NDT * wd)

    xp_dt_f = blk_di(np.ascontiguousarray(xp[0:DTR].T))
    xp_b_f = blk_di(np.ascontiguousarray(xp[DTR:DTR + DS].T))
    xp_c_f = blk_di(np.ascontiguousarray(xp[DTR + DS:DTR + 2 * DS].T))
    convw_f = blk_di(np.asarray(inputs["L0_attn_conv_w"], np.float32))
    convb_f = blk_di(np.asarray(inputs["L0_attn_conv_b"], np.float32).reshape(DI, 1))
    dtb_f = blk_di(np.asarray(inputs["L0_attn_dt_proj_b"], np.float32).reshape(DI, 1))
    alog_f = blk_di(np.asarray(inputs["L0_attn_A_log"], np.float32))
    dvec_f = blk_di(np.asarray(inputs["L0_attn_D"], np.float32).reshape(DI, 1))
    wdtt_f = np.ascontiguousarray(np.asarray(inputs["L0_attn_dt_proj_w"], np.float32).T)

    gi = np.asarray(inputs["L0_ffn_gate_idx"], np.int32)
    ui = np.asarray(inputs["L0_ffn_up_idx"], np.int32)
    dix = np.asarray(inputs["L0_ffn_down_idx"], np.int32)
    gi_p = np.zeros((interp, DM), np.int32)
    ui_p = np.zeros((interp, DM), np.int32)
    di_p = np.zeros((DM, interp), np.int32)
    gi_p[0:INTER] = gi
    ui_p[0:INTER] = ui
    di_p[:, 0:INTER] = dix
    nit_p = interp // 128
    g_all = gi_p.reshape(nit_p, 128, NFT, 128).transpose(3, 0, 2, 1).reshape(128, nit_p * NFT * 128)
    u_all = ui_p.reshape(nit_p, 128, NFT, 128).transpose(3, 0, 2, 1).reshape(128, nit_p * NFT * 128)
    d_all = di_p.reshape(NFT, 128, nit_p, 128).transpose(3, 0, 2, 1).reshape(128, NFT * nit_p * 128)

    sel = np.zeros((DS, DS * 128), ml_dtypes.bfloat16)
    for n in range(DS):
        sel[n, n * 128:(n + 1) * 128] = 1.0
    lut128 = np.ascontiguousarray(np.broadcast_to(
        np.asarray(inputs["L0_ffn_lut"], np.float32).reshape(1, NLUT), (128, NLUT)))

    fvals = np.zeros((128, 8 * 16), np.int32)
    for k in range(8):
        fvals[:, k * 16:(k + 1) * 16] = np.int32(nonce + k)

    maps = []
    ncore = 8
    for c in range(ncore):
        die = c // ndie if ndie == 4 else c // ndie
        dc = c % ndie
        dsl = slice(dc * ndtc * 2048, (dc + 1) * ndtc * 2048)
        it0 = dc * nitp
        m = {
            "xin": x,
            "xmy": np.ascontiguousarray(
                np.concatenate([x[k * TC:(k + 1) * TC]
                                for k in range(8) if k % ndie == dc], axis=0)
                if ndie < 4 else
                np.concatenate([x[c * TC:(c + 1) * TC]] * 2, axis=0)),
            "wx_blk": np.ascontiguousarray(wx_full[:, dsl]),
            "wz_blk": np.ascontiguousarray(wz_full[:, dsl]),
            "wot_blk": np.ascontiguousarray(wot_full[:, dsl]),
            "wxp_dt": np.ascontiguousarray(xp_dt_f[:, dc * ndtc * DTR:(dc + 1) * ndtc * DTR]),
            "wxp_b": np.ascontiguousarray(xp_b_f[:, dc * ndtc * DS:(dc + 1) * ndtc * DS]),
            "wxp_c": np.ascontiguousarray(xp_c_f[:, dc * ndtc * DS:(dc + 1) * ndtc * DS]),
            "wdtt": np.ascontiguousarray(wdtt_f[:, dc * ndtc * 128:(dc + 1) * ndtc * 128]),
            "p_convw_in": np.ascontiguousarray(convw_f[:, dc * ndtc * DC:(dc + 1) * ndtc * DC]),
            "p_convb_in": np.ascontiguousarray(convb_f[:, dc * ndtc:(dc + 1) * ndtc]),
            "p_dtb_in": np.ascontiguousarray(dtb_f[:, dc * ndtc:(dc + 1) * ndtc]),
            "p_alog_in": np.ascontiguousarray(alog_f[:, dc * ndtc * DS:(dc + 1) * ndtc * DS]),
            "p_dvec_in": np.ascontiguousarray(dvec_f[:, dc * ndtc:(dc + 1) * ndtc]),
            "lutv": lut128,
            "g_blk": np.ascontiguousarray(g_all[:, it0 * NFT * 128:(it0 + nitp) * NFT * 128]),
            "u_blk": np.ascontiguousarray(u_all[:, it0 * NFT * 128:(it0 + nitp) * NFT * 128]),
            "d_blk": np.ascontiguousarray(
                d_all.reshape(128, NFT, nit_p, 128)[:, :, it0:it0 + nitp, :]
                .reshape(128, NFT * nitp * 128)),
            "selmat": sel,
            "fvals_in": fvals,
        }
        hm = np.zeros((128, nitp), np.float32)
        for j in range(nitp):
            gidx0 = (it0 + j) * 128
            for p in range(128):
                hm[p, j] = 1.0 if (gidx0 + p) < INTER else 0.0
        m["hmask_in"] = hm
        fl = np.zeros((1, 64), np.int32)
        for k in range(8):
            if ndie == 4:
                active = (die == 1) or (k < 4)
                full = (die == 0 and k < 4) or (die == 1 and k >= 4)
            else:
                active = k <= 2 * die + 1
                full = k // 2 == die
            lite = active and not full
            own = full and (dc == (k % ndie))
            fl[0, k] = 1 if (lite or full) else 0
            fl[0, 8 + k] = 1 if lite else 0
            fl[0, 16 + k] = 1 if full else 0
            fl[0, 24 + k] = 1 if own else 0
            fl[0, 32 + k] = 1 if (full and not own) else 0
        m["flags_in"] = fl
        gidx = np.zeros((128, 32), np.int64)
        mypair = (c // 2) % 2 if ndie == 4 else 0
        if ndie == 4:
            lpair_global = c // 2
            mydom = DOM_OF_LPAIR[lpair_global]
            otherpair_global = (c // 2) ^ 1
            otherdom = DOM_OF_LPAIR[otherpair_global]
        p_ = np.arange(128)
        rb = lambda dcj: dcj * 8 * 192

        def delta_to(dcj):
            if ndie != 4:
                return 0
            pj = (die * ndie + dcj) // 2
            dj = DOM_OF_LPAIR[pj]
            return (HBM_BASE[dj] - HBM_BASE[mydom])

        gidx[:, GC_XW0] = dc * 8 * 192 + p_
        gidx[0:64, GC_XW1] = dc * 8 * 192 + 128 + p_[0:64]
        for j in range(ndie):
            d_rows = delta_to(j) // (TC * 4)
            gidx[:, GC_XRA[j]] = d_rows + j * 8 * 192 + p_
            gidx[0:64, GC_XRB[j]] = d_rows + j * 8 * 192 + 128 + p_[0:64]
        gidx[0:16, GC_FWL] = dc * 16 + p_[0:16]
        if ndie == 4:
            d_flg = (HBM_BASE[otherdom] - HBM_BASE[mydom]) // 4096
        else:
            d_flg = 0
        gidx[0:16, GC_FWR] = d_flg + dc * 16 + p_[0:16]
        gidx[:, GC_OW] = dc * 8 * DM + p_
        for j in range(ndie):
            d_rows = delta_to(j) // (TC * 4)
            gidx[:, GC_OR[j]] = d_rows + j * 8 * DM + p_
        gidx[:, GC_L2W] = p_
        if ndie == 4:
            for op_ in range(2):
                ownerpair_global = die * 2 + op_
                dj = DOM_OF_LPAIR[ownerpair_global]
                gidx[:, GC_L2R[op_]] = (HBM_BASE[dj] - HBM_BASE[mydom]) // (TC * 4) + p_
        else:
            gidx[:, GC_L2R[0]] = p_
            gidx[:, GC_L2R[1]] = p_
        assert gidx.max() < 2**31 and gidx.min() > -2**31
        m["gidx_in"] = gidx.astype(np.int32)
        maps.append(m)
    return maps


_NC_CACHE = {}


def _host_layout(inputs):
    import ml_dtypes
    x = np.ascontiguousarray(np.asarray(inputs["L0_attn_hidden_states"], np.float32).reshape(L, DM))
    wip = np.asarray(inputs["L0_attn_in_proj_w"], np.float32)
    xp = np.asarray(inputs["L0_attn_x_proj_w"], np.float32)

    def blk_w(wT):
        return np.ascontiguousarray(
            wT.reshape(NFT, 128, NDT, 128).transpose(1, 2, 0, 3).reshape(128, NDT * 2048))

    def blk_di(w):
        wd = w.shape[1]
        return np.ascontiguousarray(
            w.reshape(NDT, 128, wd).transpose(1, 0, 2).reshape(128, NDT * wd))

    wxT = wip[0:DI].T
    wzT = wip[DI:2 * DI].T
    woT = np.asarray(inputs["L0_attn_out_proj_w"], np.float32).T
    wot_blk = np.ascontiguousarray(
        woT.reshape(NDT, 128, NFT, 128).transpose(1, 0, 2, 3).reshape(128, NDT * 2048))

    gi = np.asarray(inputs["L0_ffn_gate_idx"], np.int32)
    ui = np.asarray(inputs["L0_ffn_up_idx"], np.int32)
    di = np.asarray(inputs["L0_ffn_down_idx"], np.int32)
    g_blk = np.ascontiguousarray(
        gi.reshape(NIT, 128, NFT, 128).transpose(3, 0, 2, 1).reshape(128, NIT * NFT * 128))
    u_blk = np.ascontiguousarray(
        ui.reshape(NIT, 128, NFT, 128).transpose(3, 0, 2, 1).reshape(128, NIT * NFT * 128))
    d_blk = np.ascontiguousarray(
        di.reshape(NFT, 128, NIT, 128).transpose(3, 0, 2, 1).reshape(128, NFT * NIT * 128))

    sel = np.zeros((DS, DS * 128), ml_dtypes.bfloat16)
    for n in range(DS):
        sel[n, n * 128:(n + 1) * 128] = 1.0

    base = {
        "xin": x,
        "wx_blk": blk_w(wxT),
        "wz_blk": blk_w(wzT),
        "wot_blk": wot_blk,
        "wxp_dt": blk_di(xp[0:DTR].T.reshape(DI, DTR)),
        "wxp_b": blk_di(xp[DTR:DTR + DS].T.reshape(DI, DS)),
        "wxp_c": blk_di(xp[DTR + DS:DTR + 2 * DS].T.reshape(DI, DS)),
        "wdtt": np.ascontiguousarray(np.asarray(inputs["L0_attn_dt_proj_w"], np.float32).T),
        "p_convw_in": blk_di(np.asarray(inputs["L0_attn_conv_w"], np.float32)),
        "p_convb_in": blk_di(np.asarray(inputs["L0_attn_conv_b"], np.float32).reshape(DI, 1)),
        "p_dtb_in": blk_di(np.asarray(inputs["L0_attn_dt_proj_b"], np.float32).reshape(DI, 1)),
        "p_alog_in": blk_di(np.asarray(inputs["L0_attn_A_log"], np.float32)),
        "p_dvec_in": blk_di(np.asarray(inputs["L0_attn_D"], np.float32).reshape(DI, 1)),
        "lutv": np.ascontiguousarray(np.broadcast_to(np.asarray(inputs["L0_ffn_lut"], np.float32).reshape(1, NLUT), (128, NLUT))),
        "g_blk": g_blk,
        "u_blk": u_blk,
        "d_blk": d_blk,
        "selmat": sel,
    }
    return base, x


def kernel(**inputs):
    if os.environ.get("BASS_SIM_FP64") == "1":
        base, x = _host_layout(inputs)
        if "nc0" not in _NC_CACHE:
            _NC_CACHE["nc0"] = build_program_v0()
        nc = _NC_CACHE["nc0"]
        in_maps = []
        for c in range(8):
            m = dict(base)
            m["xmy"] = np.ascontiguousarray(x[c * TC:(c + 1) * TC])
            m["kown_in"] = np.array([[c]], np.int32)
            in_maps.append(m)
        res = run_bass_kernel_spmd(nc, in_maps, core_ids=list(range(8)))
        out = np.concatenate([np.asarray(res.results[c]["yout"], np.float32)
                              for c in range(8)], axis=0)
        return out.reshape(1, L, DM)
    ndie = 2
    in_maps = _host_layout_v1(
        inputs, ndie=ndie,
        nonce=int(np.random.randint(1 << 20, 1 << 30)))
    if "nc1" not in _NC_CACHE:
        _NC_CACHE["nc1"] = build_program_v1(ndie=ndie)
    nc = _NC_CACHE["nc1"]
    res = run_bass_kernel_spmd(nc, in_maps, core_ids=list(range(8)))
    out = np.empty((L, DM), np.float32)
    for k in range(8):
        if ndie == 4:
            src_core, oi = k, k // 4
        else:
            src_core, oi = 2 * ((k // 2) % 4) + (k % 2), k // 2
        out[k * TC:(k + 1) * TC] = np.asarray(
            res.results[src_core]["yout"], np.float32)[oi * TC:(oi + 1) * TC]
    return out.reshape(1, L, DM)


if __name__ == "__main__":
    import reference as _r
    ins = _r.setup_inputs(0)
    out = kernel(**{k: np.asarray(v) for k, v in ins.items()})
    ref = np.asarray(_r.reference(**ins))
    err = np.linalg.norm(out - ref) / np.linalg.norm(ref)
    print("rel err:", err)
